# version 04
# baseline (speedup 1.0000x reference)
import numpy as np
import ml_dtypes
from contextlib import ExitStack
import concourse.bass as bass
import concourse.mybir as mybir
from concourse.bass_utils import run_bass_kernel_spmd

F32 = mybir.dt.float32
BF16 = mybir.dt.bfloat16
AF = mybir.ActivationFunctionType
ALU = mybir.AluOpType
AX = mybir.AxisListType

D = 2048
DT = 16
TOK = 2048
TC = 512
NCH = TOK // TC
SEQ = 8192
R = 8
EPS = 1e-5
HID = 5632
NHT = HID // 128
NHC = 11
NKT = 64
SCALE_ATTN = float(1.0 / np.sqrt(128.0))
INVD = float(1.0 / D)


class Ctr:

    def __init__(self, ctx, nc, name):
        self.h = ctx.enter_context(nc.semaphore(name))
        self.n = 0

    def bump(self, k):
        self.n += k
        return self.n


class Prog:

    ENGINES = ("sync", "gpsimd", "tensor", "scalar", "vector")

    def __init__(self, nc, ctx):
        self.nc = nc
        self.ctx = ctx
        self.cmds = {e: [] for e in self.ENGINES}
        self.qs = Ctr(ctx, nc, "qs")
        self.qg = Ctr(ctx, nc, "qg")
        self.ts = Ctr(ctx, nc, "ts")
        self.ss = Ctr(ctx, nc, "ss")
        self.vs = Ctr(ctx, nc, "vs")
        self.gs = Ctr(ctx, nc, "gs")

    def dma(self, eng, dst, src):
        q = self.qs if eng == "sync" else self.qg
        v = q.bump(16)
        self.cmds[eng].append(lambda E: E.dma_start(out=dst, in_=src).then_inc(q.h, 16))
        return v

    def wait(self, eng, ctr, val):
        self.cmds[eng].append(lambda E: E.wait_ge(ctr.h, val))

    def op(self, eng, f, inc=None):
        if inc is None:
            self.cmds[eng].append(lambda E: f(E))
            return None
        v = inc.bump(1)
        self.cmds[eng].append(lambda E: f(E).then_inc(inc.h, 1))
        return v

    def vop(self, f):
        v = self.op("vector", f, inc=self.vs)
        self.wait("vector", self.vs, v)
        return v

    def sop(self, f):
        v = self.op("scalar", f, inc=self.ss)
        self.wait("scalar", self.ss, v)
        return v

    def finalize(self, block):
        for name in self.ENGINES:
            cl = self.cmds[name]
            def body(E, cl=cl):
                for f in cl:
                    f(E)
            getattr(block, name)(body)


def sbuf(ctx, nc, name, shape, dt=F32):
    return ctx.enter_context(nc.sbuf_tensor(name, shape, dt))


def psum(ctx, nc, name, shape, dt=F32):
    return ctx.enter_context(nc.psum_tensor(name, shape, dt))



def build_launch3():
    nc = bass.Bass("TRN2", target_bir_lowering=False)
    ap = lambda n, s, d=F32, k="ExternalInput": nc.dram_tensor(n, s, d, kind=k).ap()
    x2h = ap("x2_hand", [D, TOK])
    qo = ap("q_own", [D, TOK], BF16)
    kf_d = ap("k_full", [128, SEQ], BF16)
    vf_d = ap("v_full", [128, SEQ], BF16)
    masks = ap("masks", [NCH * NKT * 128, TC], BF16)
    aoa = ap("ao_arr", [D, D])
    bo = ap("bo_cols", [128, DT])
    w1qh = ap("w1q", [D, HID], BF16)
    w2qh = ap("w2q", [D, HID], BF16)
    w3qh = ap("w3q", [HID, D], BF16)
    out_t = ap("out_t", [D, TOK], F32, "ExternalOutput")

    ctx = ExitStack()
    p = Prog(nc, ctx)
    b = {}
    b["x16a"] = sbuf(ctx, nc, "x16a", [128, TC], BF16)
    b["x16b"] = sbuf(ctx, nc, "x16b", [128, TC], BF16)
    b["sq16a"] = sbuf(ctx, nc, "sq16a", [128, TC], BF16)
    b["sq16b"] = sbuf(ctx, nc, "sq16b", [128, TC], BF16)
    b["ones16"] = sbuf(ctx, nc, "ones16", [128, 1], BF16)
    ones1f = sbuf(ctx, nc, "ones1f", [1, 128], F32)
    rb = {k: sbuf(ctx, nc, "row_" + k, [1, TC], F32)
          for k in ("S", "SS", "A", "B", "r", "mur", "t0", "t1", "t2")}
    tmp1 = sbuf(ctx, nc, "tmp1", [128, TC], F32)
    kf = sbuf(ctx, nc, "kf", [128, SEQ], BF16)
    vf = sbuf(ctx, nc, "vf", [128, SEQ], BF16)
    qp_a = sbuf(ctx, nc, "qp_a", [128, 2 * TC], BF16)
    qp_b = qp_a
    AO = sbuf(ctx, nc, "AO", [128, DT * TC], BF16)
    acc = sbuf(ctx, nc, "acc", [128, DT * TC], F32)
    xnb4 = sbuf(ctx, nc, "xnb4", [128, DT * TC], BF16)
    uni = sbuf(ctx, nc, "uni", [128, NKT * TC], BF16)
    w1s = uni[:, 0:DT * TC]
    w2s = uni[:, DT * TC:2 * DT * TC]
    w3s = uni[:, 2 * DT * TC:2 * DT * TC + 4 * D]
    exp_sb = sbuf(ctx, nc, "exp_sb", [128, 4 * TC], BF16)
    rs_row = sbuf(ctx, nc, "rs_row", [1, 2 * TC], F32)
    rs_rec = sbuf(ctx, nc, "rs_rec", [1, 2 * TC], F32)
    rcb = tmp1
    sb3a = sbuf(ctx, nc, "sb3a", [128, D], BF16)
    sb3b = sbuf(ctx, nc, "sb3b", [128, D], BF16)
    bo_c = sbuf(ctx, nc, "bo_c", [128, DT], F32)
    hid_sb = exp_sb
    sg16 = b["x16a"]

    ps_av = psum(ctx, nc, "ps_av", [128, 2 * TC])
    ps_rs = psum(ctx, nc, "ps_rs", [1, TC])
    ps_sc = psum(ctx, nc, "ps_sc", [128, 4 * TC])
    ps_bc = psum(ctx, nc, "ps_bc", [128, TC])
    b["ps_S"] = ps_bc
    b["ps_SS"] = ps_av

    p.dma("sync", bo_c[:, :], bo)
    p.dma("sync", kf[:, :], kf_d)
    kvl = p.dma("sync", vf[:, :], vf_d)
    mz = p.op("vector", lambda E: (E.memset(b["ones16"][:, :], 1.0),
                                   E.memset(ones1f[:, :], 1.0))[-1], inc=p.vs)

    prev_h_done = mz
    prev_exp = 0
    prev_ao_st = {}
    last_acc_store = 0
    prev_ffn_vs = mz
    prev_w3_use = 0
    prev_sg = 0
    last_scores = {}
    for qc in range(NCH):
        p.wait("gpsimd", p.ts, prev_w3_use)
        p.wait("gpsimd", p.vs, prev_ffn_vs)
        mask_q = {}
        for kt in range(NKT):
            mask_q[kt] = p.dma("gpsimd",
                               uni[:, kt * TC:(kt + 1) * TC],
                               masks[(qc * NKT + kt) * 128:
                                     (qc * NKT + kt) * 128 + 128, :])
        mask_ready = mask_q[NKT - 1]
        if qc > 0:
            p.wait("sync", p.vs, prev_ffn_vs)
        for dt in range(DT):
            x2l = p.dma("sync", acc[:, dt * TC:(dt + 1) * TC],
                        x2h[dt * 128:(dt + 1) * 128, qc * TC:(qc + 1) * TC])
        p.wait("tensor", p.qg, mask_ready)
        p.wait("tensor", p.qs, kvl)
        exp_done = {}
        mask_done = {}
        av_done = {}
        epi_av = {}
        epi_rs = {}
        for h in range(DT):
            qb = qp_a[:, 0:TC] if h % 2 == 0 else qp_a[:, TC:2 * TC]
            ps_avh = ps_av[:, 0:TC] if h % 2 == 0 else ps_av[:, TC:2 * TC]
            if (qc, h) != (0, 0) and (h >= 2 or qc > 0):
                key = (qc, h - 2) if h >= 2 else (qc - 1, DT - 2 + h)
                p.wait("sync", p.ts, last_scores[key])
            qldv = p.dma("sync", qb,
                         qo[h * 128:(h + 1) * 128, qc * TC:(qc + 1) * TC])
            p.wait("tensor", p.qs, qldv)
            NG = 32
            for i in range(NG):
                par = i % 2
                scs = ps_sc[:, par * 2 * TC:(par + 1) * 2 * TC]
                esl = exp_sb[:, par * 2 * TC:(par + 1) * 2 * TC]
                if i >= 2:
                    p.wait("tensor", p.ss, exp_done[(h, i - 2)])
                elif h > 0:
                    p.wait("tensor", p.ss, exp_done[(h - 1, NG - 2 + i)])
                elif qc > 0 and prev_exp:
                    p.wait("tensor", p.ss, prev_exp)
                if h == 0 and i < 2:
                    p.wait("tensor", p.vs, prev_h_done)
                    if qc > 0:
                        p.wait("tensor", p.vs, prev_ffn_vs)
                tv = None
                for j in range(2):
                    kt = i * 2 + j
                    tv = p.op("tensor", lambda E, j=j, kt=kt, par=par, qb=qb: E.matmul(
                        ps_sc[:, (par * 2 + j) * TC:(par * 2 + j + 1) * TC],
                        kf[:, kt * 128:(kt + 1) * 128], qb,
                        start=True, stop=True), inc=p.ts if j == 1 else None)
                if i == NG - 1:
                    last_scores[(qc, h)] = tv
                if i >= 1:
                    ip = i - 1
                    pp = ip % 2
                    p.wait("tensor", p.vs, mask_done[(h, ip)])
                    if ip == 0 and h >= 2:
                        p.wait("tensor", p.vs, epi_av[h - 2])
                    if ip == 0 and h >= 1:
                        p.wait("tensor", p.vs, epi_rs[h - 1])
                    av = None
                    for j in range(2):
                        kt = ip * 2 + j
                        p.op("tensor", lambda E, j=j, kt=kt, pp=pp, ip=ip,
                             ps_avh=ps_avh: E.matmul(
                            ps_avh, vf[:, kt * 128:(kt + 1) * 128],
                            exp_sb[:, (pp * 2 + j) * TC:(pp * 2 + j + 1) * TC],
                            start=(ip == 0 and j == 0), stop=False,
                            skip_group_check=True))
                        av = p.op("tensor", lambda E, j=j, pp=pp, ip=ip: E.matmul(
                            ps_rs[:, :], b["ones16"][:, :],
                            exp_sb[:, (pp * 2 + j) * TC:(pp * 2 + j + 1) * TC],
                            start=(ip == 0 and j == 0), stop=False,
                            skip_group_check=True), inc=p.ts if j == 1 else None)
                    av_done[(h, ip)] = av
                p.wait("scalar", p.ts, tv)
                if i >= 2:
                    p.wait("scalar", p.ts, av_done[(h, i - 2)])
                sv = p.op("scalar", lambda E, scs=scs, esl=esl: E.activation(
                    esl, scs, AF.Exp, scale=SCALE_ATTN), inc=p.ss)
                exp_done[(h, i)] = sv
                p.wait("vector", p.ss, sv)
                if h == 0:
                    p.wait("vector", p.qg, mask_q[i * 2 + 1])
                mv2 = p.op("vector", lambda E, i=i, esl=esl: E.tensor_mul(
                    esl, esl, uni[:, i * 2 * TC:(i + 1) * 2 * TC]), inc=p.vs)
                mask_done[(h, i)] = mv2
            ip = NG - 1
            pp = ip % 2
            p.wait("tensor", p.vs, mask_done[(h, ip)])
            av = None
            for j in range(2):
                kt = ip * 2 + j
                p.op("tensor", lambda E, j=j, kt=kt, pp=pp, ps_avh=ps_avh: E.matmul(
                    ps_avh, vf[:, kt * 128:(kt + 1) * 128],
                    exp_sb[:, (pp * 2 + j) * TC:(pp * 2 + j + 1) * TC],
                    start=False, stop=(j == 1), skip_group_check=True))
                av = p.op("tensor", lambda E, j=j, pp=pp: E.matmul(
                    ps_rs[:, :], b["ones16"][:, :],
                    exp_sb[:, (pp * 2 + j) * TC:(pp * 2 + j + 1) * TC],
                    start=False, stop=(j == 1), skip_group_check=True),
                    inc=p.ts if j == 1 else None)
            av_done[(h, ip)] = av
            p.wait("vector", p.ts, av)
            erv = p.vop(lambda E: E.tensor_single_scalar(
                rs_row[0:1, 0:TC], ps_rs[0:1, :], 1e-30, ALU.max))
            epi_rs[h] = erv
            rv = p.vop(lambda E: E.reciprocal(rs_rec[0:1, 0:TC], rs_row[0:1, 0:TC]))
            p.wait("tensor", p.vs, rv)
            if h >= 1 or qc > 0:
                p.wait("tensor", p.vs, prev_h_done)
            tv = p.op("tensor", lambda E: E.matmul(
                ps_bc[:, :], ones1f[:, :], rs_rec[0:1, 0:TC], start=True, stop=True),
                inc=p.ts)
            p.wait("vector", p.ts, tv)
            p.vop(lambda E: E.tensor_copy(rcb[:, :], ps_bc[:, :]))
            hv = p.op("vector", lambda E, h=h, ps_avh=ps_avh: E.tensor_mul(
                AO[:, h * TC:(h + 1) * TC], ps_avh, rcb[:, :]), inc=p.vs)
            epi_av[h] = hv
            prev_h_done = hv
        for do in range(DT):
            sb = sb3a if do % 2 == 0 else sb3b
            gkey = (qc, do - 2) if do >= 2 else ((qc - 1, DT - 2 + do) if qc > 0 else None)
            if gkey is not None:
                p.wait("gpsimd", p.ts, prev_ao_st.get(gkey, 0))
            lv = p.dma("gpsimd", sb[:, :], aoa[do * 128:(do + 1) * 128, :])
            p.wait("tensor", p.qg, lv)
            p.wait("tensor", p.vs, prev_h_done)
            if do > 0 or qc > 0:
                p.wait("tensor", p.vs, prev_proj_v)
            tv = None
            for hh in range(DT):
                tv = p.op("tensor", lambda E, hh=hh, sb=sb: E.matmul(
                    ps_sc[:, 0:TC], sb[:, hh * 128:(hh + 1) * 128],
                    AO[:, hh * TC:(hh + 1) * TC],
                    start=(hh == 0), stop=(hh == DT - 1)),
                    inc=p.ts if hh == DT - 1 else None)
            prev_ao_st[(qc, do)] = tv
            p.wait("vector", p.ts, tv)
            p.wait("vector", p.qs, x2l)
            p.op("vector", lambda E, do=do: E.tensor_scalar(
                tmp1[:, :], ps_sc[:, 0:TC], bo_c[:, do:do + 1], None, ALU.add))
            xv = p.op("vector", lambda E, do=do: E.tensor_add(
                acc[:, do * TC:(do + 1) * TC], acc[:, do * TC:(do + 1) * TC],
                tmp1[:, :]), inc=p.vs)
            prev_proj_v = xv
        x3_done = prev_proj_v
        prev_exp = p.ss.n

        x_tiles = [acc[:, dt * TC:(dt + 1) * TC] for dt in range(DT)]
        p.wait("vector", p.vs, x3_done)
        ev = emit_stats_chunk(p, b, x_tiles, 0, rb["S"], rb["SS"], p.qs.n)
        rm = emit_rowmath_composite(p, rb, with_inner=False, n=TC, off=0)
        p.wait("tensor", p.vs, rm)
        bt = p.op("tensor", lambda E: (
            E.matmul(ps_bc[:, :], ones1f[:, :], rb["A"][0:1, :], start=True, stop=True),
            E.matmul(ps_av[:, 0:TC], ones1f[:, :], rb["B"][0:1, :], start=True, stop=True))[-1],
            inc=p.ts)
        p.wait("vector", p.ts, bt)
        for dt in range(DT):
            p.op("vector", lambda E, dt=dt: E.tensor_mul(
                tmp1[:, :], acc[:, dt * TC:(dt + 1) * TC], ps_bc[:, :]))
            vv = p.op("vector", lambda E, dt=dt: E.tensor_sub(
                xnb4[:, dt * TC:(dt + 1) * TC], tmp1[:, :], ps_av[:, 0:TC]), inc=p.vs)
        xn_ready = vv
        w3h = [uni[:, 2 * DT * TC:2 * DT * TC + 4 * D],
               uni[:, 2 * DT * TC + 4 * D:2 * DT * TC + 8 * D]]
        p.wait("sync", p.ts, last_scores[(qc, DT - 1)])
        p.wait("sync", p.vs, prev_h_done)
        p.wait("sync", p.ss, prev_exp)
        ld_done = {}
        for dt in range(DT):
            p.dma("sync", w1s[:, dt * TC:(dt + 1) * TC],
                  w1qh[dt * 128:(dt + 1) * 128, 0:TC])
        for dt in range(DT):
            wl = p.dma("sync", w2s[:, dt * TC:(dt + 1) * TC],
                       w2qh[dt * 128:(dt + 1) * 128, 0:TC])
        for j in range(4):
            w3l = p.dma("sync", w3h[0][:, j * D:(j + 1) * D],
                        w3qh[j * 128:j * 128 + 128, :])
        ld_done[0] = max(wl, w3l)
        w3_use = {}
        for hc in range(NHC):
            w3cur = w3h[hc % 2]
            p.wait("tensor", p.qs, ld_done[hc])
            p.wait("tensor", p.vs, xn_ready)
            hid_done = {}
            for ht in range(4):
                if qc > 0 or hc > 0 or ht > 0:
                    p.wait("tensor", p.ss, prev_sg)
                    p.wait("tensor", p.vs, prev_hid)
                t1 = t2 = None
                for dt in range(DT):
                    t1 = p.op("tensor", lambda E, dt=dt, ht=ht: E.matmul(
                        ps_sc[:, 0:TC],
                        w1s[:, dt * TC + ht * 128: dt * TC + (ht + 1) * 128],
                        xnb4[:, dt * TC:(dt + 1) * TC],
                        start=(dt == 0), stop=(dt == DT - 1)),
                        inc=p.ts if dt == DT - 1 else None)
                for dt in range(DT):
                    t2 = p.op("tensor", lambda E, dt=dt, ht=ht: E.matmul(
                        ps_sc[:, TC:2 * TC],
                        w2s[:, dt * TC + ht * 128: dt * TC + (ht + 1) * 128],
                        xnb4[:, dt * TC:(dt + 1) * TC],
                        start=(dt == 0), stop=(dt == DT - 1)),
                        inc=p.ts if dt == DT - 1 else None)
                p.wait("scalar", p.ts, t1)
                sgv = p.op("scalar", lambda E: E.activation(
                    sg16[:, :], ps_sc[:, 0:TC], AF.Sigmoid), inc=p.ss)
                prev_sg = sgv
                p.wait("vector", p.ss, sgv)
                p.wait("vector", p.ts, t2)
                hv2 = p.op("vector", lambda E, ht=ht: E.tensor_mul(
                    hid_sb[:, ht * TC:(ht + 1) * TC], sg16[:, :],
                    ps_sc[:, TC:2 * TC]), inc=p.vs)
                prev_hid = hv2
                hid_done[ht] = hv2
            if hc + 1 < NHC:
                p.wait("sync", p.ts, t2)
                for dt in range(DT):
                    p.dma("sync", w1s[:, dt * TC:(dt + 1) * TC],
                          w1qh[dt * 128:(dt + 1) * 128, (hc + 1) * TC:(hc + 2) * TC])
                for dt in range(DT):
                    wl = p.dma("sync", w2s[:, dt * TC:(dt + 1) * TC],
                               w2qh[dt * 128:(dt + 1) * 128,
                                    (hc + 1) * TC:(hc + 2) * TC])
                if hc >= 1:
                    p.wait("sync", p.ts, w3_use[hc - 1])
                for j in range(4):
                    w3l = p.dma("sync", w3h[(hc + 1) % 2][:, j * D:(j + 1) * D],
                                w3qh[((hc + 1) * 4 + j) * 128:
                                     ((hc + 1) * 4 + j) * 128 + 128, :])
                ld_done[hc + 1] = max(wl, w3l)
            accv = {}
            for do in range(DT):
                ob = 2 + (do % 2)
                p.wait("tensor", p.vs, hid_done[3])
                if do >= 2:
                    p.wait("tensor", p.vs, accv[do - 2])
                elif qc > 0 or hc > 0:
                    p.wait("tensor", p.vs, prev_acc_v2)
                t3 = None
                for j in range(4):
                    t3 = p.op("tensor", lambda E, j=j, do=do, ob=ob, w3cur=w3cur: E.matmul(
                        ps_sc[:, ob * TC:(ob + 1) * TC],
                        w3cur[:, j * D + do * 128: j * D + (do + 1) * 128],
                        hid_sb[:, j * TC:(j + 1) * TC],
                        start=(j == 0), stop=(j == 3)),
                        inc=p.ts if j == 3 else None)
                p.wait("vector", p.ts, t3)
                av2 = p.op("vector", lambda E, do=do, ob=ob: E.tensor_add(
                    acc[:, do * TC:(do + 1) * TC], acc[:, do * TC:(do + 1) * TC],
                    ps_sc[:, ob * TC:(ob + 1) * TC]), inc=p.vs)
                accv[do] = av2
                prev_acc_v = av2
            prev_acc_v2 = accv[DT - 1] if DT >= 2 else prev_acc_v
            prev_acc_v2 = max(accv[DT - 1], accv[DT - 2])
            w3_use[hc] = t3
            prev_w3_use = t3
        p.wait("sync", p.vs, prev_acc_v)
        for dt in range(DT):
            last_acc_store = p.dma("sync", out_t[dt * 128:(dt + 1) * 128,
                                                 qc * TC:(qc + 1) * TC],
                                   acc[:, dt * TC:(dt + 1) * TC])
        prev_ffn_vs = prev_acc_v
        prev_exp = p.ss.n

    p.wait("sync", p.qs, p.qs.n)
    p.wait("gpsimd", p.qg, p.qg.n)

    with ctx, nc.Block() as block:
        p.finalize(block)
    return nc



def emit_row_rstd(p, tmp_rows, var_row, out_row, n=TOK, eps=EPS, off=0):
    t = tmp_rows
    s = slice(off, off + n)
    vv = p.vop(lambda E: E.tensor_scalar_add(t[0:1, s], var_row[0:1, s], float(eps)))
    p.wait("scalar", p.vs, vv)
    p.sop(lambda E: E.activation(t[0:1, s], t[0:1, s], AF.Ln))
    return p.sop(lambda E: E.activation(out_row[0:1, s], t[0:1, s],
                                        AF.Exp, scale=-0.5))


def emit_stats_chunk(p, bufs, x_tiles, ch, S_row, SS_row, x_ready_val):
    x16a, x16b, sq16, ones16 = bufs["x16a"], bufs["x16b"], bufs["sq16a"], bufs["ones16"]
    sq16b = bufs["sq16b"]
    ps_S, ps_SS = bufs["ps_S"], bufs["ps_SS"]
    p.wait("vector", p.qs, x_ready_val)
    tens_done = {}
    for dt in range(DT):
        x16 = x16a if dt % 2 == 0 else x16b
        sq = sq16 if dt % 2 == 0 else sq16b
        if dt >= 2:
            p.wait("vector", p.ts, tens_done[dt - 2])
        vv = p.op("vector", lambda E, x16=x16, dt=dt: E.tensor_copy(
            x16[:, :], x_tiles[dt]), inc=p.vs)
        p.wait("scalar", p.vs, vv)
        sv = p.op("scalar", lambda E, x16=x16, sq=sq: E.activation(
            sq[:, :], x16[:, :], AF.Square), inc=p.ss)
        p.wait("tensor", p.ss, sv)
        tv = p.op("tensor", lambda E, x16=x16, sq=sq, dt=dt: (
            E.matmul(ps_S[0:1, 0:TC], ones16[:, :], x16[:, :],
                     start=(dt == 0), stop=(dt == DT - 1), skip_group_check=True),
            E.matmul(ps_SS[0:1, 0:TC], ones16[:, :], sq[:, :],
                     start=(dt == 0), stop=(dt == DT - 1), skip_group_check=True))[-1],
            inc=p.ts)
        tens_done[dt] = tv
    p.wait("vector", p.ts, tens_done[DT - 1])
    p.op("vector", lambda E: E.tensor_copy(S_row[0:1, ch * TC:(ch + 1) * TC],
                                           ps_S[0:1, 0:TC]))
    return p.op("vector", lambda E: E.tensor_copy(SS_row[0:1, ch * TC:(ch + 1) * TC],
                                                  ps_SS[0:1, 0:TC]), inc=p.vs)


def emit_rowmath_composite(p, rb, with_inner, n=TOK, off=0):
    S, SS, A, B, r, mur, t0, t1 = (rb["S"], rb["SS"], rb["A"], rb["B"],
                                   rb["r"], rb["mur"], rb["t0"], rb["t1"])
    sl = lambda x: x[0:1, off:off + n]
    p.vop(lambda E: E.tensor_scalar_mul(sl(t0), sl(S), INVD))
    p.vop(lambda E: E.tensor_scalar_mul(sl(t1), sl(SS), INVD))
    p.vop(lambda E: E.tensor_mul(sl(B), sl(t0), sl(t0)))
    vv = p.vop(lambda E: E.tensor_sub(sl(t1), sl(t1), sl(B)))
    p.wait("scalar", p.vs, vv)
    sv = emit_row_rstd(p, rb["t2"], t1, r, n=n, off=off)
    p.wait("vector", p.ss, sv)
    p.vop(lambda E: E.tensor_mul(sl(mur), sl(t0), sl(r)))
    if not with_inner:
        p.vop(lambda E: E.tensor_copy(sl(A), sl(r)))
        return p.vop(lambda E: E.tensor_copy(sl(B), sl(mur)))
    p.vop(lambda E: E.tensor_mul(sl(t0), sl(r), sl(r)))
    vv = p.vop(lambda E: E.tensor_mul(sl(t0), sl(t1), sl(t0)))
    p.wait("scalar", p.vs, vv)
    sv = emit_row_rstd(p, rb["t2"], t0, t1, n=n, off=off)
    p.wait("vector", p.ss, sv)
    p.vop(lambda E: E.tensor_mul(sl(A), sl(r), sl(t1)))
    p.vop(lambda E: E.tensor_scalar_mul(sl(t0), sl(S), INVD))
    return p.vop(lambda E: E.tensor_mul(sl(B), sl(t0), sl(A)))



def build_launch1():
    nc = bass.Bass("TRN2", target_bir_lowering=False)
    x_t = nc.dram_tensor("x_t", [D, TOK], F32, kind="ExternalInput").ap()
    lng = nc.dram_tensor("lng_cols", [128, DT], F32, kind="ExternalInput").ap()
    lnb = nc.dram_tensor("lnb_cols", [128, DT], F32, kind="ExternalInput").ap()
    win = nc.dram_tensor("win_arr", [128, DT * R], F32, kind="ExternalInput").ap()
    binc = nc.dram_tensor("b_in_col", [R, 1], F32, kind="ExternalInput").ap()
    u_hand = nc.dram_tensor("u_hand", [R, TOK], F32, kind="ExternalOutput").ap()
    rows1 = nc.dram_tensor("rows1", [4, TOK], F32, kind="ExternalOutput").ap()

    ctx = ExitStack()
    p = Prog(nc, ctx)
    b = {}
    b["xch_a"] = sbuf(ctx, nc, "xch_a", [128, DT * TC], F32)
    b["xch_b"] = sbuf(ctx, nc, "xch_b", [128, DT * TC], F32)
    b["x16a"] = sbuf(ctx, nc, "x16a", [128, TC], BF16)
    b["x16b"] = sbuf(ctx, nc, "x16b", [128, TC], BF16)
    b["sq16a"] = sbuf(ctx, nc, "sq16a", [128, TC], BF16)
    b["sq16b"] = sbuf(ctx, nc, "sq16b", [128, TC], BF16)
    b["ones16"] = sbuf(ctx, nc, "ones16", [128, 1], BF16)
    ones1f = sbuf(ctx, nc, "ones1f", [1, 128], F32)
    lng_c = sbuf(ctx, nc, "lng_c", [128, DT], F32)
    lnb_c = sbuf(ctx, nc, "lnb_c", [128, DT], F32)
    win16 = sbuf(ctx, nc, "win16", [128, DT * R], BF16)
    bin_c = sbuf(ctx, nc, "bin_c", [R, 1], F32)
    rb = {k: sbuf(ctx, nc, "row_" + k, [1, TOK], F32)
          for k in ("S", "SS", "A", "B", "r", "mur", "t0", "t1", "t2")}
    tmp1 = sbuf(ctx, nc, "tmp1", [128, TC], F32)
    xnp = sbuf(ctx, nc, "xnp", [128, TC], F32)
    xn16 = sbuf(ctx, nc, "xn16", [128, TC], BF16)
    u_sb = sbuf(ctx, nc, "u_sb", [R, TOK], F32)
    b["ps_S"] = psum(ctx, nc, "ps_S", [1, TC])
    b["ps_SS"] = psum(ctx, nc, "ps_SS", [1, TC])
    ps_b1 = psum(ctx, nc, "ps_b1", [128, TC])
    ps_b2 = psum(ctx, nc, "ps_b2", [128, TC])
    ps_u = psum(ctx, nc, "ps_u", [R, TC])

    p.dma("sync", lng_c[:, :], lng)
    p.dma("sync", lnb_c[:, :], lnb)
    p.dma("gpsimd", win16[:, :], win)
    cv = p.dma("sync", bin_c[:, :], binc)
    gv = p.qg.n
    iv = p.op("vector", lambda E: E.memset(b["ones16"][:, :], 1.0))
    iv = p.op("vector", lambda E: E.memset(ones1f[:, :], 1.0), inc=p.vs)

    last_ev = None
    for ch in range(NCH):
        xch = b["xch_a"] if ch % 2 == 0 else b["xch_b"]
        if ch >= 2 and last_ev is not None:
            pass
        if ch >= 2:
            p.wait("sync", p.vs, stats_done[ch - 2])
        for dt in range(DT):
            p.dma("sync", xch[:, dt * TC:(dt + 1) * TC],
                  x_t[dt * 128:(dt + 1) * 128, ch * TC:(ch + 1) * TC])
        x_tiles = [xch[:, dt * TC:(dt + 1) * TC] for dt in range(DT)]
        ev = emit_stats_chunk(p, b, x_tiles, ch, rb["S"], rb["SS"], p.qs.n)
        if ch == 0:
            stats_done = {}
        stats_done[ch] = ev
    rm = emit_rowmath_composite(p, rb, with_inner=True)

    p.wait("tensor", p.qg, gv)
    p.wait("scalar", p.qs, cv)
    for ch in range(NCH):
        xch = b["xch_a"] if ch % 2 == 0 else b["xch_b"]
        if ch < 2:
            p.wait("sync", p.vs, rm)
        else:
            p.wait("sync", p.vs, xn_done[ch - 2])
        for dt in range(DT):
            p.dma("sync", xch[:, dt * TC:(dt + 1) * TC],
                  x_t[dt * 128:(dt + 1) * 128, ch * TC:(ch + 1) * TC])
        ld = p.qs.n
        p.wait("tensor", p.vs, rm)
        if ch >= 1:
            p.wait("tensor", p.vs, bc_used[ch - 1])
        tv = p.op("tensor", lambda E, ch=ch: (
            E.matmul(ps_b1[:, :], ones1f[:, :], rb["A"][0:1, ch * TC:(ch + 1) * TC],
                     start=True, stop=True),
            E.matmul(ps_b2[:, :], ones1f[:, :], rb["B"][0:1, ch * TC:(ch + 1) * TC],
                     start=True, stop=True))[-1], inc=p.ts)
        p.wait("vector", p.ts, tv)
        p.wait("vector", p.qs, ld)
        for dt in range(DT):
            vv = p.op("vector", lambda E, dt=dt, xch=xch: E.tensor_mul(
                tmp1[:, :], xch[:, dt * TC:(dt + 1) * TC], ps_b1[:, :]))
            vv = p.op("vector", lambda E: E.tensor_sub(xnp[:, :], tmp1[:, :],
                                                       ps_b2[:, :]), inc=p.vs)
            p.wait("scalar", p.vs, vv)
            sv = p.op("scalar", lambda E, dt=dt: E.activation(
                xn16[:, :], xnp[:, :], AF.Identity,
                bias=lnb_c[:, dt:dt + 1], scale=lng_c[:, dt:dt + 1]), inc=p.ss)
            p.wait("tensor", p.ss, sv)
            tv2 = p.op("tensor", lambda E, dt=dt: E.matmul(
                ps_u[:, :], win16[:, dt * R:(dt + 1) * R], xn16[:, :],
                start=(dt == 0), stop=(dt == DT - 1)), inc=p.ts)
            p.wait("vector", p.ts, tv2)
        uv = p.op("vector", lambda E, ch=ch: E.tensor_scalar(
            u_sb[:, ch * TC:(ch + 1) * TC], ps_u[:, :], bin_c[:, :], None,
            ALU.add), inc=p.vs)
        if ch == 0:
            xn_done, bc_used = {}, {}
        xn_done[ch] = uv
        bc_used[ch] = uv
    p.wait("sync", p.vs, xn_done[NCH - 1])
    p.dma("sync", u_hand, u_sb[:, :])
    p.dma("sync", rows1[0:1, :], rb["A"][0:1, :])
    p.dma("sync", rows1[1:2, :], rb["B"][0:1, :])
    p.dma("sync", rows1[2:3, :], rb["r"][0:1, :])
    fin = p.dma("sync", rows1[3:4, :], rb["mur"][0:1, :])
    p.wait("sync", p.qs, fin)

    with ctx, nc.Block() as block:
        p.finalize(block)
    return nc



def build_launch2(upto=5):
    nc = bass.Bass("TRN2", target_bir_lowering=False)
    ap = lambda n, s, d=F32, k="ExternalInput": nc.dram_tensor(n, s, d, kind=k).ap()
    x_t = ap("x_t", [D, TOK])
    rows1 = ap("rows1", [4, TOK])
    u_pad = ap("u_pad", [R, SEQ])
    dec_col = ap("dec_col", [R, 1])
    woutb = ap("woutb", [R + 1, D])
    d_cols = ap("d_cols", [128, DT])
    lng = ap("lng_cols", [128, DT])
    lnb = ap("lnb_cols", [128, DT])
    wdn = ap("wdn_arr", [128, DT * 32])
    bdn = ap("bdn_col", [32, 1])
    wupb = ap("wupb", [33, D])
    l0w = ap("l0w_arr", [D, D])
    g0 = ap("g0_cols", [128, DT])
    be0 = ap("be0_cols", [128, DT])
    lnw = ap("lnw_cols", [128, DT])
    lnbb = ap("lnbb_cols", [128, DT])
    w1a = ap("w1_arr", [D, HID])
    w2a = ap("w2_arr", [D, HID])
    w3a = ap("w3_arr", [HID, D])
    aqa = ap("aq_arr", [D, D])
    aka = ap("ak_arr", [128, D])
    ava = ap("av_arr", [128, D])
    bq = ap("bq_cols", [128, DT])
    bk = ap("bk_col", [128, 1])
    bv = ap("bv_col", [128, 1])
    idt = ap("idt32", [128, 128])

    x2h = ap("x2_hand", [D, TOK], F32, "ExternalOutput")
    qh = ap("q_hand", [D, TOK], BF16, "ExternalOutput")
    kh = ap("k_hand", [128, TOK], BF16, "ExternalOutput")
    vh = ap("v_hand", [128, TOK], BF16, "ExternalOutput")
    w1qo = ap("w1q", [D, HID], BF16, "ExternalOutput")
    w2qo = ap("w2q", [D, HID], BF16, "ExternalOutput")
    w3qo = ap("w3q", [HID, D], BF16, "ExternalOutput")
    x1s = ap("x1_scr", [D, TOK], F32, "Internal")
    sgs = ap("sgn_scr", [D, D], BF16, "Internal")

    ctx = ExitStack()
    p = Prog(nc, ctx)

    b = {}
    b["x16a"] = sbuf(ctx, nc, "x16a", [128, TC], BF16)
    b["x16b"] = sbuf(ctx, nc, "x16b", [128, TC], BF16)
    b["sq16a"] = sbuf(ctx, nc, "sq16a", [128, TC], BF16)
    b["sq16b"] = sbuf(ctx, nc, "sq16b", [128, TC], BF16)
    b["ones16"] = sbuf(ctx, nc, "ones16", [128, 1], BF16)
    ones1f = sbuf(ctx, nc, "ones1f", [1, 128], F32)
    rb = {k: sbuf(ctx, nc, "row_" + k, [1, TC], F32)
          for k in ("S", "SS", "A", "B", "r", "mur", "t0", "t1", "t2")}
    r1rows = sbuf(ctx, nc, "r1rows", [1, 4 * TC], F32)
    tmp1 = sbuf(ctx, nc, "tmp1", [128, TC], F32)
    tmp2 = sbuf(ctx, nc, "tmp2", [128, TC], F32)
    tmp3 = sbuf(ctx, nc, "tmp3", [128, TC], F32)
    cols = {}
    for nm, src in (("d", d_cols), ("lng", lng), ("lnb", lnb), ("g0", g0),
                    ("be0", be0), ("lnw", lnw), ("lnbb", lnbb), ("bq", bq)):
        cols[nm] = sbuf(ctx, nc, "c_" + nm, [128, DT], F32)
        p.dma("sync", cols[nm][:, :], {"d": d_cols, "lng": lng, "lnb": lnb,
                                       "g0": g0, "be0": be0, "lnw": lnw,
                                       "lnbb": lnbb, "bq": bq}[nm])
    bk_c = sbuf(ctx, nc, "bk_c", [128, 1], F32)
    bv_c = sbuf(ctx, nc, "bv_c", [128, 1], F32)
    bdn_c = sbuf(ctx, nc, "bdn_c", [32, 1], F32)
    p.dma("sync", bk_c[:, :], bk)
    p.dma("sync", bv_c[:, :], bv)
    p.dma("sync", bdn_c[:, :], bdn)
    woutb16 = sbuf(ctx, nc, "woutb16", [R + 1, D], BF16)
    wdn16 = sbuf(ctx, nc, "wdn16", [128, DT * 32], BF16)
    wupb16 = sbuf(ctx, nc, "wupb16", [33, D], BF16)
    idt32 = sbuf(ctx, nc, "idt32_sb", [128, 128], F32)
    idt16 = sbuf(ctx, nc, "idt16_sb", [128, 128], BF16)
    ak16 = sbuf(ctx, nc, "ak16", [128, D], BF16)
    av16 = sbuf(ctx, nc, "av16", [128, D], BF16)
    p.dma("gpsimd", woutb16[:, :], woutb)
    p.dma("gpsimd", wdn16[:, :], wdn)
    p.dma("gpsimd", wupb16[:, :], wupb)
    p.dma("sync", idt32[:, :], idt)
    p.dma("gpsimd", idt16[:, :], idt)
    p.dma("gpsimd", ak16[:, :], aka)
    cg0 = p.dma("gpsimd", av16[:, :], ava)
    h_ext16 = sbuf(ctx, nc, "h_ext16", [R + 1, TOK], BF16)
    hf_ext16 = sbuf(ctx, nc, "hf_ext16", [33, TC], BF16)
    srow = sbuf(ctx, nc, "srow", [1, 16], F32)
    s0col = sbuf(ctx, nc, "s0col", [128, 1], F32)
    sg0c = sbuf(ctx, nc, "sg0c", [128, DT], F32)
    thp_c = sbuf(ctx, nc, "thp_c", [128, 1], F32)
    thn_c = sbuf(ctx, nc, "thn_c", [128, 1], F32)
    mxacc = sbuf(ctx, nc, "mxacc", [128, 1], F32)
    mxt = sbuf(ctx, nc, "mxt", [128, 1], F32)
    gacc = sbuf(ctx, nc, "gacc", [128, 1], F32)
    c0 = p.qs.n

    b["ps_S"] = psum(ctx, nc, "ps_S", [1, TC])
    b["ps_SS"] = psum(ctx, nc, "ps_SS", [1, TC])
    ps_b1 = psum(ctx, nc, "ps_b1", [128, TC])
    ps_b2 = psum(ctx, nc, "ps_b2", [128, TC])
    ps_b3 = psum(ctx, nc, "ps_b3", [128, TC])
    ps_b4 = psum(ctx, nc, "ps_b4", [128, TC])
    ps_ar = psum(ctx, nc, "ps_ar", [128, TC])
    ps_hf = psum(ctx, nc, "ps_hf", [128, TC])

    p.op("vector", lambda E: E.memset(b["ones16"][:, :], 1.0))
    p.op("vector", lambda E: E.memset(ones1f[:, :], 1.0))
    p.op("vector", lambda E: E.memset(h_ext16[:, :], 1.0))
    mz = p.op("vector", lambda E: E.memset(hf_ext16[:, :], 1.0), inc=p.vs)

    sctx = ExitStack()
    u_sb = sbuf(sctx, nc, "u_sb", [R, SEQ], F32)
    a_b = sbuf(sctx, nc, "a_b", [R, SEQ], F32)
    h_pad = sbuf(sctx, nc, "h_pad", [R, SEQ], F32)
    dcol = sbuf(sctx, nc, "dcol", [R, 1], F32)
    p.dma("sync", dcol[:, :], dec_col)
    lu = p.dma("sync", u_sb[:, :], u_pad)
    p.wait("scalar", p.qs, lu)
    sv = p.op("scalar", lambda E: E.activation(a_b[:, 0:1], dcol[:, :], AF.Exp,
                                               scale=-1.0), inc=p.ss)
    p.wait("vector", p.ss, sv)
    p.vop(lambda E: E.tensor_scalar_add(a_b[:, 0:1], a_b[:, 0:1], 1.0))
    p.vop(lambda E: E.reciprocal(dcol[:, :], a_b[:, 0:1]))
    p.vop(lambda E: E.memset(a_b[:, :], 1.0))
    p.vop(lambda E: E.tensor_scalar_mul(a_b[:, :], a_b[:, :], dcol[:, :]))
    p.wait("vector", p.qs, lu)
    p.op("vector", lambda E: E.tensor_tensor_scan(
        h_pad[:, :], a_b[:, :], u_sb[:, :], 0.0, ALU.mult, ALU.add))
    scan_done = p.op("vector", lambda E: E.tensor_copy(
        h_ext16[0:R, :], h_pad[0:R, SEQ - TOK:SEQ]), inc=p.vs)
    sctx.close()

    pctx = ExitStack()
    wst_a = sbuf(pctx, nc, "wst_a", [128, D], F32)
    wst_b = sbuf(pctx, nc, "wst_b", [128, D], F32)
    sg16 = sbuf(pctx, nc, "sg16", [128, D], BF16)
    p.op("vector", lambda E: E.memset(mxacc[:, :], 0.0))
    sg_writes = {}
    for do in range(DT):
        wst = wst_a if do % 2 == 0 else wst_b
        if do < 2:
            p.wait("sync", p.vs, scan_done)
        else:
            p.wait("sync", p.vs, mx_done[do - 2])
        ldv = p.dma("sync", wst[:, :], l0w[do * 128:(do + 1) * 128, :])
        p.wait("scalar", p.qs, ldv)
        if do >= 1:
            p.wait("scalar", p.qg, sg_writes[do - 1])
        sv = p.op("scalar", lambda E, wst=wst: E.sign(sg16[:, :], wst[:, :]), inc=p.ss)
        p.wait("gpsimd", p.ss, sv)
        sg_writes[do] = p.dma("gpsimd", sgs[do * 128:(do + 1) * 128, :], sg16[:, :])
        p.wait("vector", p.ss, sv)
        p.vop(lambda E, wst=wst: E.tensor_reduce(
            mxt[:, :], wst[:, :], AX.X, ALU.max, apply_absolute_value=True))
        vv = p.vop(lambda E: E.tensor_max(mxacc[:, :], mxacc[:, :], mxt[:, :]))
        if do == 0:
            mx_done = {}
        mx_done[do] = vv
    p.wait("tensor", p.vs, mx_done[DT - 1])
    p.wait("tensor", p.qs, c0)
    tv = p.op("tensor", lambda E: E.matmul(b["ps_S"][0:1, 0:128], mxacc[:, :],
                                           idt32[:, :], start=True, stop=True),
              inc=p.ts)
    p.wait("vector", p.ts, tv)
    vv = p.vop(lambda E: E.tensor_reduce(
        srow[0:1, 0:1], b["ps_S"][0:1, 0:128], AX.X, ALU.max))
    p.wait("tensor", p.vs, vv)
    tv = p.op("tensor", lambda E: E.matmul(ps_b1[:, 0:1], ones1f[:, :],
                                           srow[0:1, 0:1], start=True, stop=True),
              inc=p.ts)
    p.wait("vector", p.ts, tv)
    p.vop(lambda E: E.tensor_copy(s0col[:, :], ps_b1[:, 0:1]))
    prep_done = p.vop(lambda E: E.tensor_scalar_mul(
        sg0c[:, :], cols["g0"][:, :], s0col[:, :]))
    pctx.close()

    p.wait("sync", p.vs, prep_done)
    if upto <= 1:
        p.wait("sync", p.qs, p.qs.n)
        p.wait("gpsimd", p.qg, p.qg.n)
        with ctx, nc.Block() as block:
            p.finalize(block)
        return nc

    mctx = ExitStack()
    xch_a = sbuf(mctx, nc, "xch_a", [128, DT * TC], F32)
    xch_b = sbuf(mctx, nc, "xch_b", [128, DT * TC], F32)
    xmid = sbuf(mctx, nc, "xmid", [128, DT * TC], F32)
    xnb = sbuf(mctx, nc, "xnb", [128, DT * TC], BF16)
    st_a = sbuf(mctx, nc, "st_a", [128, D], F32)
    st_b = sbuf(mctx, nc, "st_b", [128, D], F32)
    sb16a = sbuf(mctx, nc, "sb16a", [128, D], BF16)
    sb16b = sbuf(mctx, nc, "sb16b", [128, D], BF16)
    q16o = sbuf(mctx, nc, "q16o", [128, TC], BF16)
    q16ob = sbuf(mctx, nc, "q16ob", [128, TC], BF16)
    q16op = [q16o, q16ob]
    vtmp = sbuf(mctx, nc, "vtmp", [128, TOK], BF16)

    def load_xchunk(ch, src, guard=None):
        xch = xch_a if ch % 2 == 0 else xch_b
        if guard is not None:
            p.wait("sync", *guard)
        for dt in range(DT):
            v = p.dma("sync", xch[:, dt * TC:(dt + 1) * TC],
                      src[dt * 128:(dt + 1) * 128, ch * TC:(ch + 1) * TC])
        return xch, v

    def store_xchunk(ch, xch, dst, ready_ctr, ready_val):
        p.wait("sync", ready_ctr, ready_val)
        for dt in range(DT):
            v = p.dma("sync", dst[dt * 128:(dt + 1) * 128, ch * TC:(ch + 1) * TC],
                      xch[:, dt * TC:(dt + 1) * TC])
        return v

    def bcasts(ch_rows, n=TC):
        A_ap, B_ap = ch_rows
        return p.op("tensor", lambda E: (
            E.matmul(ps_b1[:, 0:n], ones1f[:, :], A_ap, start=True, stop=True),
            E.matmul(ps_b2[:, 0:n], ones1f[:, :], B_ap, start=True, stop=True))[-1],
            inc=p.ts)

    sign_last_ss = p.ss.n
    sign_last_qg = p.qg.n
    p.wait("tensor", p.vs, scan_done)
    p.wait("tensor", p.vs, prep_done)
    p.wait("tensor", p.qg, cg0)
    ps_work = ps_ar
    ps_workb = ps_hf
    work = [ps_work, ps_workb]
    work_free = [0, 0]
    sg_used = {}
    q_used = {}
    prev_bcast_tv = 0
    prev_chunk_done = None
    x16p = [b["x16a"], b["x16b"]]
    sq16p = [b["sq16a"], b["sq16b"]]
    st_free = [0, 0]
    if upto <= 1:
        p.wait("sync", p.qs, p.qs.n)
        p.wait("gpsimd", p.qg, p.qg.n)
        mctx_dummy = None
        with ctx, nc.Block() as block:
            p.finalize(block)
        return nc
    for ch in range(NCH):
        if ch == 0:
            p.wait("sync", p.vs, prep_done)
            p.wait("sync", p.ss, sign_last_ss)
            p.wait("sync", p.qg, sign_last_qg)
        xch, xld = load_xchunk(ch, x_t,
                               guard=(p.vs, prev_chunk_done) if prev_chunk_done else None)
        if ch > 0:
            p.wait("sync", p.ts, prev_bcast_tv)
        for i in range(4):
            rl = p.dma("sync", r1rows[0:1, i * TC:(i + 1) * TC],
                       rows1[i:i + 1, ch * TC:(ch + 1) * TC])
        p.wait("tensor", p.qs, rl)
        if ch > 0:
            p.wait("tensor", p.vs, prev_chunk_done)
        tv = p.op("tensor", lambda E: (
            E.matmul(ps_b1[:, :], ones1f[:, :], r1rows[0:1, 0:TC], start=True, stop=True),
            E.matmul(ps_b2[:, :], ones1f[:, :], r1rows[0:1, TC:2 * TC], start=True, stop=True),
            E.matmul(ps_b3[:, :], ones1f[:, :], r1rows[0:1, 2 * TC:3 * TC], start=True, stop=True))[-1],
            inc=p.ts)
        prev_bcast_tv = tv
        p.wait("vector", p.qs, rl)
        nm = p.vop(lambda E: E.tensor_scalar_mul(
            r1rows[0:1, 3 * TC:4 * TC], r1rows[0:1, 3 * TC:4 * TC], -1.0))
        p.wait("tensor", p.vs, nm)
        p.wait("vector", p.ts, tv)
        p.wait("vector", p.qs, xld)
        p.wait("scalar", p.qs, xld)
        stv = {}
        for dt in range(DT):
            par = dt % 2
            xt = xch[:, dt * TC:(dt + 1) * TC]
            p.wait("tensor", p.vs, work_free[par])
            tv2 = p.op("tensor", lambda E, dt=dt, ch=ch, par=par: (
                E.matmul(work[par][:, 0:TC], woutb16[:, dt * 128:(dt + 1) * 128],
                         h_ext16[:, ch * TC:(ch + 1) * TC], start=True, stop=False,
                         skip_group_check=True),
                E.matmul(work[par][:, 0:TC], ones1f[:, :],
                         r1rows[0:1, 3 * TC:4 * TC], start=False, stop=True,
                         skip_group_check=True))[-1], inc=p.ts)
            p.op("vector", lambda E, xt=xt: E.tensor_mul(tmp1[:, :], xt, ps_b1[:, :]))
            vv = p.op("vector", lambda E: E.tensor_sub(tmp2[:, :], tmp1[:, :],
                                                       ps_b2[:, :]), inc=p.vs)
            p.wait("scalar", p.vs, vv)
            sv = p.op("scalar", lambda E, dt=dt: E.activation(
                tmp3[:, :], tmp2[:, :], AF.Identity,
                bias=cols["lnb"][:, dt:dt + 1], scale=cols["lng"][:, dt:dt + 1]),
                inc=p.ss)
            p.op("vector", lambda E, xt=xt: E.tensor_mul(tmp1[:, :], xt, ps_b3[:, :]))
            p.wait("vector", p.ts, tv2)
            p.wait("vector", p.ss, sv)
            wv = p.op("vector", lambda E, dt=dt, par=par: E.scalar_tensor_tensor(
                tmp2[:, :], tmp3[:, :], cols["d"][:, dt:dt + 1], work[par][:, 0:TC],
                ALU.mult, ALU.add), inc=p.vs)
            work_free[par] = wv
            xmv = p.op("vector", lambda E, dt=dt: E.tensor_add(
                xmid[:, dt * TC:(dt + 1) * TC], tmp1[:, :], tmp2[:, :]), inc=p.vs)
            p.wait("scalar", p.vs, xmv)
            p.wait("scalar", p.ts, st_free[par])
            p.op("scalar", lambda E, dt=dt, par=par: E.copy(
                x16p[par][:, :], xmid[:, dt * TC:(dt + 1) * TC]))
            sqv = p.op("scalar", lambda E, par=par: E.activation(
                sq16p[par][:, :], x16p[par][:, :], AF.Square), inc=p.ss)
            p.wait("tensor", p.ss, sqv)
            stv[dt] = p.op("tensor", lambda E, dt=dt, par=par: (
                E.matmul(b["ps_S"][0:1, :], b["ones16"][:, :], x16p[par][:, :],
                         start=(dt == 0), stop=(dt == DT - 1), skip_group_check=True),
                E.matmul(b["ps_SS"][0:1, :], b["ones16"][:, :], sq16p[par][:, :],
                         start=(dt == 0), stop=(dt == DT - 1), skip_group_check=True))[-1],
                inc=p.ts)
            st_free[par] = stv[dt]
        p.wait("vector", p.ts, stv[DT - 1])
        p.op("vector", lambda E: E.tensor_copy(rb["S"][0:1, :], b["ps_S"][0:1, :]))
        p.op("vector", lambda E: E.tensor_copy(rb["SS"][0:1, :], b["ps_SS"][0:1, :]))
        rm = emit_rowmath_composite(p, rb, with_inner=False, n=TC, off=0)
        p.wait("tensor", p.vs, rm)
        tv = bcasts((rb["A"][0:1, :], rb["B"][0:1, :]))
        p.wait("vector", p.ts, tv)
        p.wait("scalar", p.ts, tv)
        hfv = {}
        for dt in range(DT):
            p.op("vector", lambda E, dt=dt: E.tensor_mul(
                tmp1[:, :], xmid[:, dt * TC:(dt + 1) * TC], ps_b1[:, :]))
            vv = p.op("vector", lambda E: E.tensor_sub(tmp2[:, :], tmp1[:, :],
                                                       ps_b2[:, :]), inc=p.vs)
            p.wait("scalar", p.vs, vv)
            sv = p.op("scalar", lambda E, dt=dt: E.activation(
                xnb[:, dt * TC:(dt + 1) * TC], tmp2[:, :], AF.Identity,
                bias=cols["lnb"][:, dt:dt + 1], scale=cols["lng"][:, dt:dt + 1]),
                inc=p.ss)
            p.wait("tensor", p.ss, sv)
            if dt == 0:
                p.wait("tensor", p.vs, work_free[1])
            hfv[dt] = p.op("tensor", lambda E, dt=dt: E.matmul(
                ps_workb[0:32, 0:TC], wdn16[:, dt * 32:(dt + 1) * 32],
                xnb[:, dt * TC:(dt + 1) * TC],
                start=(dt == 0), stop=(dt == DT - 1)), inc=p.ts)
        p.wait("vector", p.ts, hfv[DT - 1])
        hv = p.op("vector", lambda E: E.tensor_scalar(
            hf_ext16[0:32, 0:TC], ps_workb[0:32, 0:TC], bdn_c[:, :], 0.0,
            ALU.add, ALU.max), inc=p.vs)
        work_free[1] = hv
        p.wait("tensor", p.vs, hv)
        stv = {}
        for dt in range(DT):
            par = dt % 2
            p.wait("tensor", p.vs, work_free[par])
            tv2 = p.op("tensor", lambda E, dt=dt, par=par: E.matmul(
                work[par][:, 0:TC], wupb16[:, dt * 128:(dt + 1) * 128],
                hf_ext16[:, 0:TC], start=True, stop=True), inc=p.ts)
            p.op("vector", lambda E, dt=dt, xch=xch: E.tensor_add(
                tmp1[:, :], xch[:, dt * TC:(dt + 1) * TC],
                xmid[:, dt * TC:(dt + 1) * TC]))
            p.wait("vector", p.ts, tv2)
            xv = p.op("vector", lambda E, dt=dt, xch=xch, par=par: E.tensor_add(
                xch[:, dt * TC:(dt + 1) * TC], tmp1[:, :], work[par][:, 0:TC]),
                inc=p.vs)
            work_free[par] = xv
            p.wait("scalar", p.vs, xv)
            p.wait("scalar", p.ts, st_free[par])
            p.op("scalar", lambda E, dt=dt, par=par, xch=xch: E.copy(
                x16p[par][:, :], xch[:, dt * TC:(dt + 1) * TC]))
            sqv = p.op("scalar", lambda E, par=par: E.activation(
                sq16p[par][:, :], x16p[par][:, :], AF.Square), inc=p.ss)
            p.wait("tensor", p.ss, sqv)
            stv[dt] = p.op("tensor", lambda E, dt=dt, par=par: (
                E.matmul(b["ps_S"][0:1, :], b["ones16"][:, :], x16p[par][:, :],
                         start=(dt == 0), stop=(dt == DT - 1), skip_group_check=True),
                E.matmul(b["ps_SS"][0:1, :], b["ones16"][:, :], sq16p[par][:, :],
                         start=(dt == 0), stop=(dt == DT - 1), skip_group_check=True))[-1],
                inc=p.ts)
            st_free[par] = stv[dt]
        p.wait("vector", p.ts, stv[DT - 1])
        p.op("vector", lambda E: E.tensor_copy(rb["S"][0:1, :], b["ps_S"][0:1, :]))
        p.op("vector", lambda E: E.tensor_copy(rb["SS"][0:1, :], b["ps_SS"][0:1, :]))
        rm = emit_rowmath_composite(p, rb, with_inner=True, n=TC, off=0)
        p.wait("tensor", p.vs, rm)
        tv = bcasts((rb["A"][0:1, :], rb["B"][0:1, :]))
        p.wait("vector", p.ts, tv)
        p.wait("scalar", p.ts, tv)
        for dt in range(DT):
            p.op("vector", lambda E, dt=dt, xch=xch: E.tensor_mul(
                tmp1[:, :], xch[:, dt * TC:(dt + 1) * TC], ps_b1[:, :]))
            vv = p.op("vector", lambda E: E.tensor_sub(tmp2[:, :], tmp1[:, :],
                                                       ps_b2[:, :]), inc=p.vs)
            p.wait("scalar", p.vs, vv)
            sv = p.op("scalar", lambda E, dt=dt: E.activation(
                xnb[:, dt * TC:(dt + 1) * TC], tmp2[:, :], AF.Identity,
                bias=cols["lnbb"][:, dt:dt + 1], scale=cols["lnw"][:, dt:dt + 1]),
                inc=p.ss)
        xn_ready = sv
        p.wait("tensor", p.ss, xn_ready)
        stv = {}
        for do in range(DT):
            par = do % 2
            sb = sb16a if par == 0 else sb16b
            if (ch, do) != (0, 0) and (do >= 2 or ch > 0):
                key = (ch, do - 2) if do >= 2 else (ch - 1, DT - 2 + do)
                p.wait("sync", p.ts, sg_used[key])
            lv = p.dma("sync", sb[:, :], sgs[do * 128:(do + 1) * 128, :])
            p.wait("tensor", p.qs, lv)
            p.wait("tensor", p.vs, work_free[par])
            tv2 = None
            for dt in range(DT):
                tv2 = p.op("tensor", lambda E, dt=dt, sb=sb, par=par: E.matmul(
                    work[par][:, 0:TC], sb[:, dt * 128:(dt + 1) * 128],
                    xnb[:, dt * TC:(dt + 1) * TC],
                    start=(dt == 0), stop=(dt == DT - 1)),
                    inc=p.ts if dt == DT - 1 else None)
            sg_used[(ch, do)] = tv2
            p.wait("vector", p.ts, tv2)
            p.op("vector", lambda E, do=do, par=par: E.tensor_scalar(
                tmp1[:, :], work[par][:, 0:TC], sg0c[:, do:do + 1],
                cols["be0"][:, do:do + 1], ALU.mult, ALU.add))
            xv = p.op("vector", lambda E, do=do, xch=xch: E.tensor_add(
                xch[:, do * TC:(do + 1) * TC], xch[:, do * TC:(do + 1) * TC],
                tmp1[:, :]), inc=p.vs)
            work_free[par] = xv
            p.wait("scalar", p.vs, xv)
            p.wait("scalar", p.ts, st_free[par])
            p.op("scalar", lambda E, do=do, par=par, xch=xch: E.copy(
                x16p[par][:, :], xch[:, do * TC:(do + 1) * TC]))
            sqv = p.op("scalar", lambda E, par=par: E.activation(
                sq16p[par][:, :], x16p[par][:, :], AF.Square), inc=p.ss)
            p.wait("tensor", p.ss, sqv)
            stv[do] = p.op("tensor", lambda E, do=do, par=par: (
                E.matmul(b["ps_S"][0:1, :], b["ones16"][:, :], x16p[par][:, :],
                         start=(do == 0), stop=(do == DT - 1), skip_group_check=True),
                E.matmul(b["ps_SS"][0:1, :], b["ones16"][:, :], sq16p[par][:, :],
                         start=(do == 0), stop=(do == DT - 1), skip_group_check=True))[-1],
                inc=p.ts)
            st_free[par] = stv[do]
        sv2 = store_xchunk(ch, xch, x2h, p.vs, work_free[(DT - 1) % 2])
        p.wait("vector", p.ts, stv[DT - 1])
        p.op("vector", lambda E: E.tensor_copy(rb["S"][0:1, :], b["ps_S"][0:1, :]))
        p.op("vector", lambda E: E.tensor_copy(rb["SS"][0:1, :], b["ps_SS"][0:1, :]))
        rm = emit_rowmath_composite(p, rb, with_inner=False, n=TC, off=0)
        p.wait("tensor", p.vs, rm)
        tv = bcasts((rb["A"][0:1, :], rb["B"][0:1, :]))
        p.wait("vector", p.ts, tv)
        for dt in range(DT):
            p.op("vector", lambda E, dt=dt, xch=xch: E.tensor_mul(
                tmp1[:, :], xch[:, dt * TC:(dt + 1) * TC], ps_b1[:, :]))
            vv = p.op("vector", lambda E, dt=dt: E.tensor_sub(
                xnb[:, dt * TC:(dt + 1) * TC], tmp1[:, :], ps_b2[:, :]), inc=p.vs)
        xn_ready = vv
        for do in range(DT):
            par = do % 2
            sb = sb16a if par == 0 else sb16b
            if do >= 2:
                p.wait("gpsimd", p.ts, q_used[(ch, do - 2)])
            else:
                p.wait("gpsimd", p.ts, sg_used[(ch, DT - 2 + do)])
            lv = p.dma("gpsimd", sb[:, :], aqa[do * 128:(do + 1) * 128, :])
            p.wait("tensor", p.qg, lv)
            p.wait("tensor", p.vs, xn_ready)
            p.wait("tensor", p.vs, work_free[par])
            tv2 = None
            for dt in range(DT):
                tv2 = p.op("tensor", lambda E, dt=dt, sb=sb, par=par: E.matmul(
                    work[par][:, 0:TC], sb[:, dt * 128:(dt + 1) * 128],
                    xnb[:, dt * TC:(dt + 1) * TC],
                    start=(dt == 0), stop=(dt == DT - 1)),
                    inc=p.ts if dt == DT - 1 else None)
            q_used[(ch, do)] = tv2
            p.wait("vector", p.ts, tv2)
            if ch > 0 or do >= 2:
                p.wait("vector", p.qs, q_st.get((ch, do - 2) if do >= 2 else
                                                (ch - 1, DT - 2 + do), 0))
            qv = p.op("vector", lambda E, do=do, par=par: E.tensor_scalar(
                q16op[par][:, :], work[par][:, 0:TC], cols["bq"][:, do:do + 1], None,
                ALU.add), inc=p.vs)
            work_free[par] = qv
            p.wait("sync", p.vs, qv)
            if ch == 0 and do == 0:
                q_st = {}
            q_st[(ch, do)] = p.dma("sync", qh[do * 128:(do + 1) * 128,
                                              ch * TC:(ch + 1) * TC], q16op[par][:, :])
        p.wait("tensor", p.qg, cg0)
        for (a16w, bcol, outh) in ((ak16, bk_c, kh), (av16, bv_c, None)):
            par = 0 if outh is not None else 1
            p.wait("tensor", p.vs, work_free[par])
            tv2 = None
            for dt in range(DT):
                tv2 = p.op("tensor", lambda E, dt=dt, a16w=a16w, par=par: E.matmul(
                    work[par][:, 0:TC], a16w[:, dt * 128:(dt + 1) * 128],
                    xnb[:, dt * TC:(dt + 1) * TC],
                    start=(dt == 0), stop=(dt == DT - 1)),
                    inc=p.ts if dt == DT - 1 else None)
            p.wait("vector", p.ts, tv2)
            if outh is not None:
                p.wait("vector", p.qs, q_st[(ch, DT - 1)])
                qv = p.op("vector", lambda E, bcol=bcol, par=par: E.tensor_scalar(
                    q16op[par][:, :], work[par][:, 0:TC], bcol[:, :], None, ALU.add),
                    inc=p.vs)
                work_free[par] = qv
                p.wait("sync", p.vs, qv)
                q_st[(ch, DT)] = p.dma("sync", outh[:, ch * TC:(ch + 1) * TC],
                                       q16op[par][:, :])
            else:
                qv = p.op("vector", lambda E, bcol=bcol, ch=ch, par=par: E.tensor_scalar(
                    vtmp[:, ch * TC:(ch + 1) * TC], work[par][:, 0:TC], bcol[:, :],
                    None, ALU.add), inc=p.vs)
                work_free[par] = qv
        prev_chunk_done = qv
    prev_done2 = prev_chunk_done

    p.wait("tensor", p.vs, prev_done2)
    vt_prev = None
    for ktl in range(DT):
        if ktl > 0:
            p.wait("tensor", p.vs, vv)
        tv = p.op("tensor", lambda E, ktl=ktl: E.matmul(
            ps_ar[:, 0:128], vtmp[:, ktl * 128:(ktl + 1) * 128], idt16[:, :],
            start=True, stop=True), inc=p.ts)
        p.wait("vector", p.ts, tv)
        if vt_prev is None:
            p.wait("vector", p.qs, q_st[(NCH - 1, DT)])
        else:
            p.wait("vector", p.qg, vt_prev)
        vv = p.op("vector", lambda E: E.tensor_copy(q16o[:, 0:128], ps_ar[:, 0:128]),
                  inc=p.vs)
        p.wait("gpsimd", p.vs, vv)
        vt_prev = p.dma("gpsimd", vh[:, ktl * 128:(ktl + 1) * 128], q16o[:, 0:128])

    if upto <= 4:
        p.wait("sync", p.qs, p.qs.n)
        p.wait("gpsimd", p.qg, p.qg.n)
        mctx.close()
        with ctx, nc.Block() as block:
            p.finalize(block)
        return nc

    mctx.close()
    quant_guard_ts = p.ts.n
    quant_guard_vs = p.vs.n
    quant_guard_qg = p.qg.n
    p.wait("sync", p.ts, quant_guard_ts)
    p.wait("sync", p.vs, quant_guard_vs)
    p.wait("sync", p.qg, quant_guard_qg)
    qctx = ExitStack()
    HH = HID // 2
    qst_a = sbuf(qctx, nc, "qst_a", [128, HH], F32)
    qst_b = sbuf(qctx, nc, "qst_b", [128, HH], F32)
    qa_t = sbuf(qctx, nc, "qa_t", [128, HH], F32)
    qb_t = sbuf(qctx, nc, "qb_t", [128, HH], F32)
    q16h = sbuf(qctx, nc, "q16h", [128, HH], BF16)
    for (wa, wq, nstrip, wid) in ((w1a, w1qo, DT, HID), (w2a, w2qo, DT, HID),
                                  (w3a, w3qo, NHT, D)):
        nh = 2 if wid == HID else 1
        seg = wid // nh
        if wa is not w1a:
            p.wait("sync", p.vs, p.vs.n)
        p.op("vector", lambda E: E.memset(gacc[:, :], 0.0))
        g_done = {}
        for i in range(nstrip * nh):
            qst = qst_a if i % 2 == 0 else qst_b
            dt_i, h_i = divmod(i, nh)
            if i >= 2:
                p.wait("sync", p.vs, g_done[i - 2])
            lv = p.dma("sync", qst[:, 0:seg],
                       wa[dt_i * 128:(dt_i + 1) * 128, h_i * seg:(h_i + 1) * seg])
            p.wait("vector", p.qs, lv)
            p.vop(lambda E, qst=qst, seg=seg: E.tensor_reduce(
                mxt[:, :], qst[:, 0:seg], AX.X, ALU.add, apply_absolute_value=True))
            gv = p.vop(lambda E: E.tensor_add(gacc[:, :], gacc[:, :], mxt[:, :]))
            g_done[i] = gv
        p.wait("tensor", p.vs, g_done[nstrip * nh - 1])
        tv = p.op("tensor", lambda E: E.matmul(b["ps_S"][0:1, 0:128], gacc[:, :],
                                               idt32[:, :], start=True, stop=True),
                  inc=p.ts)
        p.wait("vector", p.ts, tv)
        p.vop(lambda E: E.tensor_reduce(srow[0:1, 1:2],
                                        b["ps_S"][0:1, 0:128], AX.X, ALU.add))
        nelem = float(D * HID)
        p.vop(lambda E: E.tensor_scalar(
            srow[0:1, 1:2], srow[0:1, 1:2], 0.5 / nelem, 0.5e-8, ALU.mult, ALU.add))
        vv = p.vop(lambda E: E.tensor_scalar_mul(
            srow[0:1, 2:3], srow[0:1, 1:2], -1.0))
        p.wait("tensor", p.vs, vv)
        tv = p.op("tensor", lambda E: (
            E.matmul(ps_b1[:, 0:1], ones1f[:, :], srow[0:1, 1:2], start=True, stop=True),
            E.matmul(ps_b2[:, 0:1], ones1f[:, :], srow[0:1, 2:3], start=True, stop=True))[-1],
            inc=p.ts)
        p.wait("vector", p.ts, tv)
        p.vop(lambda E: E.tensor_copy(thp_c[:, :], ps_b1[:, 0:1]))
        thv = p.vop(lambda E: E.tensor_copy(thn_c[:, :], ps_b2[:, 0:1]))
        qq_done = {}
        for i in range(nstrip * nh):
            qst = qst_a if i % 2 == 0 else qst_b
            dt_i, h_i = divmod(i, nh)
            if i >= 2:
                p.wait("sync", p.vs, qq_done[i - 2])
            lv = p.dma("sync", qst[:, 0:seg],
                       wa[dt_i * 128:(dt_i + 1) * 128, h_i * seg:(h_i + 1) * seg])
            p.wait("vector", p.qs, lv)
            p.wait("vector", p.vs, thv)
            av = p.op("vector", lambda E, qst=qst, seg=seg: E.tensor_single_scalar(
                qa_t[:, 0:seg], qst[:, 0:seg], thp_c[:, :], ALU.is_gt), inc=p.vs)
            bv_ = p.op("vector", lambda E, qst=qst, seg=seg: E.tensor_single_scalar(
                qb_t[:, 0:seg], qst[:, 0:seg], thn_c[:, :], ALU.is_lt), inc=p.vs)
            if i >= 1:
                p.wait("vector", p.qg, qw_prev)
            gv = p.op("vector", lambda E, seg=seg: E.tensor_sub(
                q16h[:, 0:seg], qa_t[:, 0:seg], qb_t[:, 0:seg]), inc=p.vs)
            qq_done[i] = gv
            p.wait("gpsimd", p.vs, gv)
            qw_prev = p.dma("gpsimd", wq[dt_i * 128:(dt_i + 1) * 128,
                                         h_i * seg:(h_i + 1) * seg], q16h[:, 0:seg])
    p.wait("gpsimd", p.qg, p.qg.n)
    p.wait("sync", p.qs, p.qs.n)
    qctx.close()

    with ctx, nc.Block() as block:
        p.finalize(block)
    return nc



def _cols(v):
    return np.ascontiguousarray(v.reshape(DT, 128).T).astype(np.float32)


def _strips(w_t, m_tiles):
    Kd, M = w_t.shape
    kt = Kd // 128
    a = w_t.reshape(kt, 128, m_tiles, 128)
    a = a.transpose(2, 1, 0, 3).reshape(m_tiles * 128, kt * 128)
    return np.ascontiguousarray(a).astype(np.float32)


def _stage_common(inp):
    st = {}
    st["lng_cols"] = _cols(inp["L0_attn_ln_gamma"])
    st["lnb_cols"] = _cols(inp["L0_attn_ln_beta"])
    st["win_arr"] = np.ascontiguousarray(
        inp["L0_attn_W_in"].reshape(DT, 128, R).transpose(1, 0, 2)
        .reshape(128, DT * R)).astype(np.float32)
    st["b_in_col"] = inp["L0_attn_b_in"].reshape(R, 1).astype(np.float32)
    st["dec_col"] = inp["L0_attn_decay_logit"].reshape(R, 1).astype(np.float32)
    st["woutb"] = np.ascontiguousarray(np.concatenate(
        [inp["L0_attn_W_out"], inp["L0_attn_b_out"][None, :]], 0)).astype(np.float32)
    st["d_cols"] = _cols(inp["L0_attn_D_skip"])
    st["wdn_arr"] = np.ascontiguousarray(
        inp["L0_attn_W_down"].reshape(DT, 128, 32).transpose(1, 0, 2)
        .reshape(128, DT * 32)).astype(np.float32)
    st["bdn_col"] = inp["L0_attn_b_down"].reshape(32, 1).astype(np.float32)
    st["wupb"] = np.ascontiguousarray(np.concatenate(
        [inp["L0_attn_W_up"], inp["L0_attn_b_up"][None, :]], 0)).astype(np.float32)
    st["l0w_arr"] = _strips(np.ascontiguousarray(inp["L0_ffn_weight"].T), DT)
    st["g0_cols"] = _cols(inp["L0_ffn_gamma"])
    st["be0_cols"] = _cols(inp["L0_ffn_beta"])
    st["lnw_cols"] = _cols(inp["L0_ffn_ln_weight"])
    st["lnbb_cols"] = _cols(inp["L0_ffn_ln_bias"])
    st["w1_arr"] = np.ascontiguousarray(
        inp["L1_ffn_w1"].T.reshape(DT, 128, HID).reshape(D, HID)).astype(np.float32)
    st["w2_arr"] = np.ascontiguousarray(inp["L1_ffn_w2"].T).astype(np.float32)
    st["w3_arr"] = np.ascontiguousarray(inp["L1_ffn_w3"].T).astype(np.float32)
    st["aq_arr"] = _strips(np.ascontiguousarray(inp["L1_attn_Wq"].T), DT)
    st["ak_arr"] = np.ascontiguousarray(
        inp["L1_attn_Wk"].reshape(128, DT, 128).transpose(2, 1, 0)
        .reshape(128, D)).astype(np.float32)
    st["av_arr"] = np.ascontiguousarray(
        inp["L1_attn_Wv"].reshape(128, DT, 128).transpose(2, 1, 0)
        .reshape(128, D)).astype(np.float32)
    st["bq_cols"] = _cols(inp["L1_attn_bq"])
    st["bk_col"] = inp["L1_attn_bk"].reshape(128, 1).astype(np.float32)
    st["bv_col"] = inp["L1_attn_bv"].reshape(128, 1).astype(np.float32)
    st["idt32"] = np.eye(128, dtype=np.float32)
    st["ao_arr"] = _strips(np.ascontiguousarray(inp["L1_attn_Wo"].T), DT)
    st["bo_cols"] = _cols(inp["L1_attn_bo"])
    return st


def _in1(st, x_t_c):
    return dict(x_t=x_t_c, lng_cols=st["lng_cols"], lnb_cols=st["lnb_cols"],
                win_arr=st["win_arr"], b_in_col=st["b_in_col"])


def _in2(st, x_t_c, rows1_c, u_pad_c):
    keys = ["dec_col", "woutb", "d_cols", "lng_cols", "lnb_cols", "wdn_arr",
            "bdn_col", "wupb", "l0w_arr", "g0_cols", "be0_cols", "lnw_cols",
            "lnbb_cols", "w1_arr", "w2_arr", "w3_arr", "aq_arr", "ak_arr",
            "av_arr", "bq_cols", "bk_col", "bv_col", "idt32"]
    d = {k: st[k] for k in keys}
    d["x_t"] = x_t_c
    d["rows1"] = rows1_c
    d["u_pad"] = u_pad_c
    return d


def _masks_for_core(c4):
    qb0 = c4 * TOK
    sg = np.arange(SEQ, dtype=np.int32)[:, None]
    qg = qb0 + np.arange(TOK, dtype=np.int32)[None, :]
    M = (sg < qg)
    M = M.reshape(NKT, 128, NCH, TC).transpose(2, 0, 1, 3).reshape(NCH * NKT * 128, TC)
    return np.ascontiguousarray(M).astype(ml_dtypes.bfloat16)


def _in3(st, c4, x2c, qc_, kfull, vfull):
    return dict(x2_hand=x2c, q_own=qc_, k_full=kfull, v_full=vfull,
                masks=st["masks"][c4], ao_arr=st["ao_arr"], bo_cols=st["bo_cols"],
                w1q=st["_w1q"], w2q=st["_w2q"], w3q=st["_w3q"])


def run_launches12(inp, cores=None):
    if cores is None:
        cores = list(range(8))
    x = inp["L0_attn_x"].astype(np.float32)
    st = _stage_common(inp)
    x_t = []
    for c in cores:
        bidx, c4 = c // 4, c % 4
        x_t.append(np.ascontiguousarray(x[bidx, c4 * TOK:(c4 + 1) * TOK, :].T))

    nc1 = build_launch1()
    in1 = [_in1(st, x_t[c]) for c in cores]
    r1 = run_bass_kernel_spmd(nc1, in1, core_ids=cores).results

    u_full = np.zeros((2, R, SEQ), np.float32)
    for c in cores:
        bidx, c4 = c // 4, c % 4
        u_full[bidx, :, c4 * TOK:(c4 + 1) * TOK] = r1[c]["u_hand"]

    nc2 = build_launch2()
    in2 = []
    for c in cores:
        bidx, c4 = c // 4, c % 4
        off = c4 * TOK
        u_pad = np.zeros((R, SEQ), np.float32)
        u_pad[:, SEQ - (off + TOK):] = u_full[bidx][:, :off + TOK]
        in2.append(_in2(st, x_t[c], r1[c]["rows1"], np.ascontiguousarray(u_pad)))
    r2 = run_bass_kernel_spmd(nc2, in2, core_ids=cores).results
    return st, x_t, r2


def kernel(**inp):
    inp = {k: np.asarray(v) for k, v in inp.items()}
    cores = list(range(8))
    st, x_t, r2 = run_launches12(inp, cores)
    st["masks"] = [_masks_for_core(c4) for c4 in range(4)]
    kfull = {}
    vfull = {}
    for bi in range(2):
        kk = np.zeros((128, SEQ), ml_dtypes.bfloat16)
        vv = np.zeros((128, SEQ), ml_dtypes.bfloat16)
        for c4 in range(4):
            c = bi * 4 + c4
            kk[:, c4 * TOK:(c4 + 1) * TOK] = r2[c]["k_hand"]
            vv[:, c4 * TOK:(c4 + 1) * TOK] = r2[c]["v_hand"]
        kfull[bi] = kk
        vfull[bi] = vv
    st["_w1q"], st["_w2q"], st["_w3q"] = (r2[0]["w1q"], r2[0]["w2q"], r2[0]["w3q"])
    nc3 = build_launch3()
    in3 = []
    for c in cores:
        bi, c4 = c // 4, c % 4
        st["_w1q"], st["_w2q"], st["_w3q"] = (r2[c]["w1q"], r2[c]["w2q"],
                                              r2[c]["w3q"])
        in3.append(_in3(st, c4, r2[c]["x2_hand"], r2[c]["q_hand"],
                        kfull[bi], vfull[bi]))
    r3 = run_bass_kernel_spmd(nc3, in3, core_ids=cores).results
    out = np.zeros((2, SEQ, D), np.float32)
    for c in cores:
        bi, c4 = c // 4, c % 4
        out[bi, c4 * TOK:(c4 + 1) * TOK, :] = r3[c]["out_t"].T
    return out


# version 05
# speedup vs baseline: 1.3187x; 1.3187x over previous
import numpy as np
import ml_dtypes
from contextlib import ExitStack
import concourse.bass as bass
import concourse.mybir as mybir
from concourse.bass_utils import run_bass_kernel_spmd

F32 = mybir.dt.float32
BF16 = mybir.dt.bfloat16
AF = mybir.ActivationFunctionType
ALU = mybir.AluOpType
AX = mybir.AxisListType

D = 2048
DT = 16
TOK = 2048
TC = 512
NCH = TOK // TC
SEQ = 8192
R = 8
EPS = 1e-5
HID = 5632
NHT = HID // 128
NHC = 11
NKT = 64
NKTQ = [64, 48, 32, 16]
MOFF = [0, 64, 112, 144]
MTOT = 160
SCALE_ATTN = float(1.0 / np.sqrt(128.0))
INVD = float(1.0 / D)


class Ctr:

    def __init__(self, ctx, nc, name):
        self.h = ctx.enter_context(nc.semaphore(name))
        self.n = 0

    def bump(self, k):
        self.n += k
        return self.n


class Prog:

    ENGINES = ("sync", "gpsimd", "tensor", "scalar", "vector")

    def __init__(self, nc, ctx):
        self.nc = nc
        self.ctx = ctx
        self.cmds = {e: [] for e in self.ENGINES}
        self.qs = Ctr(ctx, nc, "qs")
        self.qg = Ctr(ctx, nc, "qg")
        self.ts = Ctr(ctx, nc, "ts")
        self.ss = Ctr(ctx, nc, "ss")
        self.vs = Ctr(ctx, nc, "vs")
        self.gs = Ctr(ctx, nc, "gs")

    def dma(self, eng, dst, src):
        q = self.qs if eng == "sync" else self.qg
        v = q.bump(16)
        self.cmds[eng].append(lambda E: E.dma_start(out=dst, in_=src).then_inc(q.h, 16))
        return v

    def wait(self, eng, ctr, val):
        self.cmds[eng].append(lambda E: E.wait_ge(ctr.h, val))

    def op(self, eng, f, inc=None):
        if inc is None:
            self.cmds[eng].append(lambda E: f(E))
            return None
        v = inc.bump(1)
        self.cmds[eng].append(lambda E: f(E).then_inc(inc.h, 1))
        return v

    def vop(self, f):
        v = self.op("vector", f, inc=self.vs)
        self.wait("vector", self.vs, v)
        return v

    def sop(self, f):
        v = self.op("scalar", f, inc=self.ss)
        self.wait("scalar", self.ss, v)
        return v

    def finalize(self, block):
        for name in self.ENGINES:
            cl = self.cmds[name]
            def body(E, cl=cl):
                for f in cl:
                    f(E)
            getattr(block, name)(body)


def sbuf(ctx, nc, name, shape, dt=F32):
    return ctx.enter_context(nc.sbuf_tensor(name, shape, dt))


def psum(ctx, nc, name, shape, dt=F32):
    return ctx.enter_context(nc.psum_tensor(name, shape, dt))



def build_launch3():
    nc = bass.Bass("TRN2", target_bir_lowering=False)
    ap = lambda n, s, d=F32, k="ExternalInput": nc.dram_tensor(n, s, d, kind=k).ap()
    x2h = ap("x2_hand", [D, TOK])
    qo = ap("q_own", [D, TOK], BF16)
    kf_d = ap("k_full", [128, SEQ], BF16)
    vf_d = ap("v_full", [128, SEQ], BF16)
    masks = ap("masks", [MTOT * 128, TC], BF16)
    aoa = ap("ao_arr", [D, D])
    bo = ap("bo_cols", [128, DT])
    w1qh = ap("w1q", [D, HID], BF16)
    w2qh = ap("w2q", [D, HID], BF16)
    w3qh = ap("w3q", [HID, D], BF16)
    out_t = ap("out_t", [D, TOK], F32, "ExternalOutput")

    ctx = ExitStack()
    p = Prog(nc, ctx)
    b = {}
    b["x16a"] = sbuf(ctx, nc, "x16a", [128, TC], BF16)
    b["x16b"] = sbuf(ctx, nc, "x16b", [128, TC], BF16)
    b["sq16a"] = sbuf(ctx, nc, "sq16a", [128, TC], BF16)
    b["sq16b"] = sbuf(ctx, nc, "sq16b", [128, TC], BF16)
    b["ones16"] = sbuf(ctx, nc, "ones16", [128, 1], BF16)
    ones1f = sbuf(ctx, nc, "ones1f", [1, 128], F32)
    rb = {k: sbuf(ctx, nc, "row_" + k, [1, TC], F32)
          for k in ("S", "SS", "A", "B", "r", "mur", "t0", "t1", "t2")}
    tmp1 = sbuf(ctx, nc, "tmp1", [128, TC], F32)
    kf = sbuf(ctx, nc, "kf", [128, SEQ], BF16)
    vf = sbuf(ctx, nc, "vf", [128, SEQ], BF16)
    qp_a = sbuf(ctx, nc, "qp_a", [128, 2 * TC], BF16)
    qp_b = qp_a
    AO = sbuf(ctx, nc, "AO", [128, DT * TC], BF16)
    acc = sbuf(ctx, nc, "acc", [128, DT * TC], F32)
    xnb4 = sbuf(ctx, nc, "xnb4", [128, DT * TC], BF16)
    uni = sbuf(ctx, nc, "uni", [128, NKT * TC], BF16)
    w1s = uni[:, 0:DT * TC]
    w2s = uni[:, DT * TC:2 * DT * TC]
    w3s = uni[:, 2 * DT * TC:2 * DT * TC + 4 * D]
    exp_sb = sbuf(ctx, nc, "exp_sb", [128, 4 * TC], BF16)
    rs_row = sbuf(ctx, nc, "rs_row", [1, 2 * TC], F32)
    rs_rec = sbuf(ctx, nc, "rs_rec", [1, 2 * TC], F32)
    rcb = tmp1
    sb3a = sbuf(ctx, nc, "sb3a", [128, D], BF16)
    sb3b = sbuf(ctx, nc, "sb3b", [128, D], BF16)
    bo_c = sbuf(ctx, nc, "bo_c", [128, DT], F32)
    hid_sb = exp_sb
    sg16 = b["x16a"]

    ps_av = psum(ctx, nc, "ps_av", [128, 2 * TC])
    ps_rs = psum(ctx, nc, "ps_rs", [1, TC])
    ps_sc = psum(ctx, nc, "ps_sc", [128, 4 * TC])
    ps_bc = psum(ctx, nc, "ps_bc", [128, TC])
    b["ps_S"] = ps_bc
    b["ps_SS"] = ps_av

    p.dma("sync", bo_c[:, :], bo)
    p.dma("sync", kf[:, :], kf_d)
    kvl = p.dma("sync", vf[:, :], vf_d)
    mz = p.op("vector", lambda E: (E.memset(b["ones16"][:, :], 1.0),
                                   E.memset(ones1f[:, :], 1.0))[-1], inc=p.vs)

    prev_h_done = mz
    prev_exp = 0
    prev_ao_st = {}
    last_acc_store = 0
    prev_ffn_vs = mz
    prev_w3_use = 0
    prev_sg = 0
    last_scores = {}
    for qc in range(NCH):
        p.wait("gpsimd", p.ts, prev_w3_use)
        p.wait("gpsimd", p.vs, prev_ffn_vs)
        nktq = NKTQ[qc]
        mask_q = {}
        for kt in range(nktq):
            mask_q[kt] = p.dma("gpsimd",
                               uni[:, kt * TC:(kt + 1) * TC],
                               masks[(MOFF[qc] + kt) * 128:
                                     (MOFF[qc] + kt) * 128 + 128, :])
        if qc > 0:
            p.wait("sync", p.vs, prev_ffn_vs)
        for dt in range(DT):
            x2l = p.dma("sync", acc[:, dt * TC:(dt + 1) * TC],
                        x2h[dt * 128:(dt + 1) * 128, qc * TC:(qc + 1) * TC])
        p.wait("tensor", p.qg, mask_q[nktq - 1])
        p.wait("tensor", p.qs, kvl)
        exp_done = {}
        mask_done = {}
        av_done = {}
        epi_av = {}
        epi_rs = {}
        for h in range(DT):
            qb = qp_a[:, 0:TC] if h % 2 == 0 else qp_a[:, TC:2 * TC]
            ps_avh = ps_av[:, 0:TC] if h % 2 == 0 else ps_av[:, TC:2 * TC]
            if (qc, h) != (0, 0) and (h >= 2 or qc > 0):
                key = (qc, h - 2) if h >= 2 else (qc - 1, DT - 2 + h)
                p.wait("sync", p.ts, last_scores[key])
            qldv = p.dma("sync", qb,
                         qo[h * 128:(h + 1) * 128, qc * TC:(qc + 1) * TC])
            p.wait("tensor", p.qs, qldv)
            NG = nktq // 2
            for i in range(NG):
                par = i % 2
                scs = ps_sc[:, par * 2 * TC:(par + 1) * 2 * TC]
                esl = exp_sb[:, par * 2 * TC:(par + 1) * 2 * TC]
                if i >= 2:
                    p.wait("tensor", p.ss, exp_done[(h, i - 2)])
                elif h > 0:
                    p.wait("tensor", p.ss, exp_done[(h - 1, NG - 2 + i)])
                elif qc > 0 and prev_exp:
                    p.wait("tensor", p.ss, prev_exp)
                if h == 0 and i < 2:
                    p.wait("tensor", p.vs, prev_h_done)
                    if qc > 0:
                        p.wait("tensor", p.vs, prev_ffn_vs)
                tv = None
                for j in range(2):
                    kt = i * 2 + j
                    tv = p.op("tensor", lambda E, j=j, kt=kt, par=par, qb=qb: E.matmul(
                        ps_sc[:, (par * 2 + j) * TC:(par * 2 + j + 1) * TC],
                        kf[:, kt * 128:(kt + 1) * 128], qb,
                        start=True, stop=True), inc=p.ts if j == 1 else None)
                if i == NG - 1:
                    last_scores[(qc, h)] = tv
                if i >= 1:
                    ip = i - 1
                    pp = ip % 2
                    p.wait("tensor", p.vs, mask_done[(h, ip)])
                    if ip == 0 and h >= 2:
                        p.wait("tensor", p.vs, epi_av[h - 2])
                    if ip == 0 and h >= 1:
                        p.wait("tensor", p.vs, epi_rs[h - 1])
                    av = None
                    for j in range(2):
                        kt = ip * 2 + j
                        p.op("tensor", lambda E, j=j, kt=kt, pp=pp, ip=ip,
                             ps_avh=ps_avh: E.matmul(
                            ps_avh, vf[:, kt * 128:(kt + 1) * 128],
                            exp_sb[:, (pp * 2 + j) * TC:(pp * 2 + j + 1) * TC],
                            start=(ip == 0 and j == 0), stop=False,
                            skip_group_check=True))
                        av = p.op("tensor", lambda E, j=j, pp=pp, ip=ip: E.matmul(
                            ps_rs[:, :], b["ones16"][:, :],
                            exp_sb[:, (pp * 2 + j) * TC:(pp * 2 + j + 1) * TC],
                            start=(ip == 0 and j == 0), stop=False,
                            skip_group_check=True), inc=p.ts if j == 1 else None)
                    av_done[(h, ip)] = av
                p.wait("scalar", p.ts, tv)
                if i >= 2:
                    p.wait("scalar", p.ts, av_done[(h, i - 2)])
                sv = p.op("scalar", lambda E, scs=scs, esl=esl: E.activation(
                    esl, scs, AF.Exp, scale=SCALE_ATTN), inc=p.ss)
                exp_done[(h, i)] = sv
                p.wait("vector", p.ss, sv)
                if h == 0:
                    p.wait("vector", p.qg, mask_q[i * 2 + 1])
                mv2 = p.op("vector", lambda E, i=i, esl=esl: E.tensor_mul(
                    esl, esl, uni[:, i * 2 * TC:(i + 1) * 2 * TC]), inc=p.vs)
                mask_done[(h, i)] = mv2
            ip = NG - 1
            pp = ip % 2
            p.wait("tensor", p.vs, mask_done[(h, ip)])
            av = None
            for j in range(2):
                kt = ip * 2 + j
                p.op("tensor", lambda E, j=j, kt=kt, pp=pp, ps_avh=ps_avh: E.matmul(
                    ps_avh, vf[:, kt * 128:(kt + 1) * 128],
                    exp_sb[:, (pp * 2 + j) * TC:(pp * 2 + j + 1) * TC],
                    start=False, stop=(j == 1), skip_group_check=True))
                av = p.op("tensor", lambda E, j=j, pp=pp: E.matmul(
                    ps_rs[:, :], b["ones16"][:, :],
                    exp_sb[:, (pp * 2 + j) * TC:(pp * 2 + j + 1) * TC],
                    start=False, stop=(j == 1), skip_group_check=True),
                    inc=p.ts if j == 1 else None)
            av_done[(h, ip)] = av
            p.wait("vector", p.ts, av)
            erv = p.vop(lambda E: E.tensor_single_scalar(
                rs_row[0:1, 0:TC], ps_rs[0:1, :], 1e-30, ALU.max))
            epi_rs[h] = erv
            rv = p.vop(lambda E: E.reciprocal(rs_rec[0:1, 0:TC], rs_row[0:1, 0:TC]))
            p.wait("tensor", p.vs, rv)
            if h >= 1 or qc > 0:
                p.wait("tensor", p.vs, prev_h_done)
            tv = p.op("tensor", lambda E: E.matmul(
                ps_bc[:, :], ones1f[:, :], rs_rec[0:1, 0:TC], start=True, stop=True),
                inc=p.ts)
            p.wait("vector", p.ts, tv)
            p.vop(lambda E: E.tensor_copy(rcb[:, :], ps_bc[:, :]))
            hv = p.op("vector", lambda E, h=h, ps_avh=ps_avh: E.tensor_mul(
                AO[:, h * TC:(h + 1) * TC], ps_avh, rcb[:, :]), inc=p.vs)
            epi_av[h] = hv
            prev_h_done = hv
        for do in range(DT):
            sb = sb3a if do % 2 == 0 else sb3b
            gkey = (qc, do - 2) if do >= 2 else ((qc - 1, DT - 2 + do) if qc > 0 else None)
            if gkey is not None:
                p.wait("gpsimd", p.ts, prev_ao_st.get(gkey, 0))
            lv = p.dma("gpsimd", sb[:, :], aoa[do * 128:(do + 1) * 128, :])
            p.wait("tensor", p.qg, lv)
            p.wait("tensor", p.vs, prev_h_done)
            if do > 0 or qc > 0:
                p.wait("tensor", p.vs, prev_proj_v)
            tv = None
            for hh in range(DT):
                tv = p.op("tensor", lambda E, hh=hh, sb=sb: E.matmul(
                    ps_sc[:, 0:TC], sb[:, hh * 128:(hh + 1) * 128],
                    AO[:, hh * TC:(hh + 1) * TC],
                    start=(hh == 0), stop=(hh == DT - 1)),
                    inc=p.ts if hh == DT - 1 else None)
            prev_ao_st[(qc, do)] = tv
            p.wait("vector", p.ts, tv)
            p.wait("vector", p.qs, x2l)
            p.op("vector", lambda E, do=do: E.tensor_scalar(
                tmp1[:, :], ps_sc[:, 0:TC], bo_c[:, do:do + 1], None, ALU.add))
            xv = p.op("vector", lambda E, do=do: E.tensor_add(
                acc[:, do * TC:(do + 1) * TC], acc[:, do * TC:(do + 1) * TC],
                tmp1[:, :]), inc=p.vs)
            prev_proj_v = xv
        x3_done = prev_proj_v
        prev_exp = p.ss.n

        x_tiles = [acc[:, dt * TC:(dt + 1) * TC] for dt in range(DT)]
        p.wait("vector", p.vs, x3_done)
        ev = emit_stats_chunk(p, b, x_tiles, 0, rb["S"], rb["SS"], p.qs.n)
        rm = emit_rowmath_composite(p, rb, with_inner=False, n=TC, off=0)
        p.wait("tensor", p.vs, rm)
        bt = p.op("tensor", lambda E: (
            E.matmul(ps_bc[:, :], ones1f[:, :], rb["A"][0:1, :], start=True, stop=True),
            E.matmul(ps_av[:, 0:TC], ones1f[:, :], rb["B"][0:1, :], start=True, stop=True))[-1],
            inc=p.ts)
        p.wait("vector", p.ts, bt)
        for dt in range(DT):
            p.op("vector", lambda E, dt=dt: E.tensor_mul(
                tmp1[:, :], acc[:, dt * TC:(dt + 1) * TC], ps_bc[:, :]))
            vv = p.op("vector", lambda E, dt=dt: E.tensor_sub(
                xnb4[:, dt * TC:(dt + 1) * TC], tmp1[:, :], ps_av[:, 0:TC]), inc=p.vs)
        xn_ready = vv
        w3h = [uni[:, 2 * DT * TC:2 * DT * TC + 4 * D],
               uni[:, 2 * DT * TC + 4 * D:2 * DT * TC + 8 * D]]
        p.wait("sync", p.ts, last_scores[(qc, DT - 1)])
        p.wait("sync", p.vs, prev_h_done)
        p.wait("sync", p.ss, prev_exp)
        ld_done = {}
        for dt in range(DT):
            p.dma("sync", w1s[:, dt * TC:(dt + 1) * TC],
                  w1qh[dt * 128:(dt + 1) * 128, 0:TC])
        for dt in range(DT):
            wl = p.dma("sync", w2s[:, dt * TC:(dt + 1) * TC],
                       w2qh[dt * 128:(dt + 1) * 128, 0:TC])
        for j in range(4):
            w3l = p.dma("sync", w3h[0][:, j * D:(j + 1) * D],
                        w3qh[j * 128:j * 128 + 128, :])
        ld_done[0] = max(wl, w3l)
        w3_use = {}
        for hc in range(NHC):
            w3cur = w3h[hc % 2]
            p.wait("tensor", p.qs, ld_done[hc])
            p.wait("tensor", p.vs, xn_ready)
            hid_done = {}
            for ht in range(4):
                if qc > 0 or hc > 0 or ht > 0:
                    p.wait("tensor", p.ss, prev_sg)
                    p.wait("tensor", p.vs, prev_hid)
                t1 = t2 = None
                for dt in range(DT):
                    t1 = p.op("tensor", lambda E, dt=dt, ht=ht: E.matmul(
                        ps_sc[:, 0:TC],
                        w1s[:, dt * TC + ht * 128: dt * TC + (ht + 1) * 128],
                        xnb4[:, dt * TC:(dt + 1) * TC],
                        start=(dt == 0), stop=(dt == DT - 1)),
                        inc=p.ts if dt == DT - 1 else None)
                for dt in range(DT):
                    t2 = p.op("tensor", lambda E, dt=dt, ht=ht: E.matmul(
                        ps_sc[:, TC:2 * TC],
                        w2s[:, dt * TC + ht * 128: dt * TC + (ht + 1) * 128],
                        xnb4[:, dt * TC:(dt + 1) * TC],
                        start=(dt == 0), stop=(dt == DT - 1)),
                        inc=p.ts if dt == DT - 1 else None)
                p.wait("scalar", p.ts, t1)
                sgv = p.op("scalar", lambda E: E.activation(
                    sg16[:, :], ps_sc[:, 0:TC], AF.Sigmoid), inc=p.ss)
                prev_sg = sgv
                p.wait("vector", p.ss, sgv)
                p.wait("vector", p.ts, t2)
                hv2 = p.op("vector", lambda E, ht=ht: E.tensor_mul(
                    hid_sb[:, ht * TC:(ht + 1) * TC], sg16[:, :],
                    ps_sc[:, TC:2 * TC]), inc=p.vs)
                prev_hid = hv2
                hid_done[ht] = hv2
            if hc + 1 < NHC:
                p.wait("sync", p.ts, t2)
                for dt in range(DT):
                    p.dma("sync", w1s[:, dt * TC:(dt + 1) * TC],
                          w1qh[dt * 128:(dt + 1) * 128, (hc + 1) * TC:(hc + 2) * TC])
                for dt in range(DT):
                    wl = p.dma("sync", w2s[:, dt * TC:(dt + 1) * TC],
                               w2qh[dt * 128:(dt + 1) * 128,
                                    (hc + 1) * TC:(hc + 2) * TC])
                if hc >= 1:
                    p.wait("sync", p.ts, w3_use[hc - 1])
                for j in range(4):
                    w3l = p.dma("sync", w3h[(hc + 1) % 2][:, j * D:(j + 1) * D],
                                w3qh[((hc + 1) * 4 + j) * 128:
                                     ((hc + 1) * 4 + j) * 128 + 128, :])
                ld_done[hc + 1] = max(wl, w3l)
            accv = {}
            for do in range(DT):
                ob = 2 + (do % 2)
                p.wait("tensor", p.vs, hid_done[3])
                if do >= 2:
                    p.wait("tensor", p.vs, accv[do - 2])
                elif qc > 0 or hc > 0:
                    p.wait("tensor", p.vs, prev_acc_v2)
                t3 = None
                for j in range(4):
                    t3 = p.op("tensor", lambda E, j=j, do=do, ob=ob, w3cur=w3cur: E.matmul(
                        ps_sc[:, ob * TC:(ob + 1) * TC],
                        w3cur[:, j * D + do * 128: j * D + (do + 1) * 128],
                        hid_sb[:, j * TC:(j + 1) * TC],
                        start=(j == 0), stop=(j == 3)),
                        inc=p.ts if j == 3 else None)
                p.wait("vector", p.ts, t3)
                av2 = p.op("vector", lambda E, do=do, ob=ob: E.tensor_add(
                    acc[:, do * TC:(do + 1) * TC], acc[:, do * TC:(do + 1) * TC],
                    ps_sc[:, ob * TC:(ob + 1) * TC]), inc=p.vs)
                accv[do] = av2
                prev_acc_v = av2
            prev_acc_v2 = accv[DT - 1] if DT >= 2 else prev_acc_v
            prev_acc_v2 = max(accv[DT - 1], accv[DT - 2])
            w3_use[hc] = t3
            prev_w3_use = t3
        p.wait("sync", p.vs, prev_acc_v)
        for dt in range(DT):
            last_acc_store = p.dma("sync", out_t[dt * 128:(dt + 1) * 128,
                                                 qc * TC:(qc + 1) * TC],
                                   acc[:, dt * TC:(dt + 1) * TC])
        prev_ffn_vs = prev_acc_v
        prev_exp = p.ss.n

    p.wait("sync", p.qs, p.qs.n)
    p.wait("gpsimd", p.qg, p.qg.n)

    with ctx, nc.Block() as block:
        p.finalize(block)
    return nc



def emit_row_rstd(p, tmp_rows, var_row, out_row, n=TOK, eps=EPS, off=0):
    t = tmp_rows
    s = slice(off, off + n)
    vv = p.vop(lambda E: E.tensor_scalar_add(t[0:1, s], var_row[0:1, s], float(eps)))
    p.wait("scalar", p.vs, vv)
    p.sop(lambda E: E.activation(t[0:1, s], t[0:1, s], AF.Ln))
    return p.sop(lambda E: E.activation(out_row[0:1, s], t[0:1, s],
                                        AF.Exp, scale=-0.5))


def emit_stats_chunk(p, bufs, x_tiles, ch, S_row, SS_row, x_ready_val):
    x16a, x16b, sq16, ones16 = bufs["x16a"], bufs["x16b"], bufs["sq16a"], bufs["ones16"]
    sq16b = bufs["sq16b"]
    ps_S, ps_SS = bufs["ps_S"], bufs["ps_SS"]
    p.wait("vector", p.qs, x_ready_val)
    tens_done = {}
    for dt in range(DT):
        x16 = x16a if dt % 2 == 0 else x16b
        sq = sq16 if dt % 2 == 0 else sq16b
        if dt >= 2:
            p.wait("vector", p.ts, tens_done[dt - 2])
        vv = p.op("vector", lambda E, x16=x16, dt=dt: E.tensor_copy(
            x16[:, :], x_tiles[dt]), inc=p.vs)
        p.wait("scalar", p.vs, vv)
        sv = p.op("scalar", lambda E, x16=x16, sq=sq: E.activation(
            sq[:, :], x16[:, :], AF.Square), inc=p.ss)
        p.wait("tensor", p.ss, sv)
        tv = p.op("tensor", lambda E, x16=x16, sq=sq, dt=dt: (
            E.matmul(ps_S[0:1, 0:TC], ones16[:, :], x16[:, :],
                     start=(dt == 0), stop=(dt == DT - 1), skip_group_check=True),
            E.matmul(ps_SS[0:1, 0:TC], ones16[:, :], sq[:, :],
                     start=(dt == 0), stop=(dt == DT - 1), skip_group_check=True))[-1],
            inc=p.ts)
        tens_done[dt] = tv
    p.wait("vector", p.ts, tens_done[DT - 1])
    p.op("vector", lambda E: E.tensor_copy(S_row[0:1, ch * TC:(ch + 1) * TC],
                                           ps_S[0:1, 0:TC]))
    return p.op("vector", lambda E: E.tensor_copy(SS_row[0:1, ch * TC:(ch + 1) * TC],
                                                  ps_SS[0:1, 0:TC]), inc=p.vs)


def emit_rowmath_composite(p, rb, with_inner, n=TOK, off=0):
    S, SS, A, B, r, mur, t0, t1 = (rb["S"], rb["SS"], rb["A"], rb["B"],
                                   rb["r"], rb["mur"], rb["t0"], rb["t1"])
    sl = lambda x: x[0:1, off:off + n]
    p.vop(lambda E: E.tensor_scalar_mul(sl(t0), sl(S), INVD))
    p.vop(lambda E: E.tensor_scalar_mul(sl(t1), sl(SS), INVD))
    p.vop(lambda E: E.tensor_mul(sl(B), sl(t0), sl(t0)))
    vv = p.vop(lambda E: E.tensor_sub(sl(t1), sl(t1), sl(B)))
    p.wait("scalar", p.vs, vv)
    sv = emit_row_rstd(p, rb["t2"], t1, r, n=n, off=off)
    p.wait("vector", p.ss, sv)
    p.vop(lambda E: E.tensor_mul(sl(mur), sl(t0), sl(r)))
    if not with_inner:
        p.vop(lambda E: E.tensor_copy(sl(A), sl(r)))
        return p.vop(lambda E: E.tensor_copy(sl(B), sl(mur)))
    p.vop(lambda E: E.tensor_mul(sl(t0), sl(r), sl(r)))
    vv = p.vop(lambda E: E.tensor_mul(sl(t0), sl(t1), sl(t0)))
    p.wait("scalar", p.vs, vv)
    sv = emit_row_rstd(p, rb["t2"], t0, t1, n=n, off=off)
    p.wait("vector", p.ss, sv)
    p.vop(lambda E: E.tensor_mul(sl(A), sl(r), sl(t1)))
    p.vop(lambda E: E.tensor_scalar_mul(sl(t0), sl(S), INVD))
    return p.vop(lambda E: E.tensor_mul(sl(B), sl(t0), sl(A)))



def build_launch1():
    nc = bass.Bass("TRN2", target_bir_lowering=False)
    x_t = nc.dram_tensor("x_t", [D, TOK], F32, kind="ExternalInput").ap()
    lng = nc.dram_tensor("lng_cols", [128, DT], F32, kind="ExternalInput").ap()
    lnb = nc.dram_tensor("lnb_cols", [128, DT], F32, kind="ExternalInput").ap()
    win = nc.dram_tensor("win_arr", [128, DT * R], F32, kind="ExternalInput").ap()
    binc = nc.dram_tensor("b_in_col", [R, 1], F32, kind="ExternalInput").ap()
    u_hand = nc.dram_tensor("u_hand", [R, TOK], F32, kind="ExternalOutput").ap()
    rows1 = nc.dram_tensor("rows1", [4, TOK], F32, kind="ExternalOutput").ap()

    ctx = ExitStack()
    p = Prog(nc, ctx)
    b = {}
    b["xch_a"] = sbuf(ctx, nc, "xch_a", [128, DT * TC], F32)
    b["xch_b"] = sbuf(ctx, nc, "xch_b", [128, DT * TC], F32)
    b["x16a"] = sbuf(ctx, nc, "x16a", [128, TC], BF16)
    b["x16b"] = sbuf(ctx, nc, "x16b", [128, TC], BF16)
    b["sq16a"] = sbuf(ctx, nc, "sq16a", [128, TC], BF16)
    b["sq16b"] = sbuf(ctx, nc, "sq16b", [128, TC], BF16)
    b["ones16"] = sbuf(ctx, nc, "ones16", [128, 1], BF16)
    ones1f = sbuf(ctx, nc, "ones1f", [1, 128], F32)
    lng_c = sbuf(ctx, nc, "lng_c", [128, DT], F32)
    lnb_c = sbuf(ctx, nc, "lnb_c", [128, DT], F32)
    win16 = sbuf(ctx, nc, "win16", [128, DT * R], BF16)
    bin_c = sbuf(ctx, nc, "bin_c", [R, 1], F32)
    rb = {k: sbuf(ctx, nc, "row_" + k, [1, TOK], F32)
          for k in ("S", "SS", "A", "B", "r", "mur", "t0", "t1", "t2")}
    tmp1 = sbuf(ctx, nc, "tmp1", [128, TC], F32)
    xnp = sbuf(ctx, nc, "xnp", [128, TC], F32)
    xn16 = sbuf(ctx, nc, "xn16", [128, TC], BF16)
    u_sb = sbuf(ctx, nc, "u_sb", [R, TOK], F32)
    b["ps_S"] = psum(ctx, nc, "ps_S", [1, TC])
    b["ps_SS"] = psum(ctx, nc, "ps_SS", [1, TC])
    ps_b1 = psum(ctx, nc, "ps_b1", [128, TC])
    ps_b2 = psum(ctx, nc, "ps_b2", [128, TC])
    ps_u = psum(ctx, nc, "ps_u", [R, TC])

    p.dma("sync", lng_c[:, :], lng)
    p.dma("sync", lnb_c[:, :], lnb)
    p.dma("gpsimd", win16[:, :], win)
    cv = p.dma("sync", bin_c[:, :], binc)
    gv = p.qg.n
    iv = p.op("vector", lambda E: E.memset(b["ones16"][:, :], 1.0))
    iv = p.op("vector", lambda E: E.memset(ones1f[:, :], 1.0), inc=p.vs)

    last_ev = None
    for ch in range(NCH):
        xch = b["xch_a"] if ch % 2 == 0 else b["xch_b"]
        if ch >= 2 and last_ev is not None:
            pass
        if ch >= 2:
            p.wait("sync", p.vs, stats_done[ch - 2])
        for dt in range(DT):
            p.dma("sync", xch[:, dt * TC:(dt + 1) * TC],
                  x_t[dt * 128:(dt + 1) * 128, ch * TC:(ch + 1) * TC])
        x_tiles = [xch[:, dt * TC:(dt + 1) * TC] for dt in range(DT)]
        ev = emit_stats_chunk(p, b, x_tiles, ch, rb["S"], rb["SS"], p.qs.n)
        if ch == 0:
            stats_done = {}
        stats_done[ch] = ev
    rm = emit_rowmath_composite(p, rb, with_inner=True)

    p.wait("tensor", p.qg, gv)
    p.wait("scalar", p.qs, cv)
    for ch in range(NCH):
        xch = b["xch_a"] if ch % 2 == 0 else b["xch_b"]
        if ch < 2:
            p.wait("sync", p.vs, rm)
        else:
            p.wait("sync", p.vs, xn_done[ch - 2])
        for dt in range(DT):
            p.dma("sync", xch[:, dt * TC:(dt + 1) * TC],
                  x_t[dt * 128:(dt + 1) * 128, ch * TC:(ch + 1) * TC])
        ld = p.qs.n
        p.wait("tensor", p.vs, rm)
        if ch >= 1:
            p.wait("tensor", p.vs, bc_used[ch - 1])
        tv = p.op("tensor", lambda E, ch=ch: (
            E.matmul(ps_b1[:, :], ones1f[:, :], rb["A"][0:1, ch * TC:(ch + 1) * TC],
                     start=True, stop=True),
            E.matmul(ps_b2[:, :], ones1f[:, :], rb["B"][0:1, ch * TC:(ch + 1) * TC],
                     start=True, stop=True))[-1], inc=p.ts)
        p.wait("vector", p.ts, tv)
        p.wait("vector", p.qs, ld)
        for dt in range(DT):
            vv = p.op("vector", lambda E, dt=dt, xch=xch: E.tensor_mul(
                tmp1[:, :], xch[:, dt * TC:(dt + 1) * TC], ps_b1[:, :]))
            vv = p.op("vector", lambda E: E.tensor_sub(xnp[:, :], tmp1[:, :],
                                                       ps_b2[:, :]), inc=p.vs)
            p.wait("scalar", p.vs, vv)
            sv = p.op("scalar", lambda E, dt=dt: E.activation(
                xn16[:, :], xnp[:, :], AF.Identity,
                bias=lnb_c[:, dt:dt + 1], scale=lng_c[:, dt:dt + 1]), inc=p.ss)
            p.wait("tensor", p.ss, sv)
            tv2 = p.op("tensor", lambda E, dt=dt: E.matmul(
                ps_u[:, :], win16[:, dt * R:(dt + 1) * R], xn16[:, :],
                start=(dt == 0), stop=(dt == DT - 1)), inc=p.ts)
            p.wait("vector", p.ts, tv2)
        uv = p.op("vector", lambda E, ch=ch: E.tensor_scalar(
            u_sb[:, ch * TC:(ch + 1) * TC], ps_u[:, :], bin_c[:, :], None,
            ALU.add), inc=p.vs)
        if ch == 0:
            xn_done, bc_used = {}, {}
        xn_done[ch] = uv
        bc_used[ch] = uv
    p.wait("sync", p.vs, xn_done[NCH - 1])
    p.dma("sync", u_hand, u_sb[:, :])
    p.dma("sync", rows1[0:1, :], rb["A"][0:1, :])
    p.dma("sync", rows1[1:2, :], rb["B"][0:1, :])
    p.dma("sync", rows1[2:3, :], rb["r"][0:1, :])
    fin = p.dma("sync", rows1[3:4, :], rb["mur"][0:1, :])
    p.wait("sync", p.qs, fin)

    with ctx, nc.Block() as block:
        p.finalize(block)
    return nc



def build_launch2(upto=5):
    nc = bass.Bass("TRN2", target_bir_lowering=False)
    ap = lambda n, s, d=F32, k="ExternalInput": nc.dram_tensor(n, s, d, kind=k).ap()
    x_t = ap("x_t", [D, TOK])
    rows1 = ap("rows1", [4, TOK])
    u_pad = ap("u_pad", [R, SEQ])
    dec_col = ap("dec_col", [R, 1])
    woutb = ap("woutb", [R + 1, D])
    d_cols = ap("d_cols", [128, DT])
    lng = ap("lng_cols", [128, DT])
    lnb = ap("lnb_cols", [128, DT])
    wdn = ap("wdn_arr", [128, DT * 32])
    bdn = ap("bdn_col", [32, 1])
    wupb = ap("wupb", [33, D])
    l0w = ap("l0w_arr", [D, D])
    g0 = ap("g0_cols", [128, DT])
    be0 = ap("be0_cols", [128, DT])
    lnw = ap("lnw_cols", [128, DT])
    lnbb = ap("lnbb_cols", [128, DT])
    w1a = ap("w1_arr", [D, HID])
    w2a = ap("w2_arr", [D, HID])
    w3a = ap("w3_arr", [HID, D])
    aqa = ap("aq_arr", [D, D])
    aka = ap("ak_arr", [128, D])
    ava = ap("av_arr", [128, D])
    bq = ap("bq_cols", [128, DT])
    bk = ap("bk_col", [128, 1])
    bv = ap("bv_col", [128, 1])
    idt = ap("idt32", [128, 128])

    x2h = ap("x2_hand", [D, TOK], F32, "ExternalOutput")
    qh = ap("q_hand", [D, TOK], BF16, "ExternalOutput")
    kh = ap("k_hand", [128, TOK], BF16, "ExternalOutput")
    vh = ap("v_hand", [128, TOK], BF16, "ExternalOutput")
    w1qo = ap("w1q", [D, HID], BF16, "ExternalOutput")
    w2qo = ap("w2q", [D, HID], BF16, "ExternalOutput")
    w3qo = ap("w3q", [HID, D], BF16, "ExternalOutput")
    x1s = ap("x1_scr", [D, TOK], F32, "Internal")
    sgs = ap("sgn_scr", [D, D], BF16, "Internal")

    ctx = ExitStack()
    p = Prog(nc, ctx)

    b = {}
    b["x16a"] = sbuf(ctx, nc, "x16a", [128, TC], BF16)
    b["x16b"] = sbuf(ctx, nc, "x16b", [128, TC], BF16)
    b["sq16a"] = sbuf(ctx, nc, "sq16a", [128, TC], BF16)
    b["sq16b"] = sbuf(ctx, nc, "sq16b", [128, TC], BF16)
    b["ones16"] = sbuf(ctx, nc, "ones16", [128, 1], BF16)
    ones1f = sbuf(ctx, nc, "ones1f", [1, 128], F32)
    rb = {k: sbuf(ctx, nc, "row_" + k, [1, TC], F32)
          for k in ("S", "SS", "A", "B", "r", "mur", "t0", "t1", "t2")}
    r1rows = sbuf(ctx, nc, "r1rows", [1, 4 * TC], F32)
    tmp1 = sbuf(ctx, nc, "tmp1", [128, TC], F32)
    tmp2 = sbuf(ctx, nc, "tmp2", [128, TC], F32)
    tmp3 = sbuf(ctx, nc, "tmp3", [128, TC], F32)
    cols = {}
    for nm, src in (("d", d_cols), ("lng", lng), ("lnb", lnb), ("g0", g0),
                    ("be0", be0), ("lnw", lnw), ("lnbb", lnbb), ("bq", bq)):
        cols[nm] = sbuf(ctx, nc, "c_" + nm, [128, DT], F32)
        p.dma("sync", cols[nm][:, :], {"d": d_cols, "lng": lng, "lnb": lnb,
                                       "g0": g0, "be0": be0, "lnw": lnw,
                                       "lnbb": lnbb, "bq": bq}[nm])
    bk_c = sbuf(ctx, nc, "bk_c", [128, 1], F32)
    bv_c = sbuf(ctx, nc, "bv_c", [128, 1], F32)
    bdn_c = sbuf(ctx, nc, "bdn_c", [32, 1], F32)
    p.dma("sync", bk_c[:, :], bk)
    p.dma("sync", bv_c[:, :], bv)
    p.dma("sync", bdn_c[:, :], bdn)
    woutb16 = sbuf(ctx, nc, "woutb16", [R + 1, D], BF16)
    wdn16 = sbuf(ctx, nc, "wdn16", [128, DT * 32], BF16)
    wupb16 = sbuf(ctx, nc, "wupb16", [33, D], BF16)
    idt32 = sbuf(ctx, nc, "idt32_sb", [128, 128], F32)
    idt16 = sbuf(ctx, nc, "idt16_sb", [128, 128], BF16)
    ak16 = sbuf(ctx, nc, "ak16", [128, D], BF16)
    av16 = sbuf(ctx, nc, "av16", [128, D], BF16)
    p.dma("gpsimd", woutb16[:, :], woutb)
    p.dma("gpsimd", wdn16[:, :], wdn)
    p.dma("gpsimd", wupb16[:, :], wupb)
    p.dma("sync", idt32[:, :], idt)
    p.dma("gpsimd", idt16[:, :], idt)
    p.dma("gpsimd", ak16[:, :], aka)
    cg0 = p.dma("gpsimd", av16[:, :], ava)
    h_ext16 = sbuf(ctx, nc, "h_ext16", [R + 1, TOK], BF16)
    hf_ext16 = sbuf(ctx, nc, "hf_ext16", [33, TC], BF16)
    srow = sbuf(ctx, nc, "srow", [1, 16], F32)
    s0col = sbuf(ctx, nc, "s0col", [128, 1], F32)
    sg0c = sbuf(ctx, nc, "sg0c", [128, DT], F32)
    thp_c = sbuf(ctx, nc, "thp_c", [128, 1], F32)
    thn_c = sbuf(ctx, nc, "thn_c", [128, 1], F32)
    mxacc = sbuf(ctx, nc, "mxacc", [128, 1], F32)
    mxt = sbuf(ctx, nc, "mxt", [128, 1], F32)
    gacc = sbuf(ctx, nc, "gacc", [128, 1], F32)
    c0 = p.qs.n

    b["ps_S"] = psum(ctx, nc, "ps_S", [1, TC])
    b["ps_SS"] = psum(ctx, nc, "ps_SS", [1, TC])
    ps_b1 = psum(ctx, nc, "ps_b1", [128, TC])
    ps_b2 = psum(ctx, nc, "ps_b2", [128, TC])
    ps_b3 = psum(ctx, nc, "ps_b3", [128, TC])
    ps_b4 = psum(ctx, nc, "ps_b4", [128, TC])
    ps_ar = psum(ctx, nc, "ps_ar", [128, TC])
    ps_hf = psum(ctx, nc, "ps_hf", [128, TC])

    p.op("vector", lambda E: E.memset(b["ones16"][:, :], 1.0))
    p.op("vector", lambda E: E.memset(ones1f[:, :], 1.0))
    p.op("vector", lambda E: E.memset(h_ext16[:, :], 1.0))
    mz = p.op("vector", lambda E: E.memset(hf_ext16[:, :], 1.0), inc=p.vs)

    sctx = ExitStack()
    u_sb = sbuf(sctx, nc, "u_sb", [R, SEQ], F32)
    a_b = sbuf(sctx, nc, "a_b", [R, SEQ], F32)
    h_pad = sbuf(sctx, nc, "h_pad", [R, SEQ], F32)
    dcol = sbuf(sctx, nc, "dcol", [R, 1], F32)
    p.dma("sync", dcol[:, :], dec_col)
    lu = p.dma("sync", u_sb[:, :], u_pad)
    p.wait("scalar", p.qs, lu)
    sv = p.op("scalar", lambda E: E.activation(a_b[:, 0:1], dcol[:, :], AF.Exp,
                                               scale=-1.0), inc=p.ss)
    p.wait("vector", p.ss, sv)
    p.vop(lambda E: E.tensor_scalar_add(a_b[:, 0:1], a_b[:, 0:1], 1.0))
    p.vop(lambda E: E.reciprocal(dcol[:, :], a_b[:, 0:1]))
    p.vop(lambda E: E.memset(a_b[:, :], 1.0))
    p.vop(lambda E: E.tensor_scalar_mul(a_b[:, :], a_b[:, :], dcol[:, :]))
    p.wait("vector", p.qs, lu)
    p.op("vector", lambda E: E.tensor_tensor_scan(
        h_pad[:, :], a_b[:, :], u_sb[:, :], 0.0, ALU.mult, ALU.add))
    scan_done = p.op("vector", lambda E: E.tensor_copy(
        h_ext16[0:R, :], h_pad[0:R, SEQ - TOK:SEQ]), inc=p.vs)
    sctx.close()

    pctx = ExitStack()
    wst_a = sbuf(pctx, nc, "wst_a", [128, D], F32)
    wst_b = sbuf(pctx, nc, "wst_b", [128, D], F32)
    sg16 = sbuf(pctx, nc, "sg16", [128, D], BF16)
    p.op("vector", lambda E: E.memset(mxacc[:, :], 0.0))
    sg_writes = {}
    for do in range(DT):
        wst = wst_a if do % 2 == 0 else wst_b
        if do < 2:
            p.wait("sync", p.vs, scan_done)
        else:
            p.wait("sync", p.vs, mx_done[do - 2])
        ldv = p.dma("sync", wst[:, :], l0w[do * 128:(do + 1) * 128, :])
        p.wait("scalar", p.qs, ldv)
        if do >= 1:
            p.wait("scalar", p.qg, sg_writes[do - 1])
        sv = p.op("scalar", lambda E, wst=wst: E.sign(sg16[:, :], wst[:, :]), inc=p.ss)
        p.wait("gpsimd", p.ss, sv)
        sg_writes[do] = p.dma("gpsimd", sgs[do * 128:(do + 1) * 128, :], sg16[:, :])
        p.wait("vector", p.ss, sv)
        p.vop(lambda E, wst=wst: E.tensor_reduce(
            mxt[:, :], wst[:, :], AX.X, ALU.max, apply_absolute_value=True))
        vv = p.vop(lambda E: E.tensor_max(mxacc[:, :], mxacc[:, :], mxt[:, :]))
        if do == 0:
            mx_done = {}
        mx_done[do] = vv
    p.wait("tensor", p.vs, mx_done[DT - 1])
    p.wait("tensor", p.qs, c0)
    tv = p.op("tensor", lambda E: E.matmul(b["ps_S"][0:1, 0:128], mxacc[:, :],
                                           idt32[:, :], start=True, stop=True),
              inc=p.ts)
    p.wait("vector", p.ts, tv)
    vv = p.vop(lambda E: E.tensor_reduce(
        srow[0:1, 0:1], b["ps_S"][0:1, 0:128], AX.X, ALU.max))
    p.wait("tensor", p.vs, vv)
    tv = p.op("tensor", lambda E: E.matmul(ps_b1[:, 0:1], ones1f[:, :],
                                           srow[0:1, 0:1], start=True, stop=True),
              inc=p.ts)
    p.wait("vector", p.ts, tv)
    p.vop(lambda E: E.tensor_copy(s0col[:, :], ps_b1[:, 0:1]))
    prep_done = p.vop(lambda E: E.tensor_scalar_mul(
        sg0c[:, :], cols["g0"][:, :], s0col[:, :]))
    pctx.close()

    p.wait("sync", p.vs, prep_done)
    if upto <= 1:
        p.wait("sync", p.qs, p.qs.n)
        p.wait("gpsimd", p.qg, p.qg.n)
        with ctx, nc.Block() as block:
            p.finalize(block)
        return nc

    mctx = ExitStack()
    xch_a = sbuf(mctx, nc, "xch_a", [128, DT * TC], F32)
    xch_b = sbuf(mctx, nc, "xch_b", [128, DT * TC], F32)
    xmid = sbuf(mctx, nc, "xmid", [128, DT * TC], F32)
    xnb = sbuf(mctx, nc, "xnb", [128, DT * TC], BF16)
    st_a = sbuf(mctx, nc, "st_a", [128, D], F32)
    st_b = sbuf(mctx, nc, "st_b", [128, D], F32)
    sb16a = sbuf(mctx, nc, "sb16a", [128, D], BF16)
    sb16b = sbuf(mctx, nc, "sb16b", [128, D], BF16)
    q16o = sbuf(mctx, nc, "q16o", [128, TC], BF16)
    q16ob = sbuf(mctx, nc, "q16ob", [128, TC], BF16)
    q16op = [q16o, q16ob]
    vtmp = sbuf(mctx, nc, "vtmp", [128, TOK], BF16)

    def load_xchunk(ch, src, guard=None):
        xch = xch_a if ch % 2 == 0 else xch_b
        if guard is not None:
            p.wait("sync", *guard)
        for dt in range(DT):
            v = p.dma("sync", xch[:, dt * TC:(dt + 1) * TC],
                      src[dt * 128:(dt + 1) * 128, ch * TC:(ch + 1) * TC])
        return xch, v

    def store_xchunk(ch, xch, dst, ready_ctr, ready_val):
        p.wait("sync", ready_ctr, ready_val)
        for dt in range(DT):
            v = p.dma("sync", dst[dt * 128:(dt + 1) * 128, ch * TC:(ch + 1) * TC],
                      xch[:, dt * TC:(dt + 1) * TC])
        return v

    def bcasts(ch_rows, n=TC):
        A_ap, B_ap = ch_rows
        return p.op("tensor", lambda E: (
            E.matmul(ps_b1[:, 0:n], ones1f[:, :], A_ap, start=True, stop=True),
            E.matmul(ps_b2[:, 0:n], ones1f[:, :], B_ap, start=True, stop=True))[-1],
            inc=p.ts)

    sign_last_ss = p.ss.n
    sign_last_qg = p.qg.n
    p.wait("tensor", p.vs, scan_done)
    p.wait("tensor", p.vs, prep_done)
    p.wait("tensor", p.qg, cg0)
    ps_work = ps_ar
    ps_workb = ps_hf
    work = [ps_work, ps_workb]
    work_free = [0, 0]
    sg_used = {}
    q_used = {}
    prev_bcast_tv = 0
    prev_chunk_done = None
    x16p = [b["x16a"], b["x16b"]]
    sq16p = [b["sq16a"], b["sq16b"]]
    st_free = [0, 0]
    if upto <= 1:
        p.wait("sync", p.qs, p.qs.n)
        p.wait("gpsimd", p.qg, p.qg.n)
        mctx_dummy = None
        with ctx, nc.Block() as block:
            p.finalize(block)
        return nc
    for ch in range(NCH):
        if ch == 0:
            p.wait("sync", p.vs, prep_done)
            p.wait("sync", p.ss, sign_last_ss)
            p.wait("sync", p.qg, sign_last_qg)
        xch, xld = load_xchunk(ch, x_t,
                               guard=(p.vs, prev_chunk_done) if prev_chunk_done else None)
        if ch > 0:
            p.wait("sync", p.ts, prev_bcast_tv)
        for i in range(4):
            rl = p.dma("sync", r1rows[0:1, i * TC:(i + 1) * TC],
                       rows1[i:i + 1, ch * TC:(ch + 1) * TC])
        p.wait("tensor", p.qs, rl)
        if ch > 0:
            p.wait("tensor", p.vs, prev_chunk_done)
        tv = p.op("tensor", lambda E: (
            E.matmul(ps_b1[:, :], ones1f[:, :], r1rows[0:1, 0:TC], start=True, stop=True),
            E.matmul(ps_b2[:, :], ones1f[:, :], r1rows[0:1, TC:2 * TC], start=True, stop=True),
            E.matmul(ps_b3[:, :], ones1f[:, :], r1rows[0:1, 2 * TC:3 * TC], start=True, stop=True))[-1],
            inc=p.ts)
        prev_bcast_tv = tv
        p.wait("vector", p.qs, rl)
        nm = p.vop(lambda E: E.tensor_scalar_mul(
            r1rows[0:1, 3 * TC:4 * TC], r1rows[0:1, 3 * TC:4 * TC], -1.0))
        p.wait("tensor", p.vs, nm)
        p.wait("vector", p.ts, tv)
        p.wait("vector", p.qs, xld)
        p.wait("scalar", p.qs, xld)
        stv = {}
        for dt in range(DT):
            par = dt % 2
            xt = xch[:, dt * TC:(dt + 1) * TC]
            p.wait("tensor", p.vs, work_free[par])
            tv2 = p.op("tensor", lambda E, dt=dt, ch=ch, par=par: (
                E.matmul(work[par][:, 0:TC], woutb16[:, dt * 128:(dt + 1) * 128],
                         h_ext16[:, ch * TC:(ch + 1) * TC], start=True, stop=False,
                         skip_group_check=True),
                E.matmul(work[par][:, 0:TC], ones1f[:, :],
                         r1rows[0:1, 3 * TC:4 * TC], start=False, stop=True,
                         skip_group_check=True))[-1], inc=p.ts)
            p.op("vector", lambda E, xt=xt: E.tensor_mul(tmp1[:, :], xt, ps_b1[:, :]))
            vv = p.op("vector", lambda E: E.tensor_sub(tmp2[:, :], tmp1[:, :],
                                                       ps_b2[:, :]), inc=p.vs)
            p.wait("scalar", p.vs, vv)
            sv = p.op("scalar", lambda E, dt=dt: E.activation(
                tmp3[:, :], tmp2[:, :], AF.Identity,
                bias=cols["lnb"][:, dt:dt + 1], scale=cols["lng"][:, dt:dt + 1]),
                inc=p.ss)
            p.op("vector", lambda E, xt=xt: E.tensor_mul(tmp1[:, :], xt, ps_b3[:, :]))
            p.wait("vector", p.ts, tv2)
            p.wait("vector", p.ss, sv)
            wv = p.op("vector", lambda E, dt=dt, par=par: E.scalar_tensor_tensor(
                tmp2[:, :], tmp3[:, :], cols["d"][:, dt:dt + 1], work[par][:, 0:TC],
                ALU.mult, ALU.add), inc=p.vs)
            work_free[par] = wv
            xmv = p.op("vector", lambda E, dt=dt: E.tensor_add(
                xmid[:, dt * TC:(dt + 1) * TC], tmp1[:, :], tmp2[:, :]), inc=p.vs)
            p.wait("scalar", p.vs, xmv)
            p.wait("scalar", p.ts, st_free[par])
            p.op("scalar", lambda E, dt=dt, par=par: E.copy(
                x16p[par][:, :], xmid[:, dt * TC:(dt + 1) * TC]))
            sqv = p.op("scalar", lambda E, par=par: E.activation(
                sq16p[par][:, :], x16p[par][:, :], AF.Square), inc=p.ss)
            p.wait("tensor", p.ss, sqv)
            stv[dt] = p.op("tensor", lambda E, dt=dt, par=par: (
                E.matmul(b["ps_S"][0:1, :], b["ones16"][:, :], x16p[par][:, :],
                         start=(dt == 0), stop=(dt == DT - 1), skip_group_check=True),
                E.matmul(b["ps_SS"][0:1, :], b["ones16"][:, :], sq16p[par][:, :],
                         start=(dt == 0), stop=(dt == DT - 1), skip_group_check=True))[-1],
                inc=p.ts)
            st_free[par] = stv[dt]
        p.wait("vector", p.ts, stv[DT - 1])
        p.op("vector", lambda E: E.tensor_copy(rb["S"][0:1, :], b["ps_S"][0:1, :]))
        p.op("vector", lambda E: E.tensor_copy(rb["SS"][0:1, :], b["ps_SS"][0:1, :]))
        rm = emit_rowmath_composite(p, rb, with_inner=False, n=TC, off=0)
        p.wait("tensor", p.vs, rm)
        tv = bcasts((rb["A"][0:1, :], rb["B"][0:1, :]))
        p.wait("vector", p.ts, tv)
        p.wait("scalar", p.ts, tv)
        hfv = {}
        for dt in range(DT):
            p.op("vector", lambda E, dt=dt: E.tensor_mul(
                tmp1[:, :], xmid[:, dt * TC:(dt + 1) * TC], ps_b1[:, :]))
            vv = p.op("vector", lambda E: E.tensor_sub(tmp2[:, :], tmp1[:, :],
                                                       ps_b2[:, :]), inc=p.vs)
            p.wait("scalar", p.vs, vv)
            sv = p.op("scalar", lambda E, dt=dt: E.activation(
                xnb[:, dt * TC:(dt + 1) * TC], tmp2[:, :], AF.Identity,
                bias=cols["lnb"][:, dt:dt + 1], scale=cols["lng"][:, dt:dt + 1]),
                inc=p.ss)
            p.wait("tensor", p.ss, sv)
            if dt == 0:
                p.wait("tensor", p.vs, work_free[1])
            hfv[dt] = p.op("tensor", lambda E, dt=dt: E.matmul(
                ps_workb[0:32, 0:TC], wdn16[:, dt * 32:(dt + 1) * 32],
                xnb[:, dt * TC:(dt + 1) * TC],
                start=(dt == 0), stop=(dt == DT - 1)), inc=p.ts)
        p.wait("vector", p.ts, hfv[DT - 1])
        hv = p.op("vector", lambda E: E.tensor_scalar(
            hf_ext16[0:32, 0:TC], ps_workb[0:32, 0:TC], bdn_c[:, :], 0.0,
            ALU.add, ALU.max), inc=p.vs)
        work_free[1] = hv
        p.wait("tensor", p.vs, hv)
        stv = {}
        for dt in range(DT):
            par = dt % 2
            p.wait("tensor", p.vs, work_free[par])
            tv2 = p.op("tensor", lambda E, dt=dt, par=par: E.matmul(
                work[par][:, 0:TC], wupb16[:, dt * 128:(dt + 1) * 128],
                hf_ext16[:, 0:TC], start=True, stop=True), inc=p.ts)
            p.op("vector", lambda E, dt=dt, xch=xch: E.tensor_add(
                tmp1[:, :], xch[:, dt * TC:(dt + 1) * TC],
                xmid[:, dt * TC:(dt + 1) * TC]))
            p.wait("vector", p.ts, tv2)
            xv = p.op("vector", lambda E, dt=dt, xch=xch, par=par: E.tensor_add(
                xch[:, dt * TC:(dt + 1) * TC], tmp1[:, :], work[par][:, 0:TC]),
                inc=p.vs)
            work_free[par] = xv
            p.wait("scalar", p.vs, xv)
            p.wait("scalar", p.ts, st_free[par])
            p.op("scalar", lambda E, dt=dt, par=par, xch=xch: E.copy(
                x16p[par][:, :], xch[:, dt * TC:(dt + 1) * TC]))
            sqv = p.op("scalar", lambda E, par=par: E.activation(
                sq16p[par][:, :], x16p[par][:, :], AF.Square), inc=p.ss)
            p.wait("tensor", p.ss, sqv)
            stv[dt] = p.op("tensor", lambda E, dt=dt, par=par: (
                E.matmul(b["ps_S"][0:1, :], b["ones16"][:, :], x16p[par][:, :],
                         start=(dt == 0), stop=(dt == DT - 1), skip_group_check=True),
                E.matmul(b["ps_SS"][0:1, :], b["ones16"][:, :], sq16p[par][:, :],
                         start=(dt == 0), stop=(dt == DT - 1), skip_group_check=True))[-1],
                inc=p.ts)
            st_free[par] = stv[dt]
        p.wait("vector", p.ts, stv[DT - 1])
        p.op("vector", lambda E: E.tensor_copy(rb["S"][0:1, :], b["ps_S"][0:1, :]))
        p.op("vector", lambda E: E.tensor_copy(rb["SS"][0:1, :], b["ps_SS"][0:1, :]))
        rm = emit_rowmath_composite(p, rb, with_inner=True, n=TC, off=0)
        p.wait("tensor", p.vs, rm)
        tv = bcasts((rb["A"][0:1, :], rb["B"][0:1, :]))
        p.wait("vector", p.ts, tv)
        p.wait("scalar", p.ts, tv)
        for dt in range(DT):
            p.op("vector", lambda E, dt=dt, xch=xch: E.tensor_mul(
                tmp1[:, :], xch[:, dt * TC:(dt + 1) * TC], ps_b1[:, :]))
            vv = p.op("vector", lambda E: E.tensor_sub(tmp2[:, :], tmp1[:, :],
                                                       ps_b2[:, :]), inc=p.vs)
            p.wait("scalar", p.vs, vv)
            sv = p.op("scalar", lambda E, dt=dt: E.activation(
                xnb[:, dt * TC:(dt + 1) * TC], tmp2[:, :], AF.Identity,
                bias=cols["lnbb"][:, dt:dt + 1], scale=cols["lnw"][:, dt:dt + 1]),
                inc=p.ss)
        xn_ready = sv
        p.wait("tensor", p.ss, xn_ready)
        stv = {}
        for do in range(DT):
            par = do % 2
            sb = sb16a if par == 0 else sb16b
            if (ch, do) != (0, 0) and (do >= 2 or ch > 0):
                key = (ch, do - 2) if do >= 2 else (ch - 1, DT - 2 + do)
                p.wait("sync", p.ts, sg_used[key])
            lv = p.dma("sync", sb[:, :], sgs[do * 128:(do + 1) * 128, :])
            p.wait("tensor", p.qs, lv)
            p.wait("tensor", p.vs, work_free[par])
            tv2 = None
            for dt in range(DT):
                tv2 = p.op("tensor", lambda E, dt=dt, sb=sb, par=par: E.matmul(
                    work[par][:, 0:TC], sb[:, dt * 128:(dt + 1) * 128],
                    xnb[:, dt * TC:(dt + 1) * TC],
                    start=(dt == 0), stop=(dt == DT - 1)),
                    inc=p.ts if dt == DT - 1 else None)
            sg_used[(ch, do)] = tv2
            p.wait("vector", p.ts, tv2)
            p.op("vector", lambda E, do=do, par=par: E.tensor_scalar(
                tmp1[:, :], work[par][:, 0:TC], sg0c[:, do:do + 1],
                cols["be0"][:, do:do + 1], ALU.mult, ALU.add))
            xv = p.op("vector", lambda E, do=do, xch=xch: E.tensor_add(
                xch[:, do * TC:(do + 1) * TC], xch[:, do * TC:(do + 1) * TC],
                tmp1[:, :]), inc=p.vs)
            work_free[par] = xv
            p.wait("scalar", p.vs, xv)
            p.wait("scalar", p.ts, st_free[par])
            p.op("scalar", lambda E, do=do, par=par, xch=xch: E.copy(
                x16p[par][:, :], xch[:, do * TC:(do + 1) * TC]))
            sqv = p.op("scalar", lambda E, par=par: E.activation(
                sq16p[par][:, :], x16p[par][:, :], AF.Square), inc=p.ss)
            p.wait("tensor", p.ss, sqv)
            stv[do] = p.op("tensor", lambda E, do=do, par=par: (
                E.matmul(b["ps_S"][0:1, :], b["ones16"][:, :], x16p[par][:, :],
                         start=(do == 0), stop=(do == DT - 1), skip_group_check=True),
                E.matmul(b["ps_SS"][0:1, :], b["ones16"][:, :], sq16p[par][:, :],
                         start=(do == 0), stop=(do == DT - 1), skip_group_check=True))[-1],
                inc=p.ts)
            st_free[par] = stv[do]
        sv2 = store_xchunk(ch, xch, x2h, p.vs, work_free[(DT - 1) % 2])
        p.wait("vector", p.ts, stv[DT - 1])
        p.op("vector", lambda E: E.tensor_copy(rb["S"][0:1, :], b["ps_S"][0:1, :]))
        p.op("vector", lambda E: E.tensor_copy(rb["SS"][0:1, :], b["ps_SS"][0:1, :]))
        rm = emit_rowmath_composite(p, rb, with_inner=False, n=TC, off=0)
        p.wait("tensor", p.vs, rm)
        tv = bcasts((rb["A"][0:1, :], rb["B"][0:1, :]))
        p.wait("vector", p.ts, tv)
        for dt in range(DT):
            p.op("vector", lambda E, dt=dt, xch=xch: E.tensor_mul(
                tmp1[:, :], xch[:, dt * TC:(dt + 1) * TC], ps_b1[:, :]))
            vv = p.op("vector", lambda E, dt=dt: E.tensor_sub(
                xnb[:, dt * TC:(dt + 1) * TC], tmp1[:, :], ps_b2[:, :]), inc=p.vs)
        xn_ready = vv
        for do in range(DT):
            par = do % 2
            sb = sb16a if par == 0 else sb16b
            if do >= 2:
                p.wait("gpsimd", p.ts, q_used[(ch, do - 2)])
            else:
                p.wait("gpsimd", p.ts, sg_used[(ch, DT - 2 + do)])
            lv = p.dma("gpsimd", sb[:, :], aqa[do * 128:(do + 1) * 128, :])
            p.wait("tensor", p.qg, lv)
            p.wait("tensor", p.vs, xn_ready)
            p.wait("tensor", p.vs, work_free[par])
            tv2 = None
            for dt in range(DT):
                tv2 = p.op("tensor", lambda E, dt=dt, sb=sb, par=par: E.matmul(
                    work[par][:, 0:TC], sb[:, dt * 128:(dt + 1) * 128],
                    xnb[:, dt * TC:(dt + 1) * TC],
                    start=(dt == 0), stop=(dt == DT - 1)),
                    inc=p.ts if dt == DT - 1 else None)
            q_used[(ch, do)] = tv2
            p.wait("vector", p.ts, tv2)
            if ch > 0 or do >= 2:
                p.wait("vector", p.qs, q_st.get((ch, do - 2) if do >= 2 else
                                                (ch - 1, DT - 2 + do), 0))
            qv = p.op("vector", lambda E, do=do, par=par: E.tensor_scalar(
                q16op[par][:, :], work[par][:, 0:TC], cols["bq"][:, do:do + 1], None,
                ALU.add), inc=p.vs)
            work_free[par] = qv
            p.wait("sync", p.vs, qv)
            if ch == 0 and do == 0:
                q_st = {}
            q_st[(ch, do)] = p.dma("sync", qh[do * 128:(do + 1) * 128,
                                              ch * TC:(ch + 1) * TC], q16op[par][:, :])
        p.wait("tensor", p.qg, cg0)
        for (a16w, bcol, outh) in ((ak16, bk_c, kh), (av16, bv_c, None)):
            par = 0 if outh is not None else 1
            p.wait("tensor", p.vs, work_free[par])
            tv2 = None
            for dt in range(DT):
                tv2 = p.op("tensor", lambda E, dt=dt, a16w=a16w, par=par: E.matmul(
                    work[par][:, 0:TC], a16w[:, dt * 128:(dt + 1) * 128],
                    xnb[:, dt * TC:(dt + 1) * TC],
                    start=(dt == 0), stop=(dt == DT - 1)),
                    inc=p.ts if dt == DT - 1 else None)
            p.wait("vector", p.ts, tv2)
            if outh is not None:
                p.wait("vector", p.qs, q_st[(ch, DT - 1)])
                qv = p.op("vector", lambda E, bcol=bcol, par=par: E.tensor_scalar(
                    q16op[par][:, :], work[par][:, 0:TC], bcol[:, :], None, ALU.add),
                    inc=p.vs)
                work_free[par] = qv
                p.wait("sync", p.vs, qv)
                q_st[(ch, DT)] = p.dma("sync", outh[:, ch * TC:(ch + 1) * TC],
                                       q16op[par][:, :])
            else:
                qv = p.op("vector", lambda E, bcol=bcol, ch=ch, par=par: E.tensor_scalar(
                    vtmp[:, ch * TC:(ch + 1) * TC], work[par][:, 0:TC], bcol[:, :],
                    None, ALU.add), inc=p.vs)
                work_free[par] = qv
        prev_chunk_done = qv
    prev_done2 = prev_chunk_done

    p.wait("tensor", p.vs, prev_done2)
    vt_prev = None
    for ktl in range(DT):
        if ktl > 0:
            p.wait("tensor", p.vs, vv)
        tv = p.op("tensor", lambda E, ktl=ktl: E.matmul(
            ps_ar[:, 0:128], vtmp[:, ktl * 128:(ktl + 1) * 128], idt16[:, :],
            start=True, stop=True), inc=p.ts)
        p.wait("vector", p.ts, tv)
        if vt_prev is None:
            p.wait("vector", p.qs, q_st[(NCH - 1, DT)])
        else:
            p.wait("vector", p.qg, vt_prev)
        vv = p.op("vector", lambda E: E.tensor_copy(q16o[:, 0:128], ps_ar[:, 0:128]),
                  inc=p.vs)
        p.wait("gpsimd", p.vs, vv)
        vt_prev = p.dma("gpsimd", vh[:, ktl * 128:(ktl + 1) * 128], q16o[:, 0:128])

    if upto <= 4:
        p.wait("sync", p.qs, p.qs.n)
        p.wait("gpsimd", p.qg, p.qg.n)
        mctx.close()
        with ctx, nc.Block() as block:
            p.finalize(block)
        return nc

    mctx.close()
    quant_guard_ts = p.ts.n
    quant_guard_vs = p.vs.n
    quant_guard_qg = p.qg.n
    p.wait("sync", p.ts, quant_guard_ts)
    p.wait("sync", p.vs, quant_guard_vs)
    p.wait("sync", p.qg, quant_guard_qg)
    qctx = ExitStack()
    HH = HID // 2
    qst_a = sbuf(qctx, nc, "qst_a", [128, HH], F32)
    qst_b = sbuf(qctx, nc, "qst_b", [128, HH], F32)
    qa_t = sbuf(qctx, nc, "qa_t", [128, HH], F32)
    qb_t = sbuf(qctx, nc, "qb_t", [128, HH], F32)
    q16h = sbuf(qctx, nc, "q16h", [128, HH], BF16)
    for (wa, wq, nstrip, wid) in ((w1a, w1qo, DT, HID), (w2a, w2qo, DT, HID),
                                  (w3a, w3qo, NHT, D)):
        nh = 2 if wid == HID else 1
        seg = wid // nh
        if wa is not w1a:
            p.wait("sync", p.vs, p.vs.n)
        p.op("vector", lambda E: E.memset(gacc[:, :], 0.0))
        g_done = {}
        for i in range(nstrip * nh):
            qst = qst_a if i % 2 == 0 else qst_b
            dt_i, h_i = divmod(i, nh)
            if i >= 2:
                p.wait("sync", p.vs, g_done[i - 2])
            lv = p.dma("sync", qst[:, 0:seg],
                       wa[dt_i * 128:(dt_i + 1) * 128, h_i * seg:(h_i + 1) * seg])
            p.wait("vector", p.qs, lv)
            p.vop(lambda E, qst=qst, seg=seg: E.tensor_reduce(
                mxt[:, :], qst[:, 0:seg], AX.X, ALU.add, apply_absolute_value=True))
            gv = p.vop(lambda E: E.tensor_add(gacc[:, :], gacc[:, :], mxt[:, :]))
            g_done[i] = gv
        p.wait("tensor", p.vs, g_done[nstrip * nh - 1])
        tv = p.op("tensor", lambda E: E.matmul(b["ps_S"][0:1, 0:128], gacc[:, :],
                                               idt32[:, :], start=True, stop=True),
                  inc=p.ts)
        p.wait("vector", p.ts, tv)
        p.vop(lambda E: E.tensor_reduce(srow[0:1, 1:2],
                                        b["ps_S"][0:1, 0:128], AX.X, ALU.add))
        nelem = float(D * HID)
        p.vop(lambda E: E.tensor_scalar(
            srow[0:1, 1:2], srow[0:1, 1:2], 0.5 / nelem, 0.5e-8, ALU.mult, ALU.add))
        vv = p.vop(lambda E: E.tensor_scalar_mul(
            srow[0:1, 2:3], srow[0:1, 1:2], -1.0))
        p.wait("tensor", p.vs, vv)
        tv = p.op("tensor", lambda E: (
            E.matmul(ps_b1[:, 0:1], ones1f[:, :], srow[0:1, 1:2], start=True, stop=True),
            E.matmul(ps_b2[:, 0:1], ones1f[:, :], srow[0:1, 2:3], start=True, stop=True))[-1],
            inc=p.ts)
        p.wait("vector", p.ts, tv)
        p.vop(lambda E: E.tensor_copy(thp_c[:, :], ps_b1[:, 0:1]))
        thv = p.vop(lambda E: E.tensor_copy(thn_c[:, :], ps_b2[:, 0:1]))
        qq_done = {}
        for i in range(nstrip * nh):
            qst = qst_a if i % 2 == 0 else qst_b
            dt_i, h_i = divmod(i, nh)
            if i >= 2:
                p.wait("sync", p.vs, qq_done[i - 2])
            lv = p.dma("sync", qst[:, 0:seg],
                       wa[dt_i * 128:(dt_i + 1) * 128, h_i * seg:(h_i + 1) * seg])
            p.wait("vector", p.qs, lv)
            p.wait("vector", p.vs, thv)
            av = p.op("vector", lambda E, qst=qst, seg=seg: E.tensor_single_scalar(
                qa_t[:, 0:seg], qst[:, 0:seg], thp_c[:, :], ALU.is_gt), inc=p.vs)
            bv_ = p.op("vector", lambda E, qst=qst, seg=seg: E.tensor_single_scalar(
                qb_t[:, 0:seg], qst[:, 0:seg], thn_c[:, :], ALU.is_lt), inc=p.vs)
            if i >= 1:
                p.wait("vector", p.qg, qw_prev)
            gv = p.op("vector", lambda E, seg=seg: E.tensor_sub(
                q16h[:, 0:seg], qa_t[:, 0:seg], qb_t[:, 0:seg]), inc=p.vs)
            qq_done[i] = gv
            p.wait("gpsimd", p.vs, gv)
            qw_prev = p.dma("gpsimd", wq[dt_i * 128:(dt_i + 1) * 128,
                                         h_i * seg:(h_i + 1) * seg], q16h[:, 0:seg])
    p.wait("gpsimd", p.qg, p.qg.n)
    p.wait("sync", p.qs, p.qs.n)
    qctx.close()

    with ctx, nc.Block() as block:
        p.finalize(block)
    return nc



def _cols(v):
    return np.ascontiguousarray(v.reshape(DT, 128).T).astype(np.float32)


def _strips(w_t, m_tiles):
    Kd, M = w_t.shape
    kt = Kd // 128
    a = w_t.reshape(kt, 128, m_tiles, 128)
    a = a.transpose(2, 1, 0, 3).reshape(m_tiles * 128, kt * 128)
    return np.ascontiguousarray(a).astype(np.float32)


def _stage_common(inp):
    st = {}
    st["lng_cols"] = _cols(inp["L0_attn_ln_gamma"])
    st["lnb_cols"] = _cols(inp["L0_attn_ln_beta"])
    st["win_arr"] = np.ascontiguousarray(
        inp["L0_attn_W_in"].reshape(DT, 128, R).transpose(1, 0, 2)
        .reshape(128, DT * R)).astype(np.float32)
    st["b_in_col"] = inp["L0_attn_b_in"].reshape(R, 1).astype(np.float32)
    st["dec_col"] = inp["L0_attn_decay_logit"].reshape(R, 1).astype(np.float32)
    st["woutb"] = np.ascontiguousarray(np.concatenate(
        [inp["L0_attn_W_out"], inp["L0_attn_b_out"][None, :]], 0)).astype(np.float32)
    st["d_cols"] = _cols(inp["L0_attn_D_skip"])
    st["wdn_arr"] = np.ascontiguousarray(
        inp["L0_attn_W_down"].reshape(DT, 128, 32).transpose(1, 0, 2)
        .reshape(128, DT * 32)).astype(np.float32)
    st["bdn_col"] = inp["L0_attn_b_down"].reshape(32, 1).astype(np.float32)
    st["wupb"] = np.ascontiguousarray(np.concatenate(
        [inp["L0_attn_W_up"], inp["L0_attn_b_up"][None, :]], 0)).astype(np.float32)
    st["l0w_arr"] = _strips(np.ascontiguousarray(inp["L0_ffn_weight"].T), DT)
    st["g0_cols"] = _cols(inp["L0_ffn_gamma"])
    st["be0_cols"] = _cols(inp["L0_ffn_beta"])
    st["lnw_cols"] = _cols(inp["L0_ffn_ln_weight"])
    st["lnbb_cols"] = _cols(inp["L0_ffn_ln_bias"])
    st["w1_arr"] = np.ascontiguousarray(
        inp["L1_ffn_w1"].T.reshape(DT, 128, HID).reshape(D, HID)).astype(np.float32)
    st["w2_arr"] = np.ascontiguousarray(inp["L1_ffn_w2"].T).astype(np.float32)
    st["w3_arr"] = np.ascontiguousarray(inp["L1_ffn_w3"].T).astype(np.float32)
    st["aq_arr"] = _strips(np.ascontiguousarray(inp["L1_attn_Wq"].T), DT)
    st["ak_arr"] = np.ascontiguousarray(
        inp["L1_attn_Wk"].reshape(128, DT, 128).transpose(2, 1, 0)
        .reshape(128, D)).astype(np.float32)
    st["av_arr"] = np.ascontiguousarray(
        inp["L1_attn_Wv"].reshape(128, DT, 128).transpose(2, 1, 0)
        .reshape(128, D)).astype(np.float32)
    st["bq_cols"] = _cols(inp["L1_attn_bq"])
    st["bk_col"] = inp["L1_attn_bk"].reshape(128, 1).astype(np.float32)
    st["bv_col"] = inp["L1_attn_bv"].reshape(128, 1).astype(np.float32)
    st["idt32"] = np.eye(128, dtype=np.float32)
    st["ao_arr"] = _strips(np.ascontiguousarray(inp["L1_attn_Wo"].T), DT)
    st["bo_cols"] = _cols(inp["L1_attn_bo"])
    return st


def _in1(st, x_t_c):
    return dict(x_t=x_t_c, lng_cols=st["lng_cols"], lnb_cols=st["lnb_cols"],
                win_arr=st["win_arr"], b_in_col=st["b_in_col"])


def _in2(st, x_t_c, rows1_c, u_pad_c):
    keys = ["dec_col", "woutb", "d_cols", "lng_cols", "lnb_cols", "wdn_arr",
            "bdn_col", "wupb", "l0w_arr", "g0_cols", "be0_cols", "lnw_cols",
            "lnbb_cols", "w1_arr", "w2_arr", "w3_arr", "aq_arr", "ak_arr",
            "av_arr", "bq_cols", "bk_col", "bv_col", "idt32"]
    d = {k: st[k] for k in keys}
    d["x_t"] = x_t_c
    d["rows1"] = rows1_c
    d["u_pad"] = u_pad_c
    return d


def _masks_lb(chunks):
    sg = np.arange(SEQ, dtype=np.int32)[:, None]
    parts = []
    for s, gc in enumerate(chunks):
        qg = gc * TC + np.arange(TC, dtype=np.int32)[None, :]
        M = (sg < qg).reshape(NKT, 128, TC)[:NKTQ[s]]
        parts.append(M.reshape(NKTQ[s] * 128, TC))
    return np.ascontiguousarray(
        np.concatenate(parts, 0)).astype(ml_dtypes.bfloat16)


def _in3(st, c, x2c, qc_, kfull, vfull):
    return dict(x2_hand=x2c, q_own=qc_, k_full=kfull, v_full=vfull,
                masks=st["masks"][c], ao_arr=st["ao_arr"], bo_cols=st["bo_cols"],
                w1q=st["_w1q"], w2q=st["_w2q"], w3q=st["_w3q"])


def run_launches12(inp, cores=None):
    if cores is None:
        cores = list(range(8))
    x = inp["L0_attn_x"].astype(np.float32)
    st = _stage_common(inp)
    x_t = []
    for c in cores:
        bidx, c4 = c // 4, c % 4
        x_t.append(np.ascontiguousarray(x[bidx, c4 * TOK:(c4 + 1) * TOK, :].T))

    nc1 = build_launch1()
    in1 = [_in1(st, x_t[c]) for c in cores]
    r1 = run_bass_kernel_spmd(nc1, in1, core_ids=cores).results

    u_full = np.zeros((2, R, SEQ), np.float32)
    for c in cores:
        bidx, c4 = c // 4, c % 4
        u_full[bidx, :, c4 * TOK:(c4 + 1) * TOK] = r1[c]["u_hand"]

    nc2 = build_launch2()
    in2 = []
    for c in cores:
        bidx, c4 = c // 4, c % 4
        off = c4 * TOK
        u_pad = np.zeros((R, SEQ), np.float32)
        u_pad[:, SEQ - (off + TOK):] = u_full[bidx][:, :off + TOK]
        in2.append(_in2(st, x_t[c], r1[c]["rows1"], np.ascontiguousarray(u_pad)))
    r2 = run_bass_kernel_spmd(nc2, in2, core_ids=cores).results
    return st, x_t, r2


def _gchunks(c):
    return [15 - c // 2, 11 - c // 2, 7 - c // 2, 3 - c // 2]


def kernel(**inp):
    inp = {k: np.asarray(v) for k, v in inp.items()}
    cores = list(range(8))
    st, x_t, r2 = run_launches12(inp, cores)
    st["masks"] = [_masks_lb(_gchunks(c)) for c in cores]
    kfull, vfull, x2full, qfull = {}, {}, {}, {}
    for bi in range(2):
        kk = np.zeros((128, SEQ), ml_dtypes.bfloat16)
        vv = np.zeros((128, SEQ), ml_dtypes.bfloat16)
        for c4 in range(4):
            c = bi * 4 + c4
            kk[:, c4 * TOK:(c4 + 1) * TOK] = r2[c]["k_hand"]
            vv[:, c4 * TOK:(c4 + 1) * TOK] = r2[c]["v_hand"]
        kfull[bi] = kk
        vfull[bi] = vv
        x2full[bi] = np.concatenate(
            [r2[bi * 4 + c4]["x2_hand"] for c4 in range(4)], 1)
        qfull[bi] = np.concatenate(
            [r2[bi * 4 + c4]["q_hand"] for c4 in range(4)], 1)
    nc3 = build_launch3()
    in3 = []
    for c in cores:
        bi = c % 2
        G = _gchunks(c)
        x2c = np.ascontiguousarray(np.concatenate(
            [x2full[bi][:, gc * TC:(gc + 1) * TC] for gc in G], 1))
        qc_ = np.ascontiguousarray(np.concatenate(
            [qfull[bi][:, gc * TC:(gc + 1) * TC] for gc in G], 1))
        st["_w1q"], st["_w2q"], st["_w3q"] = (r2[c]["w1q"], r2[c]["w2q"],
                                              r2[c]["w3q"])
        in3.append(_in3(st, c, x2c, qc_, kfull[bi], vfull[bi]))
    r3 = run_bass_kernel_spmd(nc3, in3, core_ids=cores).results
    out = np.zeros((2, SEQ, D), np.float32)
    for c in cores:
        bi = c % 2
        for s, gc in enumerate(_gchunks(c)):
            out[bi, gc * TC:(gc + 1) * TC, :] = \
                r3[c]["out_t"][:, s * TC:(s + 1) * TC].T
    return out


# version 06
# speedup vs baseline: 1.3402x; 1.0164x over previous
import numpy as np
import ml_dtypes
from contextlib import ExitStack
import concourse.bass as bass
import concourse.mybir as mybir
from concourse.bass_utils import run_bass_kernel_spmd

F32 = mybir.dt.float32
BF16 = mybir.dt.bfloat16
AF = mybir.ActivationFunctionType
ALU = mybir.AluOpType
AX = mybir.AxisListType

D = 2048
DT = 16
TOK = 2048
TC = 512
NCH = TOK // TC
SEQ = 8192
R = 8
EPS = 1e-5
HID = 5632
NHT = HID // 128
NHC = 11
NKT = 64
NKTQ = [64, 48, 32, 16]
MOFF = [0, 64, 112, 144]
MTOT = 160
SCALE_ATTN = float(1.0 / np.sqrt(128.0))
INVD = float(1.0 / D)


class Ctr:

    def __init__(self, ctx, nc, name):
        self.h = ctx.enter_context(nc.semaphore(name))
        self.n = 0

    def bump(self, k):
        self.n += k
        return self.n


class Prog:

    ENGINES = ("sync", "gpsimd", "tensor", "scalar", "vector")

    def __init__(self, nc, ctx):
        self.nc = nc
        self.ctx = ctx
        self.cmds = {e: [] for e in self.ENGINES}
        self.qs = Ctr(ctx, nc, "qs")
        self.qg = Ctr(ctx, nc, "qg")
        self.ts = Ctr(ctx, nc, "ts")
        self.ss = Ctr(ctx, nc, "ss")
        self.vs = Ctr(ctx, nc, "vs")
        self.gs = Ctr(ctx, nc, "gs")

    def dma(self, eng, dst, src):
        q = self.qs if eng == "sync" else self.qg
        v = q.bump(16)
        self.cmds[eng].append(lambda E: E.dma_start(out=dst, in_=src).then_inc(q.h, 16))
        return v

    def wait(self, eng, ctr, val):
        self.cmds[eng].append(lambda E: E.wait_ge(ctr.h, val))

    def op(self, eng, f, inc=None):
        if inc is None:
            self.cmds[eng].append(lambda E: f(E))
            return None
        v = inc.bump(1)
        self.cmds[eng].append(lambda E: f(E).then_inc(inc.h, 1))
        return v

    def vop(self, f):
        v = self.op("vector", f, inc=self.vs)
        self.wait("vector", self.vs, v)
        return v

    def sop(self, f):
        v = self.op("scalar", f, inc=self.ss)
        self.wait("scalar", self.ss, v)
        return v

    def finalize(self, block):
        for name in self.ENGINES:
            cl = self.cmds[name]
            def body(E, cl=cl):
                for f in cl:
                    f(E)
            getattr(block, name)(body)


def sbuf(ctx, nc, name, shape, dt=F32):
    return ctx.enter_context(nc.sbuf_tensor(name, shape, dt))


def psum(ctx, nc, name, shape, dt=F32):
    return ctx.enter_context(nc.psum_tensor(name, shape, dt))



def build_launch3():
    nc = bass.Bass("TRN2", target_bir_lowering=False)
    ap = lambda n, s, d=F32, k="ExternalInput": nc.dram_tensor(n, s, d, kind=k).ap()
    x2h = ap("x2_hand", [D, TOK])
    qo = ap("q_own", [D, TOK], BF16)
    kf_d = ap("k_full", [128, SEQ], BF16)
    vf_d = ap("v_full", [128, SEQ], BF16)
    masks = ap("masks", [MTOT * 128, TC], BF16)
    aoa = ap("ao_arr", [D, D])
    bo = ap("bo_cols", [128, DT])
    w1qh = ap("w1q", [D, HID], BF16)
    w2qh = ap("w2q", [D, HID], BF16)
    w3qh = ap("w3q", [HID, D], BF16)
    out_t = ap("out_t", [D, TOK], F32, "ExternalOutput")

    ctx = ExitStack()
    p = Prog(nc, ctx)
    b = {}
    b["x16a"] = sbuf(ctx, nc, "x16a", [128, TC], BF16)
    b["x16b"] = sbuf(ctx, nc, "x16b", [128, TC], BF16)
    b["sq16a"] = sbuf(ctx, nc, "sq16a", [128, TC], BF16)
    b["sq16b"] = sbuf(ctx, nc, "sq16b", [128, TC], BF16)
    b["ones16"] = sbuf(ctx, nc, "ones16", [128, 1], BF16)
    ones1f = sbuf(ctx, nc, "ones1f", [1, 128], F32)
    rb = {k: sbuf(ctx, nc, "row_" + k, [1, TC], F32)
          for k in ("S", "SS", "A", "B", "r", "mur", "t0", "t1", "t2")}
    tmp1 = sbuf(ctx, nc, "tmp1", [128, TC], F32)
    kf = sbuf(ctx, nc, "kf", [128, SEQ], BF16)
    vf = sbuf(ctx, nc, "vf", [128, SEQ], BF16)
    qp_a = sbuf(ctx, nc, "qp_a", [128, 2 * TC], BF16)
    qp_b = qp_a
    AO = sbuf(ctx, nc, "AO", [128, DT * TC], BF16)
    acc = sbuf(ctx, nc, "acc", [128, DT * TC], F32)
    xnb4 = sbuf(ctx, nc, "xnb4", [128, DT * TC], BF16)
    rs_pair = xnb4[:, 14 * TC:16 * TC]
    uni = sbuf(ctx, nc, "uni", [128, NKT * TC], BF16)
    w1s = uni[:, 0:DT * TC]
    w2s = uni[:, DT * TC:2 * DT * TC]
    w3s = uni[:, 2 * DT * TC:2 * DT * TC + 4 * D]
    exp_sb = sbuf(ctx, nc, "exp_sb", [128, 4 * TC], BF16)
    rs_row = sbuf(ctx, nc, "rs_row", [1, 2 * TC], F32)
    rs_rec = sbuf(ctx, nc, "rs_rec", [1, 2 * TC], F32)
    rcb = tmp1
    sb3a = sbuf(ctx, nc, "sb3a", [128, D], BF16)
    sb3b = sbuf(ctx, nc, "sb3b", [128, D], BF16)
    bo_c = sbuf(ctx, nc, "bo_c", [128, DT], F32)
    hid_sb = exp_sb
    sg16 = b["x16a"]

    ps_av = psum(ctx, nc, "ps_av", [128, 2 * TC])
    ps_rs = psum(ctx, nc, "ps_rs", [1, TC])
    ps_sc = psum(ctx, nc, "ps_sc", [128, 4 * TC])
    ps_bc = psum(ctx, nc, "ps_bc", [128, TC])
    b["ps_S"] = ps_bc
    b["ps_SS"] = ps_av

    p.dma("sync", bo_c[:, :], bo)
    p.dma("sync", kf[:, :], kf_d)
    kvl = p.dma("sync", vf[:, :], vf_d)
    mz = p.op("vector", lambda E: (E.memset(b["ones16"][:, :], 1.0),
                                   E.memset(ones1f[:, :], 1.0))[-1], inc=p.vs)

    prev_h_done = mz
    prev_exp = 0
    rsp_last = {0: 0, 1: 0}
    prev_ao_st = {}
    last_acc_store = 0
    prev_ffn_vs = mz
    prev_w3_use = 0
    prev_sg = 0
    last_scores = {}
    for qc in range(NCH):
        nktq = NKTQ[qc]
        mask_q = {}
        kt_order = (list(range(48, nktq)) + list(range(0, min(nktq, 32)))
                    + list(range(32, min(nktq, 48))))
        for kt in kt_order:
            if qc > 0:
                if kt >= 48:
                    p.wait("gpsimd", p.ts, prev_w3b_use)
                elif kt < 32:
                    p.wait("gpsimd", p.ts, prev_w12_use)
                else:
                    p.wait("gpsimd", p.ts, prev_w3_use)
            mask_q[kt] = p.dma("gpsimd",
                               uni[:, kt * TC:(kt + 1) * TC],
                               masks[(MOFF[qc] + kt) * 128:
                                     (MOFF[qc] + kt) * 128 + 128, :])
        if qc > 0:
            p.wait("sync", p.vs, prev_ffn_vs)
        for dt in range(DT):
            x2l = p.dma("sync", acc[:, dt * TC:(dt + 1) * TC],
                        x2h[dt * 128:(dt + 1) * 128, qc * TC:(qc + 1) * TC])
        p.wait("tensor", p.qg, max(mask_q.values()))
        p.wait("tensor", p.qs, kvl)
        exp_done = {}
        mask_done = {}
        pair_done = {}
        rs_mm = {}
        av_done = {}
        epi_av = {}
        epi_rs = {}
        for h in range(DT):
            qb = qp_a[:, 0:TC] if h % 2 == 0 else qp_a[:, TC:2 * TC]
            ps_avh = ps_av[:, 0:TC] if h % 2 == 0 else ps_av[:, TC:2 * TC]
            if (qc, h) != (0, 0) and (h >= 2 or qc > 0):
                key = (qc, h - 2) if h >= 2 else (qc - 1, DT - 2 + h)
                p.wait("sync", p.ts, last_scores[key])
            qldv = p.dma("sync", qb,
                         qo[h * 128:(h + 1) * 128, qc * TC:(qc + 1) * TC])
            p.wait("tensor", p.qs, qldv)
            NG = nktq // 2
            for i in range(NG):
                par = i % 2
                scs = ps_sc[:, par * 2 * TC:(par + 1) * 2 * TC]
                esl = exp_sb[:, par * 2 * TC:(par + 1) * 2 * TC]
                if i >= 2:
                    p.wait("tensor", p.ss, exp_done[(h, i - 2)])
                elif h > 0:
                    p.wait("tensor", p.ss, exp_done[(h - 1, NG - 2 + i)])
                elif qc > 0 and prev_exp:
                    p.wait("tensor", p.ss, prev_exp)
                if h == 0 and i < 2:
                    p.wait("tensor", p.vs, prev_h_done)
                    if qc > 0:
                        p.wait("tensor", p.vs, prev_ffn_vs)
                tv = None
                for j in range(2):
                    kt = i * 2 + j
                    tv = p.op("tensor", lambda E, j=j, kt=kt, par=par, qb=qb: E.matmul(
                        ps_sc[:, (par * 2 + j) * TC:(par * 2 + j + 1) * TC],
                        kf[:, kt * 128:(kt + 1) * 128], qb,
                        start=True, stop=True), inc=p.ts if j == 1 else None)
                if i == NG - 1:
                    last_scores[(qc, h)] = tv
                if i >= 1:
                    ip = i - 1
                    pp = ip % 2
                    p.wait("tensor", p.vs, pair_done[(h, ip)])
                    if ip == 0 and h >= 2:
                        p.wait("tensor", p.vs, epi_av[h - 2])
                    if ip == 0 and h >= 1:
                        p.wait("tensor", p.vs, epi_rs[h - 1])
                    for j in range(2):
                        kt = ip * 2 + j
                        p.op("tensor", lambda E, j=j, kt=kt, pp=pp, ip=ip,
                             ps_avh=ps_avh: E.matmul(
                            ps_avh, vf[:, kt * 128:(kt + 1) * 128],
                            exp_sb[:, (pp * 2 + j) * TC:(pp * 2 + j + 1) * TC],
                            start=(ip == 0 and j == 0), stop=False,
                            skip_group_check=True))
                    av = p.op("tensor", lambda E, pp=pp, ip=ip, NG=NG: E.matmul(
                        ps_rs[:, :], b["ones16"][:, :],
                        rs_pair[:, pp * TC:(pp + 1) * TC],
                        start=(ip == 0), stop=False,
                        skip_group_check=True), inc=p.ts)
                    rs_mm[(h, ip)] = av
                    av_done[(h, ip)] = av
                p.wait("scalar", p.ts, tv)
                if i >= 2:
                    p.wait("scalar", p.ts, av_done[(h, i - 2)])
                sv = p.op("scalar", lambda E, scs=scs, esl=esl: E.activation(
                    esl, scs, AF.Exp, scale=SCALE_ATTN), inc=p.ss)
                exp_done[(h, i)] = sv
                p.wait("vector", p.ss, sv)
                if h == 0:
                    p.wait("vector", p.qg, mask_q[i * 2 + 1])
                mv2 = p.op("vector", lambda E, i=i, esl=esl: E.tensor_mul(
                    esl, esl, uni[:, i * 2 * TC:(i + 1) * 2 * TC]), inc=p.vs)
                mask_done[(h, i)] = mv2
                p.wait("vector", p.ts, rsp_last[par])
                pv = p.op("vector", lambda E, par=par: E.tensor_add(
                    rs_pair[:, par * TC:(par + 1) * TC],
                    exp_sb[:, (par * 2) * TC:(par * 2 + 1) * TC],
                    exp_sb[:, (par * 2 + 1) * TC:(par * 2 + 2) * TC]), inc=p.vs)
                pair_done[(h, i)] = pv
            ip = NG - 1
            pp = ip % 2
            p.wait("tensor", p.vs, pair_done[(h, ip)])
            for j in range(2):
                kt = ip * 2 + j
                p.op("tensor", lambda E, j=j, kt=kt, pp=pp, ps_avh=ps_avh: E.matmul(
                    ps_avh, vf[:, kt * 128:(kt + 1) * 128],
                    exp_sb[:, (pp * 2 + j) * TC:(pp * 2 + j + 1) * TC],
                    start=False, stop=(j == 1), skip_group_check=True))
            av = p.op("tensor", lambda E, pp=pp, ip=ip: E.matmul(
                ps_rs[:, :], b["ones16"][:, :],
                rs_pair[:, pp * TC:(pp + 1) * TC],
                start=False, stop=True, skip_group_check=True), inc=p.ts)
            rs_mm[(h, ip)] = av
            rsp_last[0] = rs_mm[(h, NG - 2)]
            rsp_last[1] = rs_mm[(h, NG - 1)]
            av_done[(h, ip)] = av
            p.wait("vector", p.ts, av)
            erv = p.vop(lambda E: E.tensor_single_scalar(
                rs_row[0:1, 0:TC], ps_rs[0:1, :], 1e-30, ALU.max))
            epi_rs[h] = erv
            rv = p.vop(lambda E: E.reciprocal(rs_rec[0:1, 0:TC], rs_row[0:1, 0:TC]))
            p.wait("tensor", p.vs, rv)
            if h >= 1 or qc > 0:
                p.wait("tensor", p.vs, prev_h_done)
            tv = p.op("tensor", lambda E: E.matmul(
                ps_bc[:, :], ones1f[:, :], rs_rec[0:1, 0:TC], start=True, stop=True),
                inc=p.ts)
            p.wait("vector", p.ts, tv)
            p.vop(lambda E: E.tensor_copy(rcb[:, :], ps_bc[:, :]))
            hv = p.op("vector", lambda E, h=h, ps_avh=ps_avh: E.tensor_mul(
                AO[:, h * TC:(h + 1) * TC], ps_avh, rcb[:, :]), inc=p.vs)
            epi_av[h] = hv
            prev_h_done = hv
        for do in range(DT):
            sb = sb3a if do % 2 == 0 else sb3b
            gkey = (qc, do - 2) if do >= 2 else ((qc - 1, DT - 2 + do) if qc > 0 else None)
            if gkey is not None:
                p.wait("gpsimd", p.ts, prev_ao_st.get(gkey, 0))
            lv = p.dma("gpsimd", sb[:, :], aoa[do * 128:(do + 1) * 128, :])
            p.wait("tensor", p.qg, lv)
            p.wait("tensor", p.vs, prev_h_done)
            if do > 0 or qc > 0:
                p.wait("tensor", p.vs, prev_proj_v)
            tv = None
            for hh in range(DT):
                tv = p.op("tensor", lambda E, hh=hh, sb=sb: E.matmul(
                    ps_sc[:, 0:TC], sb[:, hh * 128:(hh + 1) * 128],
                    AO[:, hh * TC:(hh + 1) * TC],
                    start=(hh == 0), stop=(hh == DT - 1)),
                    inc=p.ts if hh == DT - 1 else None)
            prev_ao_st[(qc, do)] = tv
            p.wait("vector", p.ts, tv)
            p.wait("vector", p.qs, x2l)
            p.op("vector", lambda E, do=do: E.tensor_scalar(
                tmp1[:, :], ps_sc[:, 0:TC], bo_c[:, do:do + 1], None, ALU.add))
            xv = p.op("vector", lambda E, do=do: E.tensor_add(
                acc[:, do * TC:(do + 1) * TC], acc[:, do * TC:(do + 1) * TC],
                tmp1[:, :]), inc=p.vs)
            prev_proj_v = xv
        x3_done = prev_proj_v
        prev_exp = p.ss.n

        x_tiles = [acc[:, dt * TC:(dt + 1) * TC] for dt in range(DT)]
        p.wait("vector", p.vs, x3_done)
        ev = emit_stats_chunk(p, b, x_tiles, 0, rb["S"], rb["SS"], p.qs.n)
        rm = emit_rowmath_composite(p, rb, with_inner=False, n=TC, off=0)
        p.wait("tensor", p.vs, rm)
        bt = p.op("tensor", lambda E: (
            E.matmul(ps_bc[:, :], ones1f[:, :], rb["A"][0:1, :], start=True, stop=True),
            E.matmul(ps_av[:, 0:TC], ones1f[:, :], rb["B"][0:1, :], start=True, stop=True))[-1],
            inc=p.ts)
        p.wait("vector", p.ts, bt)
        for dt in range(DT):
            p.op("vector", lambda E, dt=dt: E.tensor_mul(
                tmp1[:, :], acc[:, dt * TC:(dt + 1) * TC], ps_bc[:, :]))
            vv = p.op("vector", lambda E, dt=dt: E.tensor_sub(
                xnb4[:, dt * TC:(dt + 1) * TC], tmp1[:, :], ps_av[:, 0:TC]), inc=p.vs)
        xn_ready = vv
        w3h = [uni[:, 2 * DT * TC:2 * DT * TC + 4 * D],
               uni[:, 2 * DT * TC + 4 * D:2 * DT * TC + 8 * D]]
        ld_done = {}
        lastp = lambda t: pair_done.get((DT - 1, t // 2), 0)
        for dt in range(DT):
            p.wait("sync", p.vs, lastp(dt))
            p.dma("sync", w1s[:, dt * TC:(dt + 1) * TC],
                  w1qh[dt * 128:(dt + 1) * 128, 0:TC])
        for dt in range(DT):
            p.wait("sync", p.vs, lastp(16 + dt))
            wl = p.dma("sync", w2s[:, dt * TC:(dt + 1) * TC],
                       w2qh[dt * 128:(dt + 1) * 128, 0:TC])
        for j in range(4):
            p.wait("sync", p.vs, lastp(35 + 4 * j))
            w3l = p.dma("sync", w3h[0][:, j * D:(j + 1) * D],
                        w3qh[j * 128:j * 128 + 128, :])
        ld_done[0] = max(wl, w3l)
        w3_use = {}
        for hc in range(NHC):
            w3cur = w3h[hc % 2]
            p.wait("tensor", p.qs, ld_done[hc])
            p.wait("tensor", p.vs, xn_ready)
            hid_done = {}
            for ht in range(4):
                if qc > 0 or hc > 0 or ht > 0:
                    p.wait("tensor", p.ss, prev_sg)
                    p.wait("tensor", p.vs, prev_hid)
                t1 = t2 = None
                for dt in range(DT):
                    t1 = p.op("tensor", lambda E, dt=dt, ht=ht: E.matmul(
                        ps_sc[:, 0:TC],
                        w1s[:, dt * TC + ht * 128: dt * TC + (ht + 1) * 128],
                        xnb4[:, dt * TC:(dt + 1) * TC],
                        start=(dt == 0), stop=(dt == DT - 1)),
                        inc=p.ts if dt == DT - 1 else None)
                for dt in range(DT):
                    t2 = p.op("tensor", lambda E, dt=dt, ht=ht: E.matmul(
                        ps_sc[:, TC:2 * TC],
                        w2s[:, dt * TC + ht * 128: dt * TC + (ht + 1) * 128],
                        xnb4[:, dt * TC:(dt + 1) * TC],
                        start=(dt == 0), stop=(dt == DT - 1)),
                        inc=p.ts if dt == DT - 1 else None)
                p.wait("scalar", p.ts, t1)
                sgv = p.op("scalar", lambda E: E.activation(
                    sg16[:, :], ps_sc[:, 0:TC], AF.Sigmoid), inc=p.ss)
                prev_sg = sgv
                p.wait("vector", p.ss, sgv)
                p.wait("vector", p.ts, t2)
                hv2 = p.op("vector", lambda E, ht=ht: E.tensor_mul(
                    hid_sb[:, ht * TC:(ht + 1) * TC], sg16[:, :],
                    ps_sc[:, TC:2 * TC]), inc=p.vs)
                prev_hid = hv2
                hid_done[ht] = hv2
            if hc + 1 < NHC:
                p.wait("sync", p.ts, t2)
                for dt in range(DT):
                    p.dma("sync", w1s[:, dt * TC:(dt + 1) * TC],
                          w1qh[dt * 128:(dt + 1) * 128, (hc + 1) * TC:(hc + 2) * TC])
                for dt in range(DT):
                    wl = p.dma("sync", w2s[:, dt * TC:(dt + 1) * TC],
                               w2qh[dt * 128:(dt + 1) * 128,
                                    (hc + 1) * TC:(hc + 2) * TC])
                if hc >= 1:
                    p.wait("sync", p.ts, w3_use[hc - 1])
                for j in range(4):
                    w3l = p.dma("sync", w3h[(hc + 1) % 2][:, j * D:(j + 1) * D],
                                w3qh[((hc + 1) * 4 + j) * 128:
                                     ((hc + 1) * 4 + j) * 128 + 128, :])
                ld_done[hc + 1] = max(wl, w3l)
            accv = {}
            for do in range(DT):
                ob = 2 + (do % 2)
                p.wait("tensor", p.vs, hid_done[3])
                if do >= 2:
                    p.wait("tensor", p.vs, accv[do - 2])
                elif qc > 0 or hc > 0:
                    p.wait("tensor", p.vs, prev_acc_v2)
                t3 = None
                for j in range(4):
                    t3 = p.op("tensor", lambda E, j=j, do=do, ob=ob, w3cur=w3cur: E.matmul(
                        ps_sc[:, ob * TC:(ob + 1) * TC],
                        w3cur[:, j * D + do * 128: j * D + (do + 1) * 128],
                        hid_sb[:, j * TC:(j + 1) * TC],
                        start=(j == 0), stop=(j == 3)),
                        inc=p.ts if j == 3 else None)
                p.wait("vector", p.ts, t3)
                av2 = p.op("vector", lambda E, do=do, ob=ob: E.tensor_add(
                    acc[:, do * TC:(do + 1) * TC], acc[:, do * TC:(do + 1) * TC],
                    ps_sc[:, ob * TC:(ob + 1) * TC]), inc=p.vs)
                accv[do] = av2
                prev_acc_v = av2
            prev_acc_v2 = accv[DT - 1] if DT >= 2 else prev_acc_v
            prev_acc_v2 = max(accv[DT - 1], accv[DT - 2])
            w3_use[hc] = t3
            prev_w3_use = t3
        prev_w12_use = t2
        prev_w3b_use = w3_use[NHC - 2]
        p.wait("sync", p.vs, prev_acc_v)
        for dt in range(DT):
            last_acc_store = p.dma("sync", out_t[dt * 128:(dt + 1) * 128,
                                                 qc * TC:(qc + 1) * TC],
                                   acc[:, dt * TC:(dt + 1) * TC])
        prev_ffn_vs = prev_acc_v
        prev_exp = p.ss.n

    p.wait("sync", p.qs, p.qs.n)
    p.wait("gpsimd", p.qg, p.qg.n)

    with ctx, nc.Block() as block:
        p.finalize(block)
    return nc



def emit_row_rstd(p, tmp_rows, var_row, out_row, n=TOK, eps=EPS, off=0):
    t = tmp_rows
    s = slice(off, off + n)
    vv = p.vop(lambda E: E.tensor_scalar_add(t[0:1, s], var_row[0:1, s], float(eps)))
    p.wait("scalar", p.vs, vv)
    p.sop(lambda E: E.activation(t[0:1, s], t[0:1, s], AF.Ln))
    return p.sop(lambda E: E.activation(out_row[0:1, s], t[0:1, s],
                                        AF.Exp, scale=-0.5))


def emit_stats_chunk(p, bufs, x_tiles, ch, S_row, SS_row, x_ready_val):
    x16a, x16b, sq16, ones16 = bufs["x16a"], bufs["x16b"], bufs["sq16a"], bufs["ones16"]
    sq16b = bufs["sq16b"]
    ps_S, ps_SS = bufs["ps_S"], bufs["ps_SS"]
    p.wait("vector", p.qs, x_ready_val)
    tens_done = {}
    for dt in range(DT):
        x16 = x16a if dt % 2 == 0 else x16b
        sq = sq16 if dt % 2 == 0 else sq16b
        if dt >= 2:
            p.wait("vector", p.ts, tens_done[dt - 2])
        vv = p.op("vector", lambda E, x16=x16, dt=dt: E.tensor_copy(
            x16[:, :], x_tiles[dt]), inc=p.vs)
        p.wait("scalar", p.vs, vv)
        sv = p.op("scalar", lambda E, x16=x16, sq=sq: E.activation(
            sq[:, :], x16[:, :], AF.Square), inc=p.ss)
        p.wait("tensor", p.ss, sv)
        tv = p.op("tensor", lambda E, x16=x16, sq=sq, dt=dt: (
            E.matmul(ps_S[0:1, 0:TC], ones16[:, :], x16[:, :],
                     start=(dt == 0), stop=(dt == DT - 1), skip_group_check=True),
            E.matmul(ps_SS[0:1, 0:TC], ones16[:, :], sq[:, :],
                     start=(dt == 0), stop=(dt == DT - 1), skip_group_check=True))[-1],
            inc=p.ts)
        tens_done[dt] = tv
    p.wait("vector", p.ts, tens_done[DT - 1])
    p.op("vector", lambda E: E.tensor_copy(S_row[0:1, ch * TC:(ch + 1) * TC],
                                           ps_S[0:1, 0:TC]))
    return p.op("vector", lambda E: E.tensor_copy(SS_row[0:1, ch * TC:(ch + 1) * TC],
                                                  ps_SS[0:1, 0:TC]), inc=p.vs)


def emit_rowmath_composite(p, rb, with_inner, n=TOK, off=0):
    S, SS, A, B, r, mur, t0, t1 = (rb["S"], rb["SS"], rb["A"], rb["B"],
                                   rb["r"], rb["mur"], rb["t0"], rb["t1"])
    sl = lambda x: x[0:1, off:off + n]
    p.vop(lambda E: E.tensor_scalar_mul(sl(t0), sl(S), INVD))
    p.vop(lambda E: E.tensor_scalar_mul(sl(t1), sl(SS), INVD))
    p.vop(lambda E: E.tensor_mul(sl(B), sl(t0), sl(t0)))
    vv = p.vop(lambda E: E.tensor_sub(sl(t1), sl(t1), sl(B)))
    p.wait("scalar", p.vs, vv)
    sv = emit_row_rstd(p, rb["t2"], t1, r, n=n, off=off)
    p.wait("vector", p.ss, sv)
    p.vop(lambda E: E.tensor_mul(sl(mur), sl(t0), sl(r)))
    if not with_inner:
        p.vop(lambda E: E.tensor_copy(sl(A), sl(r)))
        return p.vop(lambda E: E.tensor_copy(sl(B), sl(mur)))
    p.vop(lambda E: E.tensor_mul(sl(t0), sl(r), sl(r)))
    vv = p.vop(lambda E: E.tensor_mul(sl(t0), sl(t1), sl(t0)))
    p.wait("scalar", p.vs, vv)
    sv = emit_row_rstd(p, rb["t2"], t0, t1, n=n, off=off)
    p.wait("vector", p.ss, sv)
    p.vop(lambda E: E.tensor_mul(sl(A), sl(r), sl(t1)))
    p.vop(lambda E: E.tensor_scalar_mul(sl(t0), sl(S), INVD))
    return p.vop(lambda E: E.tensor_mul(sl(B), sl(t0), sl(A)))



def build_launch1():
    nc = bass.Bass("TRN2", target_bir_lowering=False)
    x_t = nc.dram_tensor("x_t", [D, TOK], F32, kind="ExternalInput").ap()
    lng = nc.dram_tensor("lng_cols", [128, DT], F32, kind="ExternalInput").ap()
    lnb = nc.dram_tensor("lnb_cols", [128, DT], F32, kind="ExternalInput").ap()
    win = nc.dram_tensor("win_arr", [128, DT * R], F32, kind="ExternalInput").ap()
    binc = nc.dram_tensor("b_in_col", [R, 1], F32, kind="ExternalInput").ap()
    u_hand = nc.dram_tensor("u_hand", [R, TOK], F32, kind="ExternalOutput").ap()
    rows1 = nc.dram_tensor("rows1", [4, TOK], F32, kind="ExternalOutput").ap()

    ctx = ExitStack()
    p = Prog(nc, ctx)
    b = {}
    b["xch_a"] = sbuf(ctx, nc, "xch_a", [128, DT * TC], F32)
    b["xch_b"] = sbuf(ctx, nc, "xch_b", [128, DT * TC], F32)
    b["x16a"] = sbuf(ctx, nc, "x16a", [128, TC], BF16)
    b["x16b"] = sbuf(ctx, nc, "x16b", [128, TC], BF16)
    b["sq16a"] = sbuf(ctx, nc, "sq16a", [128, TC], BF16)
    b["sq16b"] = sbuf(ctx, nc, "sq16b", [128, TC], BF16)
    b["ones16"] = sbuf(ctx, nc, "ones16", [128, 1], BF16)
    ones1f = sbuf(ctx, nc, "ones1f", [1, 128], F32)
    lng_c = sbuf(ctx, nc, "lng_c", [128, DT], F32)
    lnb_c = sbuf(ctx, nc, "lnb_c", [128, DT], F32)
    win16 = sbuf(ctx, nc, "win16", [128, DT * R], BF16)
    bin_c = sbuf(ctx, nc, "bin_c", [R, 1], F32)
    rb = {k: sbuf(ctx, nc, "row_" + k, [1, TOK], F32)
          for k in ("S", "SS", "A", "B", "r", "mur", "t0", "t1", "t2")}
    tmp1 = sbuf(ctx, nc, "tmp1", [128, TC], F32)
    xnp = sbuf(ctx, nc, "xnp", [128, TC], F32)
    xn16 = sbuf(ctx, nc, "xn16", [128, TC], BF16)
    u_sb = sbuf(ctx, nc, "u_sb", [R, TOK], F32)
    b["ps_S"] = psum(ctx, nc, "ps_S", [1, TC])
    b["ps_SS"] = psum(ctx, nc, "ps_SS", [1, TC])
    ps_b1 = psum(ctx, nc, "ps_b1", [128, TC])
    ps_b2 = psum(ctx, nc, "ps_b2", [128, TC])
    ps_u = psum(ctx, nc, "ps_u", [R, TC])

    p.dma("sync", lng_c[:, :], lng)
    p.dma("sync", lnb_c[:, :], lnb)
    p.dma("gpsimd", win16[:, :], win)
    cv = p.dma("sync", bin_c[:, :], binc)
    gv = p.qg.n
    iv = p.op("vector", lambda E: E.memset(b["ones16"][:, :], 1.0))
    iv = p.op("vector", lambda E: E.memset(ones1f[:, :], 1.0), inc=p.vs)

    last_ev = None
    for ch in range(NCH):
        xch = b["xch_a"] if ch % 2 == 0 else b["xch_b"]
        if ch >= 2 and last_ev is not None:
            pass
        if ch >= 2:
            p.wait("sync", p.vs, stats_done[ch - 2])
        for dt in range(DT):
            p.dma("sync", xch[:, dt * TC:(dt + 1) * TC],
                  x_t[dt * 128:(dt + 1) * 128, ch * TC:(ch + 1) * TC])
        x_tiles = [xch[:, dt * TC:(dt + 1) * TC] for dt in range(DT)]
        ev = emit_stats_chunk(p, b, x_tiles, ch, rb["S"], rb["SS"], p.qs.n)
        if ch == 0:
            stats_done = {}
        stats_done[ch] = ev
    rm = emit_rowmath_composite(p, rb, with_inner=True)

    p.wait("tensor", p.qg, gv)
    p.wait("scalar", p.qs, cv)
    for ch in range(NCH):
        xch = b["xch_a"] if ch % 2 == 0 else b["xch_b"]
        if ch < 2:
            p.wait("sync", p.vs, rm)
        else:
            p.wait("sync", p.vs, xn_done[ch - 2])
        for dt in range(DT):
            p.dma("sync", xch[:, dt * TC:(dt + 1) * TC],
                  x_t[dt * 128:(dt + 1) * 128, ch * TC:(ch + 1) * TC])
        ld = p.qs.n
        p.wait("tensor", p.vs, rm)
        if ch >= 1:
            p.wait("tensor", p.vs, bc_used[ch - 1])
        tv = p.op("tensor", lambda E, ch=ch: (
            E.matmul(ps_b1[:, :], ones1f[:, :], rb["A"][0:1, ch * TC:(ch + 1) * TC],
                     start=True, stop=True),
            E.matmul(ps_b2[:, :], ones1f[:, :], rb["B"][0:1, ch * TC:(ch + 1) * TC],
                     start=True, stop=True))[-1], inc=p.ts)
        p.wait("vector", p.ts, tv)
        p.wait("vector", p.qs, ld)
        for dt in range(DT):
            vv = p.op("vector", lambda E, dt=dt, xch=xch: E.tensor_mul(
                tmp1[:, :], xch[:, dt * TC:(dt + 1) * TC], ps_b1[:, :]))
            vv = p.op("vector", lambda E: E.tensor_sub(xnp[:, :], tmp1[:, :],
                                                       ps_b2[:, :]), inc=p.vs)
            p.wait("scalar", p.vs, vv)
            sv = p.op("scalar", lambda E, dt=dt: E.activation(
                xn16[:, :], xnp[:, :], AF.Identity,
                bias=lnb_c[:, dt:dt + 1], scale=lng_c[:, dt:dt + 1]), inc=p.ss)
            p.wait("tensor", p.ss, sv)
            tv2 = p.op("tensor", lambda E, dt=dt: E.matmul(
                ps_u[:, :], win16[:, dt * R:(dt + 1) * R], xn16[:, :],
                start=(dt == 0), stop=(dt == DT - 1)), inc=p.ts)
            p.wait("vector", p.ts, tv2)
        uv = p.op("vector", lambda E, ch=ch: E.tensor_scalar(
            u_sb[:, ch * TC:(ch + 1) * TC], ps_u[:, :], bin_c[:, :], None,
            ALU.add), inc=p.vs)
        if ch == 0:
            xn_done, bc_used = {}, {}
        xn_done[ch] = uv
        bc_used[ch] = uv
    p.wait("sync", p.vs, xn_done[NCH - 1])
    p.dma("sync", u_hand, u_sb[:, :])
    p.dma("sync", rows1[0:1, :], rb["A"][0:1, :])
    p.dma("sync", rows1[1:2, :], rb["B"][0:1, :])
    p.dma("sync", rows1[2:3, :], rb["r"][0:1, :])
    fin = p.dma("sync", rows1[3:4, :], rb["mur"][0:1, :])
    p.wait("sync", p.qs, fin)

    with ctx, nc.Block() as block:
        p.finalize(block)
    return nc



def build_launch2(upto=5):
    nc = bass.Bass("TRN2", target_bir_lowering=False)
    ap = lambda n, s, d=F32, k="ExternalInput": nc.dram_tensor(n, s, d, kind=k).ap()
    x_t = ap("x_t", [D, TOK])
    rows1 = ap("rows1", [4, TOK])
    u_pad = ap("u_pad", [R, SEQ])
    dec_col = ap("dec_col", [R, 1])
    woutb = ap("woutb", [R + 1, D])
    d_cols = ap("d_cols", [128, DT])
    lng = ap("lng_cols", [128, DT])
    lnb = ap("lnb_cols", [128, DT])
    wdn = ap("wdn_arr", [128, DT * 32])
    bdn = ap("bdn_col", [32, 1])
    wupb = ap("wupb", [33, D])
    l0w = ap("l0w_arr", [D, D])
    g0 = ap("g0_cols", [128, DT])
    be0 = ap("be0_cols", [128, DT])
    lnw = ap("lnw_cols", [128, DT])
    lnbb = ap("lnbb_cols", [128, DT])
    w1a = ap("w1_arr", [D, HID])
    w2a = ap("w2_arr", [D, HID])
    w3a = ap("w3_arr", [HID, D])
    aqa = ap("aq_arr", [D, D])
    aka = ap("ak_arr", [128, D])
    ava = ap("av_arr", [128, D])
    bq = ap("bq_cols", [128, DT])
    bk = ap("bk_col", [128, 1])
    bv = ap("bv_col", [128, 1])
    idt = ap("idt32", [128, 128])

    x2h = ap("x2_hand", [D, TOK], F32, "ExternalOutput")
    qh = ap("q_hand", [D, TOK], BF16, "ExternalOutput")
    kh = ap("k_hand", [128, TOK], BF16, "ExternalOutput")
    vh = ap("v_hand", [128, TOK], BF16, "ExternalOutput")
    w1qo = ap("w1q", [D, HID], BF16, "ExternalOutput")
    w2qo = ap("w2q", [D, HID], BF16, "ExternalOutput")
    w3qo = ap("w3q", [HID, D], BF16, "ExternalOutput")
    x1s = ap("x1_scr", [D, TOK], F32, "Internal")
    sgs = ap("sgn_scr", [D, D], BF16, "Internal")

    ctx = ExitStack()
    p = Prog(nc, ctx)

    b = {}
    b["x16a"] = sbuf(ctx, nc, "x16a", [128, TC], BF16)
    b["x16b"] = sbuf(ctx, nc, "x16b", [128, TC], BF16)
    b["sq16a"] = sbuf(ctx, nc, "sq16a", [128, TC], BF16)
    b["sq16b"] = sbuf(ctx, nc, "sq16b", [128, TC], BF16)
    b["ones16"] = sbuf(ctx, nc, "ones16", [128, 1], BF16)
    ones1f = sbuf(ctx, nc, "ones1f", [1, 128], F32)
    rb = {k: sbuf(ctx, nc, "row_" + k, [1, TC], F32)
          for k in ("S", "SS", "A", "B", "r", "mur", "t0", "t1", "t2")}
    r1rows = sbuf(ctx, nc, "r1rows", [1, 4 * TC], F32)
    tmp1 = sbuf(ctx, nc, "tmp1", [128, TC], F32)
    tmp2 = sbuf(ctx, nc, "tmp2", [128, TC], F32)
    tmp3 = sbuf(ctx, nc, "tmp3", [128, TC], F32)
    cols = {}
    for nm, src in (("d", d_cols), ("lng", lng), ("lnb", lnb), ("g0", g0),
                    ("be0", be0), ("lnw", lnw), ("lnbb", lnbb), ("bq", bq)):
        cols[nm] = sbuf(ctx, nc, "c_" + nm, [128, DT], F32)
        p.dma("sync", cols[nm][:, :], {"d": d_cols, "lng": lng, "lnb": lnb,
                                       "g0": g0, "be0": be0, "lnw": lnw,
                                       "lnbb": lnbb, "bq": bq}[nm])
    bk_c = sbuf(ctx, nc, "bk_c", [128, 1], F32)
    bv_c = sbuf(ctx, nc, "bv_c", [128, 1], F32)
    bdn_c = sbuf(ctx, nc, "bdn_c", [32, 1], F32)
    p.dma("sync", bk_c[:, :], bk)
    p.dma("sync", bv_c[:, :], bv)
    p.dma("sync", bdn_c[:, :], bdn)
    woutb16 = sbuf(ctx, nc, "woutb16", [R + 1, D], BF16)
    wdn16 = sbuf(ctx, nc, "wdn16", [128, DT * 32], BF16)
    wupb16 = sbuf(ctx, nc, "wupb16", [33, D], BF16)
    idt32 = sbuf(ctx, nc, "idt32_sb", [128, 128], F32)
    idt16 = sbuf(ctx, nc, "idt16_sb", [128, 128], BF16)
    ak16 = sbuf(ctx, nc, "ak16", [128, D], BF16)
    av16 = sbuf(ctx, nc, "av16", [128, D], BF16)
    p.dma("gpsimd", woutb16[:, :], woutb)
    p.dma("gpsimd", wdn16[:, :], wdn)
    p.dma("gpsimd", wupb16[:, :], wupb)
    p.dma("sync", idt32[:, :], idt)
    p.dma("gpsimd", idt16[:, :], idt)
    p.dma("gpsimd", ak16[:, :], aka)
    cg0 = p.dma("gpsimd", av16[:, :], ava)
    h_ext16 = sbuf(ctx, nc, "h_ext16", [R + 1, TOK], BF16)
    hf_ext16 = sbuf(ctx, nc, "hf_ext16", [33, TC], BF16)
    srow = sbuf(ctx, nc, "srow", [1, 16], F32)
    s0col = sbuf(ctx, nc, "s0col", [128, 1], F32)
    sg0c = sbuf(ctx, nc, "sg0c", [128, DT], F32)
    thp_c = sbuf(ctx, nc, "thp_c", [128, 1], F32)
    thn_c = sbuf(ctx, nc, "thn_c", [128, 1], F32)
    mxacc = sbuf(ctx, nc, "mxacc", [128, 1], F32)
    mxt = sbuf(ctx, nc, "mxt", [128, 1], F32)
    gacc = sbuf(ctx, nc, "gacc", [128, 1], F32)
    c0 = p.qs.n

    b["ps_S"] = psum(ctx, nc, "ps_S", [1, TC])
    b["ps_SS"] = psum(ctx, nc, "ps_SS", [1, TC])
    ps_b1 = psum(ctx, nc, "ps_b1", [128, TC])
    ps_b2 = psum(ctx, nc, "ps_b2", [128, TC])
    ps_b3 = psum(ctx, nc, "ps_b3", [128, TC])
    ps_b4 = psum(ctx, nc, "ps_b4", [128, TC])
    ps_ar = psum(ctx, nc, "ps_ar", [128, TC])
    ps_hf = psum(ctx, nc, "ps_hf", [128, TC])

    p.op("vector", lambda E: E.memset(b["ones16"][:, :], 1.0))
    p.op("vector", lambda E: E.memset(ones1f[:, :], 1.0))
    p.op("vector", lambda E: E.memset(h_ext16[:, :], 1.0))
    mz = p.op("vector", lambda E: E.memset(hf_ext16[:, :], 1.0), inc=p.vs)

    sctx = ExitStack()
    u_sb = sbuf(sctx, nc, "u_sb", [R, SEQ], F32)
    a_b = sbuf(sctx, nc, "a_b", [R, SEQ], F32)
    h_pad = sbuf(sctx, nc, "h_pad", [R, SEQ], F32)
    dcol = sbuf(sctx, nc, "dcol", [R, 1], F32)
    p.dma("sync", dcol[:, :], dec_col)
    lu = p.dma("sync", u_sb[:, :], u_pad)
    p.wait("scalar", p.qs, lu)
    sv = p.op("scalar", lambda E: E.activation(a_b[:, 0:1], dcol[:, :], AF.Exp,
                                               scale=-1.0), inc=p.ss)
    p.wait("vector", p.ss, sv)
    p.vop(lambda E: E.tensor_scalar_add(a_b[:, 0:1], a_b[:, 0:1], 1.0))
    p.vop(lambda E: E.reciprocal(dcol[:, :], a_b[:, 0:1]))
    p.vop(lambda E: E.memset(a_b[:, :], 1.0))
    p.vop(lambda E: E.tensor_scalar_mul(a_b[:, :], a_b[:, :], dcol[:, :]))
    p.wait("vector", p.qs, lu)
    p.op("vector", lambda E: E.tensor_tensor_scan(
        h_pad[:, :], a_b[:, :], u_sb[:, :], 0.0, ALU.mult, ALU.add))
    scan_done = p.op("vector", lambda E: E.tensor_copy(
        h_ext16[0:R, :], h_pad[0:R, SEQ - TOK:SEQ]), inc=p.vs)
    sctx.close()

    pctx = ExitStack()
    wst_a = sbuf(pctx, nc, "wst_a", [128, D], F32)
    wst_b = sbuf(pctx, nc, "wst_b", [128, D], F32)
    sg16 = sbuf(pctx, nc, "sg16", [128, D], BF16)
    p.op("vector", lambda E: E.memset(mxacc[:, :], 0.0))
    sg_writes = {}
    for do in range(DT):
        wst = wst_a if do % 2 == 0 else wst_b
        if do < 2:
            p.wait("sync", p.vs, scan_done)
        else:
            p.wait("sync", p.vs, mx_done[do - 2])
        ldv = p.dma("sync", wst[:, :], l0w[do * 128:(do + 1) * 128, :])
        p.wait("scalar", p.qs, ldv)
        if do >= 1:
            p.wait("scalar", p.qg, sg_writes[do - 1])
        sv = p.op("scalar", lambda E, wst=wst: E.sign(sg16[:, :], wst[:, :]), inc=p.ss)
        p.wait("gpsimd", p.ss, sv)
        sg_writes[do] = p.dma("gpsimd", sgs[do * 128:(do + 1) * 128, :], sg16[:, :])
        p.wait("vector", p.ss, sv)
        p.vop(lambda E, wst=wst: E.tensor_reduce(
            mxt[:, :], wst[:, :], AX.X, ALU.max, apply_absolute_value=True))
        vv = p.vop(lambda E: E.tensor_max(mxacc[:, :], mxacc[:, :], mxt[:, :]))
        if do == 0:
            mx_done = {}
        mx_done[do] = vv
    p.wait("tensor", p.vs, mx_done[DT - 1])
    p.wait("tensor", p.qs, c0)
    tv = p.op("tensor", lambda E: E.matmul(b["ps_S"][0:1, 0:128], mxacc[:, :],
                                           idt32[:, :], start=True, stop=True),
              inc=p.ts)
    p.wait("vector", p.ts, tv)
    vv = p.vop(lambda E: E.tensor_reduce(
        srow[0:1, 0:1], b["ps_S"][0:1, 0:128], AX.X, ALU.max))
    p.wait("tensor", p.vs, vv)
    tv = p.op("tensor", lambda E: E.matmul(ps_b1[:, 0:1], ones1f[:, :],
                                           srow[0:1, 0:1], start=True, stop=True),
              inc=p.ts)
    p.wait("vector", p.ts, tv)
    p.vop(lambda E: E.tensor_copy(s0col[:, :], ps_b1[:, 0:1]))
    prep_done = p.vop(lambda E: E.tensor_scalar_mul(
        sg0c[:, :], cols["g0"][:, :], s0col[:, :]))
    pctx.close()

    p.wait("sync", p.vs, prep_done)
    if upto <= 1:
        p.wait("sync", p.qs, p.qs.n)
        p.wait("gpsimd", p.qg, p.qg.n)
        with ctx, nc.Block() as block:
            p.finalize(block)
        return nc

    mctx = ExitStack()
    xch_a = sbuf(mctx, nc, "xch_a", [128, DT * TC], F32)
    xch_b = sbuf(mctx, nc, "xch_b", [128, DT * TC], F32)
    xmid = sbuf(mctx, nc, "xmid", [128, DT * TC], F32)
    xnb = sbuf(mctx, nc, "xnb", [128, DT * TC], BF16)
    st_a = sbuf(mctx, nc, "st_a", [128, D], F32)
    st_b = sbuf(mctx, nc, "st_b", [128, D], F32)
    sb16a = sbuf(mctx, nc, "sb16a", [128, D], BF16)
    sb16b = sbuf(mctx, nc, "sb16b", [128, D], BF16)
    q16o = sbuf(mctx, nc, "q16o", [128, TC], BF16)
    q16ob = sbuf(mctx, nc, "q16ob", [128, TC], BF16)
    q16op = [q16o, q16ob]
    vtmp = sbuf(mctx, nc, "vtmp", [128, TOK], BF16)

    def load_xchunk(ch, src, guard=None):
        xch = xch_a if ch % 2 == 0 else xch_b
        if guard is not None:
            p.wait("sync", *guard)
        for dt in range(DT):
            v = p.dma("sync", xch[:, dt * TC:(dt + 1) * TC],
                      src[dt * 128:(dt + 1) * 128, ch * TC:(ch + 1) * TC])
        return xch, v

    def store_xchunk(ch, xch, dst, ready_ctr, ready_val):
        p.wait("sync", ready_ctr, ready_val)
        for dt in range(DT):
            v = p.dma("sync", dst[dt * 128:(dt + 1) * 128, ch * TC:(ch + 1) * TC],
                      xch[:, dt * TC:(dt + 1) * TC])
        return v

    def bcasts(ch_rows, n=TC):
        A_ap, B_ap = ch_rows
        return p.op("tensor", lambda E: (
            E.matmul(ps_b1[:, 0:n], ones1f[:, :], A_ap, start=True, stop=True),
            E.matmul(ps_b2[:, 0:n], ones1f[:, :], B_ap, start=True, stop=True))[-1],
            inc=p.ts)

    sign_last_ss = p.ss.n
    sign_last_qg = p.qg.n
    p.wait("tensor", p.vs, scan_done)
    p.wait("tensor", p.vs, prep_done)
    p.wait("tensor", p.qg, cg0)
    ps_work = ps_ar
    ps_workb = ps_hf
    work = [ps_work, ps_workb]
    work_free = [0, 0]
    sg_used = {}
    q_used = {}
    prev_bcast_tv = 0
    prev_chunk_done = None
    x16p = [b["x16a"], b["x16b"]]
    sq16p = [b["sq16a"], b["sq16b"]]
    st_free = [0, 0]
    if upto <= 1:
        p.wait("sync", p.qs, p.qs.n)
        p.wait("gpsimd", p.qg, p.qg.n)
        mctx_dummy = None
        with ctx, nc.Block() as block:
            p.finalize(block)
        return nc
    for ch in range(NCH):
        if ch == 0:
            p.wait("sync", p.vs, prep_done)
            p.wait("sync", p.ss, sign_last_ss)
            p.wait("sync", p.qg, sign_last_qg)
        xch, xld = load_xchunk(ch, x_t,
                               guard=(p.vs, prev_chunk_done) if prev_chunk_done else None)
        if ch > 0:
            p.wait("sync", p.ts, prev_bcast_tv)
        for i in range(4):
            rl = p.dma("sync", r1rows[0:1, i * TC:(i + 1) * TC],
                       rows1[i:i + 1, ch * TC:(ch + 1) * TC])
        p.wait("tensor", p.qs, rl)
        if ch > 0:
            p.wait("tensor", p.vs, prev_chunk_done)
        tv = p.op("tensor", lambda E: (
            E.matmul(ps_b1[:, :], ones1f[:, :], r1rows[0:1, 0:TC], start=True, stop=True),
            E.matmul(ps_b2[:, :], ones1f[:, :], r1rows[0:1, TC:2 * TC], start=True, stop=True),
            E.matmul(ps_b3[:, :], ones1f[:, :], r1rows[0:1, 2 * TC:3 * TC], start=True, stop=True))[-1],
            inc=p.ts)
        prev_bcast_tv = tv
        p.wait("vector", p.qs, rl)
        nm = p.vop(lambda E: E.tensor_scalar_mul(
            r1rows[0:1, 3 * TC:4 * TC], r1rows[0:1, 3 * TC:4 * TC], -1.0))
        p.wait("tensor", p.vs, nm)
        p.wait("vector", p.ts, tv)
        p.wait("vector", p.qs, xld)
        p.wait("scalar", p.qs, xld)
        stv = {}
        for dt in range(DT):
            par = dt % 2
            xt = xch[:, dt * TC:(dt + 1) * TC]
            p.wait("tensor", p.vs, work_free[par])
            tv2 = p.op("tensor", lambda E, dt=dt, ch=ch, par=par: (
                E.matmul(work[par][:, 0:TC], woutb16[:, dt * 128:(dt + 1) * 128],
                         h_ext16[:, ch * TC:(ch + 1) * TC], start=True, stop=False,
                         skip_group_check=True),
                E.matmul(work[par][:, 0:TC], ones1f[:, :],
                         r1rows[0:1, 3 * TC:4 * TC], start=False, stop=True,
                         skip_group_check=True))[-1], inc=p.ts)
            p.op("vector", lambda E, xt=xt: E.tensor_mul(tmp1[:, :], xt, ps_b1[:, :]))
            vv = p.op("vector", lambda E: E.tensor_sub(tmp2[:, :], tmp1[:, :],
                                                       ps_b2[:, :]), inc=p.vs)
            p.wait("scalar", p.vs, vv)
            sv = p.op("scalar", lambda E, dt=dt: E.activation(
                tmp3[:, :], tmp2[:, :], AF.Identity,
                bias=cols["lnb"][:, dt:dt + 1], scale=cols["lng"][:, dt:dt + 1]),
                inc=p.ss)
            p.op("vector", lambda E, xt=xt: E.tensor_mul(tmp1[:, :], xt, ps_b3[:, :]))
            p.wait("vector", p.ts, tv2)
            p.wait("vector", p.ss, sv)
            wv = p.op("vector", lambda E, dt=dt, par=par: E.scalar_tensor_tensor(
                tmp2[:, :], tmp3[:, :], cols["d"][:, dt:dt + 1], work[par][:, 0:TC],
                ALU.mult, ALU.add), inc=p.vs)
            work_free[par] = wv
            xmv = p.op("vector", lambda E, dt=dt: E.tensor_add(
                xmid[:, dt * TC:(dt + 1) * TC], tmp1[:, :], tmp2[:, :]), inc=p.vs)
            p.wait("scalar", p.vs, xmv)
            p.wait("scalar", p.ts, st_free[par])
            p.op("scalar", lambda E, dt=dt, par=par: E.copy(
                x16p[par][:, :], xmid[:, dt * TC:(dt + 1) * TC]))
            sqv = p.op("scalar", lambda E, par=par: E.activation(
                sq16p[par][:, :], x16p[par][:, :], AF.Square), inc=p.ss)
            p.wait("tensor", p.ss, sqv)
            stv[dt] = p.op("tensor", lambda E, dt=dt, par=par: (
                E.matmul(b["ps_S"][0:1, :], b["ones16"][:, :], x16p[par][:, :],
                         start=(dt == 0), stop=(dt == DT - 1), skip_group_check=True),
                E.matmul(b["ps_SS"][0:1, :], b["ones16"][:, :], sq16p[par][:, :],
                         start=(dt == 0), stop=(dt == DT - 1), skip_group_check=True))[-1],
                inc=p.ts)
            st_free[par] = stv[dt]
        p.wait("vector", p.ts, stv[DT - 1])
        p.op("vector", lambda E: E.tensor_copy(rb["S"][0:1, :], b["ps_S"][0:1, :]))
        p.op("vector", lambda E: E.tensor_copy(rb["SS"][0:1, :], b["ps_SS"][0:1, :]))
        rm = emit_rowmath_composite(p, rb, with_inner=False, n=TC, off=0)
        p.wait("tensor", p.vs, rm)
        tv = bcasts((rb["A"][0:1, :], rb["B"][0:1, :]))
        p.wait("vector", p.ts, tv)
        p.wait("scalar", p.ts, tv)
        hfv = {}
        for dt in range(DT):
            p.op("vector", lambda E, dt=dt: E.tensor_mul(
                tmp1[:, :], xmid[:, dt * TC:(dt + 1) * TC], ps_b1[:, :]))
            vv = p.op("vector", lambda E: E.tensor_sub(tmp2[:, :], tmp1[:, :],
                                                       ps_b2[:, :]), inc=p.vs)
            p.wait("scalar", p.vs, vv)
            sv = p.op("scalar", lambda E, dt=dt: E.activation(
                xnb[:, dt * TC:(dt + 1) * TC], tmp2[:, :], AF.Identity,
                bias=cols["lnb"][:, dt:dt + 1], scale=cols["lng"][:, dt:dt + 1]),
                inc=p.ss)
            p.wait("tensor", p.ss, sv)
            if dt == 0:
                p.wait("tensor", p.vs, work_free[1])
            hfv[dt] = p.op("tensor", lambda E, dt=dt: E.matmul(
                ps_workb[0:32, 0:TC], wdn16[:, dt * 32:(dt + 1) * 32],
                xnb[:, dt * TC:(dt + 1) * TC],
                start=(dt == 0), stop=(dt == DT - 1)), inc=p.ts)
        p.wait("vector", p.ts, hfv[DT - 1])
        hv = p.op("vector", lambda E: E.tensor_scalar(
            hf_ext16[0:32, 0:TC], ps_workb[0:32, 0:TC], bdn_c[:, :], 0.0,
            ALU.add, ALU.max), inc=p.vs)
        work_free[1] = hv
        p.wait("tensor", p.vs, hv)
        stv = {}
        for dt in range(DT):
            par = dt % 2
            p.wait("tensor", p.vs, work_free[par])
            tv2 = p.op("tensor", lambda E, dt=dt, par=par: E.matmul(
                work[par][:, 0:TC], wupb16[:, dt * 128:(dt + 1) * 128],
                hf_ext16[:, 0:TC], start=True, stop=True), inc=p.ts)
            p.op("vector", lambda E, dt=dt, xch=xch: E.tensor_add(
                tmp1[:, :], xch[:, dt * TC:(dt + 1) * TC],
                xmid[:, dt * TC:(dt + 1) * TC]))
            p.wait("vector", p.ts, tv2)
            xv = p.op("vector", lambda E, dt=dt, xch=xch, par=par: E.tensor_add(
                xch[:, dt * TC:(dt + 1) * TC], tmp1[:, :], work[par][:, 0:TC]),
                inc=p.vs)
            work_free[par] = xv
            p.wait("scalar", p.vs, xv)
            p.wait("scalar", p.ts, st_free[par])
            p.op("scalar", lambda E, dt=dt, par=par, xch=xch: E.copy(
                x16p[par][:, :], xch[:, dt * TC:(dt + 1) * TC]))
            sqv = p.op("scalar", lambda E, par=par: E.activation(
                sq16p[par][:, :], x16p[par][:, :], AF.Square), inc=p.ss)
            p.wait("tensor", p.ss, sqv)
            stv[dt] = p.op("tensor", lambda E, dt=dt, par=par: (
                E.matmul(b["ps_S"][0:1, :], b["ones16"][:, :], x16p[par][:, :],
                         start=(dt == 0), stop=(dt == DT - 1), skip_group_check=True),
                E.matmul(b["ps_SS"][0:1, :], b["ones16"][:, :], sq16p[par][:, :],
                         start=(dt == 0), stop=(dt == DT - 1), skip_group_check=True))[-1],
                inc=p.ts)
            st_free[par] = stv[dt]
        p.wait("vector", p.ts, stv[DT - 1])
        p.op("vector", lambda E: E.tensor_copy(rb["S"][0:1, :], b["ps_S"][0:1, :]))
        p.op("vector", lambda E: E.tensor_copy(rb["SS"][0:1, :], b["ps_SS"][0:1, :]))
        rm = emit_rowmath_composite(p, rb, with_inner=True, n=TC, off=0)
        p.wait("tensor", p.vs, rm)
        tv = bcasts((rb["A"][0:1, :], rb["B"][0:1, :]))
        p.wait("vector", p.ts, tv)
        p.wait("scalar", p.ts, tv)
        for dt in range(DT):
            p.op("vector", lambda E, dt=dt, xch=xch: E.tensor_mul(
                tmp1[:, :], xch[:, dt * TC:(dt + 1) * TC], ps_b1[:, :]))
            vv = p.op("vector", lambda E: E.tensor_sub(tmp2[:, :], tmp1[:, :],
                                                       ps_b2[:, :]), inc=p.vs)
            p.wait("scalar", p.vs, vv)
            sv = p.op("scalar", lambda E, dt=dt: E.activation(
                xnb[:, dt * TC:(dt + 1) * TC], tmp2[:, :], AF.Identity,
                bias=cols["lnbb"][:, dt:dt + 1], scale=cols["lnw"][:, dt:dt + 1]),
                inc=p.ss)
        xn_ready = sv
        p.wait("tensor", p.ss, xn_ready)
        stv = {}
        for do in range(DT):
            par = do % 2
            sb = sb16a if par == 0 else sb16b
            if (ch, do) != (0, 0) and (do >= 2 or ch > 0):
                key = (ch, do - 2) if do >= 2 else (ch - 1, DT - 2 + do)
                p.wait("sync", p.ts, sg_used[key])
            lv = p.dma("sync", sb[:, :], sgs[do * 128:(do + 1) * 128, :])
            p.wait("tensor", p.qs, lv)
            p.wait("tensor", p.vs, work_free[par])
            tv2 = None
            for dt in range(DT):
                tv2 = p.op("tensor", lambda E, dt=dt, sb=sb, par=par: E.matmul(
                    work[par][:, 0:TC], sb[:, dt * 128:(dt + 1) * 128],
                    xnb[:, dt * TC:(dt + 1) * TC],
                    start=(dt == 0), stop=(dt == DT - 1)),
                    inc=p.ts if dt == DT - 1 else None)
            sg_used[(ch, do)] = tv2
            p.wait("vector", p.ts, tv2)
            p.op("vector", lambda E, do=do, par=par: E.tensor_scalar(
                tmp1[:, :], work[par][:, 0:TC], sg0c[:, do:do + 1],
                cols["be0"][:, do:do + 1], ALU.mult, ALU.add))
            xv = p.op("vector", lambda E, do=do, xch=xch: E.tensor_add(
                xch[:, do * TC:(do + 1) * TC], xch[:, do * TC:(do + 1) * TC],
                tmp1[:, :]), inc=p.vs)
            work_free[par] = xv
            p.wait("scalar", p.vs, xv)
            p.wait("scalar", p.ts, st_free[par])
            p.op("scalar", lambda E, do=do, par=par, xch=xch: E.copy(
                x16p[par][:, :], xch[:, do * TC:(do + 1) * TC]))
            sqv = p.op("scalar", lambda E, par=par: E.activation(
                sq16p[par][:, :], x16p[par][:, :], AF.Square), inc=p.ss)
            p.wait("tensor", p.ss, sqv)
            stv[do] = p.op("tensor", lambda E, do=do, par=par: (
                E.matmul(b["ps_S"][0:1, :], b["ones16"][:, :], x16p[par][:, :],
                         start=(do == 0), stop=(do == DT - 1), skip_group_check=True),
                E.matmul(b["ps_SS"][0:1, :], b["ones16"][:, :], sq16p[par][:, :],
                         start=(do == 0), stop=(do == DT - 1), skip_group_check=True))[-1],
                inc=p.ts)
            st_free[par] = stv[do]
        sv2 = store_xchunk(ch, xch, x2h, p.vs, work_free[(DT - 1) % 2])
        p.wait("vector", p.ts, stv[DT - 1])
        p.op("vector", lambda E: E.tensor_copy(rb["S"][0:1, :], b["ps_S"][0:1, :]))
        p.op("vector", lambda E: E.tensor_copy(rb["SS"][0:1, :], b["ps_SS"][0:1, :]))
        rm = emit_rowmath_composite(p, rb, with_inner=False, n=TC, off=0)
        p.wait("tensor", p.vs, rm)
        tv = bcasts((rb["A"][0:1, :], rb["B"][0:1, :]))
        p.wait("vector", p.ts, tv)
        for dt in range(DT):
            p.op("vector", lambda E, dt=dt, xch=xch: E.tensor_mul(
                tmp1[:, :], xch[:, dt * TC:(dt + 1) * TC], ps_b1[:, :]))
            vv = p.op("vector", lambda E, dt=dt: E.tensor_sub(
                xnb[:, dt * TC:(dt + 1) * TC], tmp1[:, :], ps_b2[:, :]), inc=p.vs)
        xn_ready = vv
        for do in range(DT):
            par = do % 2
            sb = sb16a if par == 0 else sb16b
            if do >= 2:
                p.wait("gpsimd", p.ts, q_used[(ch, do - 2)])
            else:
                p.wait("gpsimd", p.ts, sg_used[(ch, DT - 2 + do)])
            lv = p.dma("gpsimd", sb[:, :], aqa[do * 128:(do + 1) * 128, :])
            p.wait("tensor", p.qg, lv)
            p.wait("tensor", p.vs, xn_ready)
            p.wait("tensor", p.vs, work_free[par])
            tv2 = None
            for dt in range(DT):
                tv2 = p.op("tensor", lambda E, dt=dt, sb=sb, par=par: E.matmul(
                    work[par][:, 0:TC], sb[:, dt * 128:(dt + 1) * 128],
                    xnb[:, dt * TC:(dt + 1) * TC],
                    start=(dt == 0), stop=(dt == DT - 1)),
                    inc=p.ts if dt == DT - 1 else None)
            q_used[(ch, do)] = tv2
            p.wait("vector", p.ts, tv2)
            if ch > 0 or do >= 2:
                p.wait("vector", p.qs, q_st.get((ch, do - 2) if do >= 2 else
                                                (ch - 1, DT - 2 + do), 0))
            qv = p.op("vector", lambda E, do=do, par=par: E.tensor_scalar(
                q16op[par][:, :], work[par][:, 0:TC], cols["bq"][:, do:do + 1], None,
                ALU.add), inc=p.vs)
            work_free[par] = qv
            p.wait("sync", p.vs, qv)
            if ch == 0 and do == 0:
                q_st = {}
            q_st[(ch, do)] = p.dma("sync", qh[do * 128:(do + 1) * 128,
                                              ch * TC:(ch + 1) * TC], q16op[par][:, :])
        p.wait("tensor", p.qg, cg0)
        for (a16w, bcol, outh) in ((ak16, bk_c, kh), (av16, bv_c, None)):
            par = 0 if outh is not None else 1
            p.wait("tensor", p.vs, work_free[par])
            tv2 = None
            for dt in range(DT):
                tv2 = p.op("tensor", lambda E, dt=dt, a16w=a16w, par=par: E.matmul(
                    work[par][:, 0:TC], a16w[:, dt * 128:(dt + 1) * 128],
                    xnb[:, dt * TC:(dt + 1) * TC],
                    start=(dt == 0), stop=(dt == DT - 1)),
                    inc=p.ts if dt == DT - 1 else None)
            p.wait("vector", p.ts, tv2)
            if outh is not None:
                p.wait("vector", p.qs, q_st[(ch, DT - 1)])
                qv = p.op("vector", lambda E, bcol=bcol, par=par: E.tensor_scalar(
                    q16op[par][:, :], work[par][:, 0:TC], bcol[:, :], None, ALU.add),
                    inc=p.vs)
                work_free[par] = qv
                p.wait("sync", p.vs, qv)
                q_st[(ch, DT)] = p.dma("sync", outh[:, ch * TC:(ch + 1) * TC],
                                       q16op[par][:, :])
            else:
                qv = p.op("vector", lambda E, bcol=bcol, ch=ch, par=par: E.tensor_scalar(
                    vtmp[:, ch * TC:(ch + 1) * TC], work[par][:, 0:TC], bcol[:, :],
                    None, ALU.add), inc=p.vs)
                work_free[par] = qv
        prev_chunk_done = qv
    prev_done2 = prev_chunk_done

    p.wait("tensor", p.vs, prev_done2)
    vt_prev = None
    for ktl in range(DT):
        if ktl > 0:
            p.wait("tensor", p.vs, vv)
        tv = p.op("tensor", lambda E, ktl=ktl: E.matmul(
            ps_ar[:, 0:128], vtmp[:, ktl * 128:(ktl + 1) * 128], idt16[:, :],
            start=True, stop=True), inc=p.ts)
        p.wait("vector", p.ts, tv)
        if vt_prev is None:
            p.wait("vector", p.qs, q_st[(NCH - 1, DT)])
        else:
            p.wait("vector", p.qg, vt_prev)
        vv = p.op("vector", lambda E: E.tensor_copy(q16o[:, 0:128], ps_ar[:, 0:128]),
                  inc=p.vs)
        p.wait("gpsimd", p.vs, vv)
        vt_prev = p.dma("gpsimd", vh[:, ktl * 128:(ktl + 1) * 128], q16o[:, 0:128])

    if upto <= 4:
        p.wait("sync", p.qs, p.qs.n)
        p.wait("gpsimd", p.qg, p.qg.n)
        mctx.close()
        with ctx, nc.Block() as block:
            p.finalize(block)
        return nc

    mctx.close()
    quant_guard_ts = p.ts.n
    quant_guard_vs = p.vs.n
    quant_guard_qg = p.qg.n
    p.wait("sync", p.ts, quant_guard_ts)
    p.wait("sync", p.vs, quant_guard_vs)
    p.wait("sync", p.qg, quant_guard_qg)
    qctx = ExitStack()
    HH = HID // 2
    qst_a = sbuf(qctx, nc, "qst_a", [128, HH], F32)
    qst_b = sbuf(qctx, nc, "qst_b", [128, HH], F32)
    qa_t = sbuf(qctx, nc, "qa_t", [128, HH], F32)
    qb_t = sbuf(qctx, nc, "qb_t", [128, HH], F32)
    q16h = sbuf(qctx, nc, "q16h", [128, HH], BF16)
    for (wa, wq, nstrip, wid) in ((w1a, w1qo, DT, HID), (w2a, w2qo, DT, HID),
                                  (w3a, w3qo, NHT, D)):
        nh = 2 if wid == HID else 1
        seg = wid // nh
        if wa is not w1a:
            p.wait("sync", p.vs, p.vs.n)
        p.op("vector", lambda E: E.memset(gacc[:, :], 0.0))
        g_done = {}
        for i in range(nstrip * nh):
            qst = qst_a if i % 2 == 0 else qst_b
            dt_i, h_i = divmod(i, nh)
            if i >= 2:
                p.wait("sync", p.vs, g_done[i - 2])
            lv = p.dma("sync", qst[:, 0:seg],
                       wa[dt_i * 128:(dt_i + 1) * 128, h_i * seg:(h_i + 1) * seg])
            p.wait("vector", p.qs, lv)
            p.vop(lambda E, qst=qst, seg=seg: E.tensor_reduce(
                mxt[:, :], qst[:, 0:seg], AX.X, ALU.add, apply_absolute_value=True))
            gv = p.vop(lambda E: E.tensor_add(gacc[:, :], gacc[:, :], mxt[:, :]))
            g_done[i] = gv
        p.wait("tensor", p.vs, g_done[nstrip * nh - 1])
        tv = p.op("tensor", lambda E: E.matmul(b["ps_S"][0:1, 0:128], gacc[:, :],
                                               idt32[:, :], start=True, stop=True),
                  inc=p.ts)
        p.wait("vector", p.ts, tv)
        p.vop(lambda E: E.tensor_reduce(srow[0:1, 1:2],
                                        b["ps_S"][0:1, 0:128], AX.X, ALU.add))
        nelem = float(D * HID)
        p.vop(lambda E: E.tensor_scalar(
            srow[0:1, 1:2], srow[0:1, 1:2], 0.5 / nelem, 0.5e-8, ALU.mult, ALU.add))
        vv = p.vop(lambda E: E.tensor_scalar_mul(
            srow[0:1, 2:3], srow[0:1, 1:2], -1.0))
        p.wait("tensor", p.vs, vv)
        tv = p.op("tensor", lambda E: (
            E.matmul(ps_b1[:, 0:1], ones1f[:, :], srow[0:1, 1:2], start=True, stop=True),
            E.matmul(ps_b2[:, 0:1], ones1f[:, :], srow[0:1, 2:3], start=True, stop=True))[-1],
            inc=p.ts)
        p.wait("vector", p.ts, tv)
        p.vop(lambda E: E.tensor_copy(thp_c[:, :], ps_b1[:, 0:1]))
        thv = p.vop(lambda E: E.tensor_copy(thn_c[:, :], ps_b2[:, 0:1]))
        qq_done = {}
        for i in range(nstrip * nh):
            qst = qst_a if i % 2 == 0 else qst_b
            dt_i, h_i = divmod(i, nh)
            if i >= 2:
                p.wait("sync", p.vs, qq_done[i - 2])
            lv = p.dma("sync", qst[:, 0:seg],
                       wa[dt_i * 128:(dt_i + 1) * 128, h_i * seg:(h_i + 1) * seg])
            p.wait("vector", p.qs, lv)
            p.wait("vector", p.vs, thv)
            av = p.op("vector", lambda E, qst=qst, seg=seg: E.tensor_single_scalar(
                qa_t[:, 0:seg], qst[:, 0:seg], thp_c[:, :], ALU.is_gt), inc=p.vs)
            bv_ = p.op("vector", lambda E, qst=qst, seg=seg: E.tensor_single_scalar(
                qb_t[:, 0:seg], qst[:, 0:seg], thn_c[:, :], ALU.is_lt), inc=p.vs)
            if i >= 1:
                p.wait("vector", p.qg, qw_prev)
            gv = p.op("vector", lambda E, seg=seg: E.tensor_sub(
                q16h[:, 0:seg], qa_t[:, 0:seg], qb_t[:, 0:seg]), inc=p.vs)
            qq_done[i] = gv
            p.wait("gpsimd", p.vs, gv)
            qw_prev = p.dma("gpsimd", wq[dt_i * 128:(dt_i + 1) * 128,
                                         h_i * seg:(h_i + 1) * seg], q16h[:, 0:seg])
    p.wait("gpsimd", p.qg, p.qg.n)
    p.wait("sync", p.qs, p.qs.n)
    qctx.close()

    with ctx, nc.Block() as block:
        p.finalize(block)
    return nc



def _cols(v):
    return np.ascontiguousarray(v.reshape(DT, 128).T).astype(np.float32)


def _strips(w_t, m_tiles):
    Kd, M = w_t.shape
    kt = Kd // 128
    a = w_t.reshape(kt, 128, m_tiles, 128)
    a = a.transpose(2, 1, 0, 3).reshape(m_tiles * 128, kt * 128)
    return np.ascontiguousarray(a).astype(np.float32)


def _stage_common(inp):
    st = {}
    st["lng_cols"] = _cols(inp["L0_attn_ln_gamma"])
    st["lnb_cols"] = _cols(inp["L0_attn_ln_beta"])
    st["win_arr"] = np.ascontiguousarray(
        inp["L0_attn_W_in"].reshape(DT, 128, R).transpose(1, 0, 2)
        .reshape(128, DT * R)).astype(np.float32)
    st["b_in_col"] = inp["L0_attn_b_in"].reshape(R, 1).astype(np.float32)
    st["dec_col"] = inp["L0_attn_decay_logit"].reshape(R, 1).astype(np.float32)
    st["woutb"] = np.ascontiguousarray(np.concatenate(
        [inp["L0_attn_W_out"], inp["L0_attn_b_out"][None, :]], 0)).astype(np.float32)
    st["d_cols"] = _cols(inp["L0_attn_D_skip"])
    st["wdn_arr"] = np.ascontiguousarray(
        inp["L0_attn_W_down"].reshape(DT, 128, 32).transpose(1, 0, 2)
        .reshape(128, DT * 32)).astype(np.float32)
    st["bdn_col"] = inp["L0_attn_b_down"].reshape(32, 1).astype(np.float32)
    st["wupb"] = np.ascontiguousarray(np.concatenate(
        [inp["L0_attn_W_up"], inp["L0_attn_b_up"][None, :]], 0)).astype(np.float32)
    st["l0w_arr"] = _strips(np.ascontiguousarray(inp["L0_ffn_weight"].T), DT)
    st["g0_cols"] = _cols(inp["L0_ffn_gamma"])
    st["be0_cols"] = _cols(inp["L0_ffn_beta"])
    st["lnw_cols"] = _cols(inp["L0_ffn_ln_weight"])
    st["lnbb_cols"] = _cols(inp["L0_ffn_ln_bias"])
    st["w1_arr"] = np.ascontiguousarray(
        inp["L1_ffn_w1"].T.reshape(DT, 128, HID).reshape(D, HID)).astype(np.float32)
    st["w2_arr"] = np.ascontiguousarray(inp["L1_ffn_w2"].T).astype(np.float32)
    st["w3_arr"] = np.ascontiguousarray(inp["L1_ffn_w3"].T).astype(np.float32)
    st["aq_arr"] = _strips(np.ascontiguousarray(inp["L1_attn_Wq"].T), DT)
    st["ak_arr"] = np.ascontiguousarray(
        inp["L1_attn_Wk"].reshape(128, DT, 128).transpose(2, 1, 0)
        .reshape(128, D)).astype(np.float32)
    st["av_arr"] = np.ascontiguousarray(
        inp["L1_attn_Wv"].reshape(128, DT, 128).transpose(2, 1, 0)
        .reshape(128, D)).astype(np.float32)
    st["bq_cols"] = _cols(inp["L1_attn_bq"])
    st["bk_col"] = inp["L1_attn_bk"].reshape(128, 1).astype(np.float32)
    st["bv_col"] = inp["L1_attn_bv"].reshape(128, 1).astype(np.float32)
    st["idt32"] = np.eye(128, dtype=np.float32)
    st["ao_arr"] = _strips(np.ascontiguousarray(inp["L1_attn_Wo"].T), DT)
    st["bo_cols"] = _cols(inp["L1_attn_bo"])
    return st


def _in1(st, x_t_c):
    return dict(x_t=x_t_c, lng_cols=st["lng_cols"], lnb_cols=st["lnb_cols"],
                win_arr=st["win_arr"], b_in_col=st["b_in_col"])


def _in2(st, x_t_c, rows1_c, u_pad_c):
    keys = ["dec_col", "woutb", "d_cols", "lng_cols", "lnb_cols", "wdn_arr",
            "bdn_col", "wupb", "l0w_arr", "g0_cols", "be0_cols", "lnw_cols",
            "lnbb_cols", "w1_arr", "w2_arr", "w3_arr", "aq_arr", "ak_arr",
            "av_arr", "bq_cols", "bk_col", "bv_col", "idt32"]
    d = {k: st[k] for k in keys}
    d["x_t"] = x_t_c
    d["rows1"] = rows1_c
    d["u_pad"] = u_pad_c
    return d


def _masks_lb(chunks):
    sg = np.arange(SEQ, dtype=np.int32)[:, None]
    parts = []
    for s, gc in enumerate(chunks):
        qg = gc * TC + np.arange(TC, dtype=np.int32)[None, :]
        M = (sg < qg).reshape(NKT, 128, TC)[:NKTQ[s]]
        parts.append(M.reshape(NKTQ[s] * 128, TC))
    return np.ascontiguousarray(
        np.concatenate(parts, 0)).astype(ml_dtypes.bfloat16)


def _in3(st, c, x2c, qc_, kfull, vfull):
    return dict(x2_hand=x2c, q_own=qc_, k_full=kfull, v_full=vfull,
                masks=st["masks"][c], ao_arr=st["ao_arr"], bo_cols=st["bo_cols"],
                w1q=st["_w1q"], w2q=st["_w2q"], w3q=st["_w3q"])


def run_launches12(inp, cores=None):
    if cores is None:
        cores = list(range(8))
    x = inp["L0_attn_x"].astype(np.float32)
    st = _stage_common(inp)
    x_t = []
    for c in cores:
        bidx, c4 = c // 4, c % 4
        x_t.append(np.ascontiguousarray(x[bidx, c4 * TOK:(c4 + 1) * TOK, :].T))

    nc1 = build_launch1()
    in1 = [_in1(st, x_t[c]) for c in cores]
    r1 = run_bass_kernel_spmd(nc1, in1, core_ids=cores).results

    u_full = np.zeros((2, R, SEQ), np.float32)
    for c in cores:
        bidx, c4 = c // 4, c % 4
        u_full[bidx, :, c4 * TOK:(c4 + 1) * TOK] = r1[c]["u_hand"]

    nc2 = build_launch2()
    in2 = []
    for c in cores:
        bidx, c4 = c // 4, c % 4
        off = c4 * TOK
        u_pad = np.zeros((R, SEQ), np.float32)
        u_pad[:, SEQ - (off + TOK):] = u_full[bidx][:, :off + TOK]
        in2.append(_in2(st, x_t[c], r1[c]["rows1"], np.ascontiguousarray(u_pad)))
    r2 = run_bass_kernel_spmd(nc2, in2, core_ids=cores).results
    return st, x_t, r2


def _gchunks(c):
    return [15 - c // 2, 11 - c // 2, 7 - c // 2, 3 - c // 2]


def kernel(**inp):
    inp = {k: np.asarray(v) for k, v in inp.items()}
    cores = list(range(8))
    st, x_t, r2 = run_launches12(inp, cores)
    st["masks"] = [_masks_lb(_gchunks(c)) for c in cores]
    kfull, vfull, x2full, qfull = {}, {}, {}, {}
    for bi in range(2):
        kk = np.zeros((128, SEQ), ml_dtypes.bfloat16)
        vv = np.zeros((128, SEQ), ml_dtypes.bfloat16)
        for c4 in range(4):
            c = bi * 4 + c4
            kk[:, c4 * TOK:(c4 + 1) * TOK] = r2[c]["k_hand"]
            vv[:, c4 * TOK:(c4 + 1) * TOK] = r2[c]["v_hand"]
        kfull[bi] = kk
        vfull[bi] = vv
        x2full[bi] = np.concatenate(
            [r2[bi * 4 + c4]["x2_hand"] for c4 in range(4)], 1)
        qfull[bi] = np.concatenate(
            [r2[bi * 4 + c4]["q_hand"] for c4 in range(4)], 1)
    nc3 = build_launch3()
    in3 = []
    for c in cores:
        bi = c % 2
        G = _gchunks(c)
        x2c = np.ascontiguousarray(np.concatenate(
            [x2full[bi][:, gc * TC:(gc + 1) * TC] for gc in G], 1))
        qc_ = np.ascontiguousarray(np.concatenate(
            [qfull[bi][:, gc * TC:(gc + 1) * TC] for gc in G], 1))
        st["_w1q"], st["_w2q"], st["_w3q"] = (r2[c]["w1q"], r2[c]["w2q"],
                                              r2[c]["w3q"])
        in3.append(_in3(st, c, x2c, qc_, kfull[bi], vfull[bi]))
    r3 = run_bass_kernel_spmd(nc3, in3, core_ids=cores).results
    out = np.zeros((2, SEQ, D), np.float32)
    for c in cores:
        bi = c % 2
        for s, gc in enumerate(_gchunks(c)):
            out[bi, gc * TC:(gc + 1) * TC, :] = \
                r3[c]["out_t"][:, s * TC:(s + 1) * TC].T
    return out


# version 08
# speedup vs baseline: 1.4985x; 1.1181x over previous
import numpy as np
import ml_dtypes
from contextlib import ExitStack
import concourse.bass as bass
import concourse.mybir as mybir
from concourse.bass_utils import run_bass_kernel_spmd

F32 = mybir.dt.float32
BF16 = mybir.dt.bfloat16
AF = mybir.ActivationFunctionType
ALU = mybir.AluOpType
AX = mybir.AxisListType

D = 2048
DT = 16
TOK = 2048
TC = 512
NCH = TOK // TC
SEQ = 8192
R = 8
EPS = 1e-5
HID = 5632
NHT = HID // 128
NHC = 11
NKT = 64
NKTQ = [64, 48, 32, 16]
MOFF = [0, 64, 112, 144]
MTOT = 160
HSH = HID // 8
DSH = D // 8
SCALE_ATTN = float(1.0 / np.sqrt(128.0))
INVD = float(1.0 / D)


class Ctr:

    def __init__(self, ctx, nc, name):
        self.h = ctx.enter_context(nc.semaphore(name))
        self.n = 0

    def bump(self, k):
        self.n += k
        return self.n


class Prog:

    ENGINES = ("sync", "gpsimd", "tensor", "scalar", "vector")

    def __init__(self, nc, ctx):
        self.nc = nc
        self.ctx = ctx
        self.cmds = {e: [] for e in self.ENGINES}
        self.qs = Ctr(ctx, nc, "qs")
        self.qg = Ctr(ctx, nc, "qg")
        self.ts = Ctr(ctx, nc, "ts")
        self.ss = Ctr(ctx, nc, "ss")
        self.vs = Ctr(ctx, nc, "vs")
        self.gs = Ctr(ctx, nc, "gs")

    def dma(self, eng, dst, src):
        q = self.qs if eng == "sync" else self.qg
        v = q.bump(16)
        self.cmds[eng].append(lambda E: E.dma_start(out=dst, in_=src).then_inc(q.h, 16))
        return v

    def wait(self, eng, ctr, val):
        self.cmds[eng].append(lambda E: E.wait_ge(ctr.h, val))

    def op(self, eng, f, inc=None):
        if inc is None:
            self.cmds[eng].append(lambda E: f(E))
            return None
        v = inc.bump(1)
        self.cmds[eng].append(lambda E: f(E).then_inc(inc.h, 1))
        return v

    def vop(self, f):
        v = self.op("vector", f, inc=self.vs)
        self.wait("vector", self.vs, v)
        return v

    def sop(self, f):
        v = self.op("scalar", f, inc=self.ss)
        self.wait("scalar", self.ss, v)
        return v

    def finalize(self, block):
        for name in self.ENGINES:
            cl = self.cmds[name]
            def body(E, cl=cl):
                for f in cl:
                    f(E)
            getattr(block, name)(body)


def sbuf(ctx, nc, name, shape, dt=F32):
    return ctx.enter_context(nc.sbuf_tensor(name, shape, dt))


def psum(ctx, nc, name, shape, dt=F32):
    return ctx.enter_context(nc.psum_tensor(name, shape, dt))



def build_launch3():
    nc = bass.Bass("TRN2", target_bir_lowering=False)
    ap = lambda n, s, d=F32, k="ExternalInput": nc.dram_tensor(n, s, d, kind=k).ap()
    x2h = ap("x2_hand", [D, TOK])
    qo = ap("q_own", [D, TOK], BF16)
    kf_d = ap("k_full", [128, SEQ], BF16)
    vf_d = ap("v_full", [128, SEQ], BF16)
    masks = ap("masks", [MTOT * 128, TC], BF16)
    aoa = ap("ao_arr", [D, D])
    bo = ap("bo_cols", [128, DT])
    w1qh = ap("w1q", [D, HID], BF16)
    w2qh = ap("w2q", [D, HID], BF16)
    w3qh = ap("w3q", [HID, D], BF16)
    out_t = ap("out_t", [D, TOK], F32, "ExternalOutput")

    ctx = ExitStack()
    p = Prog(nc, ctx)
    b = {}
    b["x16a"] = sbuf(ctx, nc, "x16a", [128, TC], BF16)
    b["x16b"] = sbuf(ctx, nc, "x16b", [128, TC], BF16)
    b["sq16a"] = sbuf(ctx, nc, "sq16a", [128, TC], BF16)
    b["sq16b"] = sbuf(ctx, nc, "sq16b", [128, TC], BF16)
    b["ones16"] = sbuf(ctx, nc, "ones16", [128, 1], BF16)
    ones1f = sbuf(ctx, nc, "ones1f", [1, 128], F32)
    rb = {k: sbuf(ctx, nc, "row_" + k, [1, TC], F32)
          for k in ("S", "SS", "A", "B", "r", "mur", "t0", "t1", "t2")}
    tmp1 = sbuf(ctx, nc, "tmp1", [128, TC], F32)
    kf = sbuf(ctx, nc, "kf", [128, SEQ], BF16)
    vf = sbuf(ctx, nc, "vf", [128, SEQ], BF16)
    qp_a = sbuf(ctx, nc, "qp_a", [128, 2 * TC], BF16)
    qp_b = qp_a
    AO = sbuf(ctx, nc, "AO", [128, DT * TC], BF16)
    acc = sbuf(ctx, nc, "acc", [128, DT * TC], F32)
    xnb4 = sbuf(ctx, nc, "xnb4", [128, DT * TC], BF16)
    rs_pair = xnb4[:, 14 * TC:16 * TC]
    uni = sbuf(ctx, nc, "uni", [128, NKT * TC], BF16)
    w1s = uni[:, 0:DT * TC]
    w2s = uni[:, DT * TC:2 * DT * TC]
    w3s = uni[:, 2 * DT * TC:2 * DT * TC + 4 * D]
    exp_sb = sbuf(ctx, nc, "exp_sb", [128, 4 * TC], BF16)
    rs_row = sbuf(ctx, nc, "rs_row", [1, 2 * TC], F32)
    rs_rec = sbuf(ctx, nc, "rs_rec", [1, 2 * TC], F32)
    rcb = tmp1
    sb3a = sbuf(ctx, nc, "sb3a", [128, D], BF16)
    sb3b = sbuf(ctx, nc, "sb3b", [128, D], BF16)
    bo_c = sbuf(ctx, nc, "bo_c", [128, DT], F32)
    hid_sb = exp_sb
    sg16 = b["x16a"]

    ps_av = psum(ctx, nc, "ps_av", [128, 2 * TC])
    ps_rs = psum(ctx, nc, "ps_rs", [1, TC])
    ps_sc = psum(ctx, nc, "ps_sc", [128, 4 * TC])
    ps_bc = psum(ctx, nc, "ps_bc", [128, TC])
    b["ps_S"] = ps_bc
    b["ps_SS"] = ps_av

    p.dma("sync", bo_c[:, :], bo)
    p.dma("sync", kf[:, :], kf_d)
    kvl = p.dma("sync", vf[:, :], vf_d)
    mz = p.op("vector", lambda E: (E.memset(b["ones16"][:, :], 1.0),
                                   E.memset(ones1f[:, :], 1.0))[-1], inc=p.vs)

    prev_h_done = mz
    prev_exp = 0
    rsp_last = {0: 0, 1: 0}
    prev_ao_st = {}
    last_acc_store = 0
    prev_ffn_vs = mz
    prev_w3_use = 0
    prev_sg = 0
    last_scores = {}
    for qc in range(NCH):
        nktq = NKTQ[qc]
        mask_q = {}
        kt_order = (list(range(48, nktq)) + list(range(0, min(nktq, 32)))
                    + list(range(32, min(nktq, 48))))
        for kt in kt_order:
            if qc > 0:
                if kt >= 48:
                    p.wait("gpsimd", p.ts, prev_w3b_use)
                elif kt < 32:
                    p.wait("gpsimd", p.ts, prev_w12_use)
                else:
                    p.wait("gpsimd", p.ts, prev_w3_use)
            mask_q[kt] = p.dma("gpsimd",
                               uni[:, kt * TC:(kt + 1) * TC],
                               masks[(MOFF[qc] + kt) * 128:
                                     (MOFF[qc] + kt) * 128 + 128, :])
        if qc > 0:
            p.wait("sync", p.vs, prev_ffn_vs)
        for dt in range(DT):
            x2l = p.dma("sync", acc[:, dt * TC:(dt + 1) * TC],
                        x2h[dt * 128:(dt + 1) * 128, qc * TC:(qc + 1) * TC])
        p.wait("tensor", p.qg, max(mask_q.values()))
        p.wait("tensor", p.qs, kvl)
        exp_done = {}
        mask_done = {}
        pair_done = {}
        rs_mm = {}
        av_done = {}
        epi_av = {}
        epi_rs = {}
        for h in range(DT):
            qb = qp_a[:, 0:TC] if h % 2 == 0 else qp_a[:, TC:2 * TC]
            ps_avh = ps_av[:, 0:TC] if h % 2 == 0 else ps_av[:, TC:2 * TC]
            if (qc, h) != (0, 0) and (h >= 2 or qc > 0):
                key = (qc, h - 2) if h >= 2 else (qc - 1, DT - 2 + h)
                p.wait("sync", p.ts, last_scores[key])
            qldv = p.dma("sync", qb,
                         qo[h * 128:(h + 1) * 128, qc * TC:(qc + 1) * TC])
            p.wait("tensor", p.qs, qldv)
            NG = nktq // 2
            for i in range(NG):
                par = i % 2
                scs = ps_sc[:, par * 2 * TC:(par + 1) * 2 * TC]
                esl = exp_sb[:, par * 2 * TC:(par + 1) * 2 * TC]
                if i >= 2:
                    p.wait("tensor", p.ss, exp_done[(h, i - 2)])
                elif h > 0:
                    p.wait("tensor", p.ss, exp_done[(h - 1, NG - 2 + i)])
                elif qc > 0 and prev_exp:
                    p.wait("tensor", p.ss, prev_exp)
                if h == 0 and i < 2:
                    p.wait("tensor", p.vs, prev_h_done)
                    if qc > 0:
                        p.wait("tensor", p.vs, prev_ffn_vs)
                tv = None
                for j in range(2):
                    kt = i * 2 + j
                    tv = p.op("tensor", lambda E, j=j, kt=kt, par=par, qb=qb: E.matmul(
                        ps_sc[:, (par * 2 + j) * TC:(par * 2 + j + 1) * TC],
                        kf[:, kt * 128:(kt + 1) * 128], qb,
                        start=True, stop=True), inc=p.ts if j == 1 else None)
                if i == NG - 1:
                    last_scores[(qc, h)] = tv
                if i >= 1:
                    ip = i - 1
                    pp = ip % 2
                    p.wait("tensor", p.vs, pair_done[(h, ip)])
                    if ip == 0 and h >= 2:
                        p.wait("tensor", p.vs, epi_av[h - 2])
                    if ip == 0 and h >= 1:
                        p.wait("tensor", p.vs, epi_rs[h - 1])
                    for j in range(2):
                        kt = ip * 2 + j
                        p.op("tensor", lambda E, j=j, kt=kt, pp=pp, ip=ip,
                             ps_avh=ps_avh: E.matmul(
                            ps_avh, vf[:, kt * 128:(kt + 1) * 128],
                            exp_sb[:, (pp * 2 + j) * TC:(pp * 2 + j + 1) * TC],
                            start=(ip == 0 and j == 0), stop=False,
                            skip_group_check=True))
                    av = p.op("tensor", lambda E, pp=pp, ip=ip, NG=NG: E.matmul(
                        ps_rs[:, :], b["ones16"][:, :],
                        rs_pair[:, pp * TC:(pp + 1) * TC],
                        start=(ip == 0), stop=False,
                        skip_group_check=True), inc=p.ts)
                    rs_mm[(h, ip)] = av
                    av_done[(h, ip)] = av
                p.wait("scalar", p.ts, tv)
                if i >= 2:
                    p.wait("scalar", p.ts, av_done[(h, i - 2)])
                sv = p.op("scalar", lambda E, scs=scs, esl=esl: E.activation(
                    esl, scs, AF.Exp, scale=SCALE_ATTN), inc=p.ss)
                exp_done[(h, i)] = sv
                p.wait("vector", p.ss, sv)
                if h == 0:
                    p.wait("vector", p.qg, mask_q[i * 2 + 1])
                mv2 = p.op("vector", lambda E, i=i, esl=esl: E.tensor_mul(
                    esl, esl, uni[:, i * 2 * TC:(i + 1) * 2 * TC]), inc=p.vs)
                mask_done[(h, i)] = mv2
                p.wait("vector", p.ts, rsp_last[par])
                pv = p.op("vector", lambda E, par=par: E.tensor_add(
                    rs_pair[:, par * TC:(par + 1) * TC],
                    exp_sb[:, (par * 2) * TC:(par * 2 + 1) * TC],
                    exp_sb[:, (par * 2 + 1) * TC:(par * 2 + 2) * TC]), inc=p.vs)
                pair_done[(h, i)] = pv
            ip = NG - 1
            pp = ip % 2
            p.wait("tensor", p.vs, pair_done[(h, ip)])
            for j in range(2):
                kt = ip * 2 + j
                p.op("tensor", lambda E, j=j, kt=kt, pp=pp, ps_avh=ps_avh: E.matmul(
                    ps_avh, vf[:, kt * 128:(kt + 1) * 128],
                    exp_sb[:, (pp * 2 + j) * TC:(pp * 2 + j + 1) * TC],
                    start=False, stop=(j == 1), skip_group_check=True))
            av = p.op("tensor", lambda E, pp=pp, ip=ip: E.matmul(
                ps_rs[:, :], b["ones16"][:, :],
                rs_pair[:, pp * TC:(pp + 1) * TC],
                start=False, stop=True, skip_group_check=True), inc=p.ts)
            rs_mm[(h, ip)] = av
            rsp_last[0] = rs_mm[(h, NG - 2)]
            rsp_last[1] = rs_mm[(h, NG - 1)]
            av_done[(h, ip)] = av
            p.wait("vector", p.ts, av)
            erv = p.vop(lambda E: E.tensor_single_scalar(
                rs_row[0:1, 0:TC], ps_rs[0:1, :], 1e-30, ALU.max))
            epi_rs[h] = erv
            rv = p.vop(lambda E: E.reciprocal(rs_rec[0:1, 0:TC], rs_row[0:1, 0:TC]))
            p.wait("tensor", p.vs, rv)
            if h >= 1 or qc > 0:
                p.wait("tensor", p.vs, prev_h_done)
            tv = p.op("tensor", lambda E: E.matmul(
                ps_bc[:, :], ones1f[:, :], rs_rec[0:1, 0:TC], start=True, stop=True),
                inc=p.ts)
            p.wait("vector", p.ts, tv)
            p.vop(lambda E: E.tensor_copy(rcb[:, :], ps_bc[:, :]))
            hv = p.op("vector", lambda E, h=h, ps_avh=ps_avh: E.tensor_mul(
                AO[:, h * TC:(h + 1) * TC], ps_avh, rcb[:, :]), inc=p.vs)
            epi_av[h] = hv
            prev_h_done = hv
        for do in range(DT):
            sb = sb3a if do % 2 == 0 else sb3b
            gkey = (qc, do - 2) if do >= 2 else ((qc - 1, DT - 2 + do) if qc > 0 else None)
            if gkey is not None:
                p.wait("gpsimd", p.ts, prev_ao_st.get(gkey, 0))
            lv = p.dma("gpsimd", sb[:, :], aoa[do * 128:(do + 1) * 128, :])
            p.wait("tensor", p.qg, lv)
            p.wait("tensor", p.vs, prev_h_done)
            if do > 0 or qc > 0:
                p.wait("tensor", p.vs, prev_proj_v)
            tv = None
            for hh in range(DT):
                tv = p.op("tensor", lambda E, hh=hh, sb=sb: E.matmul(
                    ps_sc[:, 0:TC], sb[:, hh * 128:(hh + 1) * 128],
                    AO[:, hh * TC:(hh + 1) * TC],
                    start=(hh == 0), stop=(hh == DT - 1)),
                    inc=p.ts if hh == DT - 1 else None)
            prev_ao_st[(qc, do)] = tv
            p.wait("vector", p.ts, tv)
            p.wait("vector", p.qs, x2l)
            p.op("vector", lambda E, do=do: E.tensor_scalar(
                tmp1[:, :], ps_sc[:, 0:TC], bo_c[:, do:do + 1], None, ALU.add))
            xv = p.op("vector", lambda E, do=do: E.tensor_add(
                acc[:, do * TC:(do + 1) * TC], acc[:, do * TC:(do + 1) * TC],
                tmp1[:, :]), inc=p.vs)
            prev_proj_v = xv
        x3_done = prev_proj_v
        prev_exp = p.ss.n

        x_tiles = [acc[:, dt * TC:(dt + 1) * TC] for dt in range(DT)]
        p.wait("vector", p.vs, x3_done)
        ev = emit_stats_chunk(p, b, x_tiles, 0, rb["S"], rb["SS"], p.qs.n)
        rm = emit_rowmath_composite(p, rb, with_inner=False, n=TC, off=0)
        p.wait("tensor", p.vs, rm)
        bt = p.op("tensor", lambda E: (
            E.matmul(ps_bc[:, :], ones1f[:, :], rb["A"][0:1, :], start=True, stop=True),
            E.matmul(ps_av[:, 0:TC], ones1f[:, :], rb["B"][0:1, :], start=True, stop=True))[-1],
            inc=p.ts)
        p.wait("vector", p.ts, bt)
        for dt in range(DT):
            p.op("vector", lambda E, dt=dt: E.tensor_mul(
                tmp1[:, :], acc[:, dt * TC:(dt + 1) * TC], ps_bc[:, :]))
            vv = p.op("vector", lambda E, dt=dt: E.tensor_sub(
                xnb4[:, dt * TC:(dt + 1) * TC], tmp1[:, :], ps_av[:, 0:TC]), inc=p.vs)
        xn_ready = vv
        w3h = [uni[:, 2 * DT * TC:2 * DT * TC + 4 * D],
               uni[:, 2 * DT * TC + 4 * D:2 * DT * TC + 8 * D]]
        ld_done = {}
        lastp = lambda t: pair_done.get((DT - 1, t // 2), 0)
        for dt in range(DT):
            p.wait("sync", p.vs, lastp(dt))
            p.dma("sync", w1s[:, dt * TC:(dt + 1) * TC],
                  w1qh[dt * 128:(dt + 1) * 128, 0:TC])
        for dt in range(DT):
            p.wait("sync", p.vs, lastp(16 + dt))
            wl = p.dma("sync", w2s[:, dt * TC:(dt + 1) * TC],
                       w2qh[dt * 128:(dt + 1) * 128, 0:TC])
        for j in range(4):
            p.wait("sync", p.vs, lastp(35 + 4 * j))
            w3l = p.dma("sync", w3h[0][:, j * D:(j + 1) * D],
                        w3qh[j * 128:j * 128 + 128, :])
        ld_done[0] = max(wl, w3l)
        w3_use = {}
        for hc in range(NHC):
            w3cur = w3h[hc % 2]
            p.wait("tensor", p.qs, ld_done[hc])
            p.wait("tensor", p.vs, xn_ready)
            hid_done = {}
            for ht in range(4):
                if qc > 0 or hc > 0 or ht > 0:
                    p.wait("tensor", p.ss, prev_sg)
                    p.wait("tensor", p.vs, prev_hid)
                t1 = t2 = None
                for dt in range(DT):
                    t1 = p.op("tensor", lambda E, dt=dt, ht=ht: E.matmul(
                        ps_sc[:, 0:TC],
                        w1s[:, dt * TC + ht * 128: dt * TC + (ht + 1) * 128],
                        xnb4[:, dt * TC:(dt + 1) * TC],
                        start=(dt == 0), stop=(dt == DT - 1)),
                        inc=p.ts if dt == DT - 1 else None)
                for dt in range(DT):
                    t2 = p.op("tensor", lambda E, dt=dt, ht=ht: E.matmul(
                        ps_sc[:, TC:2 * TC],
                        w2s[:, dt * TC + ht * 128: dt * TC + (ht + 1) * 128],
                        xnb4[:, dt * TC:(dt + 1) * TC],
                        start=(dt == 0), stop=(dt == DT - 1)),
                        inc=p.ts if dt == DT - 1 else None)
                p.wait("scalar", p.ts, t1)
                sgv = p.op("scalar", lambda E: E.activation(
                    sg16[:, :], ps_sc[:, 0:TC], AF.Sigmoid), inc=p.ss)
                prev_sg = sgv
                p.wait("vector", p.ss, sgv)
                p.wait("vector", p.ts, t2)
                hv2 = p.op("vector", lambda E, ht=ht: E.tensor_mul(
                    hid_sb[:, ht * TC:(ht + 1) * TC], sg16[:, :],
                    ps_sc[:, TC:2 * TC]), inc=p.vs)
                prev_hid = hv2
                hid_done[ht] = hv2
            if hc + 1 < NHC:
                p.wait("sync", p.ts, t2)
                for dt in range(DT):
                    p.dma("sync", w1s[:, dt * TC:(dt + 1) * TC],
                          w1qh[dt * 128:(dt + 1) * 128, (hc + 1) * TC:(hc + 2) * TC])
                for dt in range(DT):
                    wl = p.dma("sync", w2s[:, dt * TC:(dt + 1) * TC],
                               w2qh[dt * 128:(dt + 1) * 128,
                                    (hc + 1) * TC:(hc + 2) * TC])
                if hc >= 1:
                    p.wait("sync", p.ts, w3_use[hc - 1])
                for j in range(4):
                    w3l = p.dma("sync", w3h[(hc + 1) % 2][:, j * D:(j + 1) * D],
                                w3qh[((hc + 1) * 4 + j) * 128:
                                     ((hc + 1) * 4 + j) * 128 + 128, :])
                ld_done[hc + 1] = max(wl, w3l)
            accv = {}
            for do in range(DT):
                ob = 2 + (do % 2)
                p.wait("tensor", p.vs, hid_done[3])
                if do >= 2:
                    p.wait("tensor", p.vs, accv[do - 2])
                elif qc > 0 or hc > 0:
                    p.wait("tensor", p.vs, prev_acc_v2)
                t3 = None
                for j in range(4):
                    t3 = p.op("tensor", lambda E, j=j, do=do, ob=ob, w3cur=w3cur: E.matmul(
                        ps_sc[:, ob * TC:(ob + 1) * TC],
                        w3cur[:, j * D + do * 128: j * D + (do + 1) * 128],
                        hid_sb[:, j * TC:(j + 1) * TC],
                        start=(j == 0), stop=(j == 3)),
                        inc=p.ts if j == 3 else None)
                p.wait("vector", p.ts, t3)
                av2 = p.op("vector", lambda E, do=do, ob=ob: E.tensor_add(
                    acc[:, do * TC:(do + 1) * TC], acc[:, do * TC:(do + 1) * TC],
                    ps_sc[:, ob * TC:(ob + 1) * TC]), inc=p.vs)
                accv[do] = av2
                prev_acc_v = av2
            prev_acc_v2 = accv[DT - 1] if DT >= 2 else prev_acc_v
            prev_acc_v2 = max(accv[DT - 1], accv[DT - 2])
            w3_use[hc] = t3
            prev_w3_use = t3
        prev_w12_use = t2
        prev_w3b_use = w3_use[NHC - 2]
        p.wait("sync", p.vs, prev_acc_v)
        for dt in range(DT):
            last_acc_store = p.dma("sync", out_t[dt * 128:(dt + 1) * 128,
                                                 qc * TC:(qc + 1) * TC],
                                   acc[:, dt * TC:(dt + 1) * TC])
        prev_ffn_vs = prev_acc_v
        prev_exp = p.ss.n

    p.wait("sync", p.qs, p.qs.n)
    p.wait("gpsimd", p.qg, p.qg.n)

    with ctx, nc.Block() as block:
        p.finalize(block)
    return nc



def emit_row_rstd(p, tmp_rows, var_row, out_row, n=TOK, eps=EPS, off=0):
    t = tmp_rows
    s = slice(off, off + n)
    vv = p.vop(lambda E: E.tensor_scalar_add(t[0:1, s], var_row[0:1, s], float(eps)))
    p.wait("scalar", p.vs, vv)
    p.sop(lambda E: E.activation(t[0:1, s], t[0:1, s], AF.Ln))
    return p.sop(lambda E: E.activation(out_row[0:1, s], t[0:1, s],
                                        AF.Exp, scale=-0.5))


def emit_stats_chunk(p, bufs, x_tiles, ch, S_row, SS_row, x_ready_val):
    x16a, x16b, sq16, ones16 = bufs["x16a"], bufs["x16b"], bufs["sq16a"], bufs["ones16"]
    sq16b = bufs["sq16b"]
    ps_S, ps_SS = bufs["ps_S"], bufs["ps_SS"]
    p.wait("vector", p.qs, x_ready_val)
    tens_done = {}
    for dt in range(DT):
        x16 = x16a if dt % 2 == 0 else x16b
        sq = sq16 if dt % 2 == 0 else sq16b
        if dt >= 2:
            p.wait("vector", p.ts, tens_done[dt - 2])
        vv = p.op("vector", lambda E, x16=x16, dt=dt: E.tensor_copy(
            x16[:, :], x_tiles[dt]), inc=p.vs)
        p.wait("scalar", p.vs, vv)
        sv = p.op("scalar", lambda E, x16=x16, sq=sq: E.activation(
            sq[:, :], x16[:, :], AF.Square), inc=p.ss)
        p.wait("tensor", p.ss, sv)
        tv = p.op("tensor", lambda E, x16=x16, sq=sq, dt=dt: (
            E.matmul(ps_S[0:1, 0:TC], ones16[:, :], x16[:, :],
                     start=(dt == 0), stop=(dt == DT - 1), skip_group_check=True),
            E.matmul(ps_SS[0:1, 0:TC], ones16[:, :], sq[:, :],
                     start=(dt == 0), stop=(dt == DT - 1), skip_group_check=True))[-1],
            inc=p.ts)
        tens_done[dt] = tv
    p.wait("vector", p.ts, tens_done[DT - 1])
    p.op("vector", lambda E: E.tensor_copy(S_row[0:1, ch * TC:(ch + 1) * TC],
                                           ps_S[0:1, 0:TC]))
    return p.op("vector", lambda E: E.tensor_copy(SS_row[0:1, ch * TC:(ch + 1) * TC],
                                                  ps_SS[0:1, 0:TC]), inc=p.vs)


def emit_rowmath_composite(p, rb, with_inner, n=TOK, off=0):
    S, SS, A, B, r, mur, t0, t1 = (rb["S"], rb["SS"], rb["A"], rb["B"],
                                   rb["r"], rb["mur"], rb["t0"], rb["t1"])
    sl = lambda x: x[0:1, off:off + n]
    p.vop(lambda E: E.tensor_scalar_mul(sl(t0), sl(S), INVD))
    p.vop(lambda E: E.tensor_scalar_mul(sl(t1), sl(SS), INVD))
    p.vop(lambda E: E.tensor_mul(sl(B), sl(t0), sl(t0)))
    vv = p.vop(lambda E: E.tensor_sub(sl(t1), sl(t1), sl(B)))
    p.wait("scalar", p.vs, vv)
    sv = emit_row_rstd(p, rb["t2"], t1, r, n=n, off=off)
    p.wait("vector", p.ss, sv)
    p.vop(lambda E: E.tensor_mul(sl(mur), sl(t0), sl(r)))
    if not with_inner:
        p.vop(lambda E: E.tensor_copy(sl(A), sl(r)))
        return p.vop(lambda E: E.tensor_copy(sl(B), sl(mur)))
    p.vop(lambda E: E.tensor_mul(sl(t0), sl(r), sl(r)))
    vv = p.vop(lambda E: E.tensor_mul(sl(t0), sl(t1), sl(t0)))
    p.wait("scalar", p.vs, vv)
    sv = emit_row_rstd(p, rb["t2"], t0, t1, n=n, off=off)
    p.wait("vector", p.ss, sv)
    p.vop(lambda E: E.tensor_mul(sl(A), sl(r), sl(t1)))
    p.vop(lambda E: E.tensor_scalar_mul(sl(t0), sl(S), INVD))
    return p.vop(lambda E: E.tensor_mul(sl(B), sl(t0), sl(A)))



def build_launch1():
    nc = bass.Bass("TRN2", target_bir_lowering=False)
    x_t = nc.dram_tensor("x_t", [D, TOK], F32, kind="ExternalInput").ap()
    lng = nc.dram_tensor("lng_cols", [128, DT], F32, kind="ExternalInput").ap()
    lnb = nc.dram_tensor("lnb_cols", [128, DT], F32, kind="ExternalInput").ap()
    win = nc.dram_tensor("win_arr", [128, DT * R], F32, kind="ExternalInput").ap()
    binc = nc.dram_tensor("b_in_col", [R, 1], F32, kind="ExternalInput").ap()
    w1s_d = nc.dram_tensor("w1_sh", [D, HSH], F32, kind="ExternalInput").ap()
    w2s_d = nc.dram_tensor("w2_sh", [D, HSH], F32, kind="ExternalInput").ap()
    w3s_d = nc.dram_tensor("w3_sh", [HID, DSH], F32, kind="ExternalInput").ap()
    idt = nc.dram_tensor("idt32", [128, 128], F32, kind="ExternalInput").ap()
    u_hand = nc.dram_tensor("u_hand", [R, TOK], F32, kind="ExternalOutput").ap()
    rows1 = nc.dram_tensor("rows1", [4, TOK], F32, kind="ExternalOutput").ap()
    gpart = nc.dram_tensor("gpart", [1, 4], F32, kind="ExternalOutput").ap()

    ctx = ExitStack()
    p = Prog(nc, ctx)
    b = {}
    b["xch_a"] = sbuf(ctx, nc, "xch_a", [128, DT * TC], F32)
    b["xch_b"] = sbuf(ctx, nc, "xch_b", [128, DT * TC], F32)
    b["x16a"] = sbuf(ctx, nc, "x16a", [128, TC], BF16)
    b["x16b"] = sbuf(ctx, nc, "x16b", [128, TC], BF16)
    b["sq16a"] = sbuf(ctx, nc, "sq16a", [128, TC], BF16)
    b["sq16b"] = sbuf(ctx, nc, "sq16b", [128, TC], BF16)
    b["ones16"] = sbuf(ctx, nc, "ones16", [128, 1], BF16)
    ones1f = sbuf(ctx, nc, "ones1f", [1, 128], F32)
    lng_c = sbuf(ctx, nc, "lng_c", [128, DT], F32)
    lnb_c = sbuf(ctx, nc, "lnb_c", [128, DT], F32)
    win16 = sbuf(ctx, nc, "win16", [128, DT * R], BF16)
    bin_c = sbuf(ctx, nc, "bin_c", [R, 1], F32)
    rb = {k: sbuf(ctx, nc, "row_" + k, [1, TOK], F32)
          for k in ("S", "SS", "A", "B", "r", "mur", "t0", "t1", "t2")}
    tmp1 = sbuf(ctx, nc, "tmp1", [128, TC], F32)
    xnp = sbuf(ctx, nc, "xnp", [128, TC], F32)
    xn16 = sbuf(ctx, nc, "xn16", [128, TC], BF16)
    u_sb = sbuf(ctx, nc, "u_sb", [R, TOK], F32)
    b["ps_S"] = psum(ctx, nc, "ps_S", [1, TC])
    b["ps_SS"] = psum(ctx, nc, "ps_SS", [1, TC])
    ps_b1 = psum(ctx, nc, "ps_b1", [128, TC])
    ps_b2 = psum(ctx, nc, "ps_b2", [128, TC])
    ps_u = psum(ctx, nc, "ps_u", [R, TC])

    p.dma("sync", lng_c[:, :], lng)
    p.dma("sync", lnb_c[:, :], lnb)
    p.dma("gpsimd", win16[:, :], win)
    cv = p.dma("sync", bin_c[:, :], binc)
    gv = p.qg.n
    iv = p.op("vector", lambda E: E.memset(b["ones16"][:, :], 1.0))
    iv = p.op("vector", lambda E: E.memset(ones1f[:, :], 1.0), inc=p.vs)

    last_ev = None
    for ch in range(NCH):
        xch = b["xch_a"] if ch % 2 == 0 else b["xch_b"]
        if ch >= 2 and last_ev is not None:
            pass
        if ch >= 2:
            p.wait("sync", p.vs, stats_done[ch - 2])
        for dt in range(DT):
            p.dma("sync", xch[:, dt * TC:(dt + 1) * TC],
                  x_t[dt * 128:(dt + 1) * 128, ch * TC:(ch + 1) * TC])
        x_tiles = [xch[:, dt * TC:(dt + 1) * TC] for dt in range(DT)]
        ev = emit_stats_chunk(p, b, x_tiles, ch, rb["S"], rb["SS"], p.qs.n)
        if ch == 0:
            stats_done = {}
        stats_done[ch] = ev
    rm = emit_rowmath_composite(p, rb, with_inner=True)

    p.wait("tensor", p.qg, gv)
    p.wait("scalar", p.qs, cv)
    for ch in range(NCH):
        xch = b["xch_a"] if ch % 2 == 0 else b["xch_b"]
        if ch < 2:
            p.wait("sync", p.vs, rm)
        else:
            p.wait("sync", p.vs, xn_done[ch - 2])
        for dt in range(DT):
            p.dma("sync", xch[:, dt * TC:(dt + 1) * TC],
                  x_t[dt * 128:(dt + 1) * 128, ch * TC:(ch + 1) * TC])
        ld = p.qs.n
        p.wait("tensor", p.vs, rm)
        if ch >= 1:
            p.wait("tensor", p.vs, bc_used[ch - 1])
        tv = p.op("tensor", lambda E, ch=ch: (
            E.matmul(ps_b1[:, :], ones1f[:, :], rb["A"][0:1, ch * TC:(ch + 1) * TC],
                     start=True, stop=True),
            E.matmul(ps_b2[:, :], ones1f[:, :], rb["B"][0:1, ch * TC:(ch + 1) * TC],
                     start=True, stop=True))[-1], inc=p.ts)
        p.wait("vector", p.ts, tv)
        p.wait("vector", p.qs, ld)
        for dt in range(DT):
            vv = p.op("vector", lambda E, dt=dt, xch=xch: E.tensor_mul(
                tmp1[:, :], xch[:, dt * TC:(dt + 1) * TC], ps_b1[:, :]))
            vv = p.op("vector", lambda E: E.tensor_sub(xnp[:, :], tmp1[:, :],
                                                       ps_b2[:, :]), inc=p.vs)
            p.wait("scalar", p.vs, vv)
            sv = p.op("scalar", lambda E, dt=dt: E.activation(
                xn16[:, :], xnp[:, :], AF.Identity,
                bias=lnb_c[:, dt:dt + 1], scale=lng_c[:, dt:dt + 1]), inc=p.ss)
            p.wait("tensor", p.ss, sv)
            tv2 = p.op("tensor", lambda E, dt=dt: E.matmul(
                ps_u[:, :], win16[:, dt * R:(dt + 1) * R], xn16[:, :],
                start=(dt == 0), stop=(dt == DT - 1)), inc=p.ts)
            p.wait("vector", p.ts, tv2)
        uv = p.op("vector", lambda E, ch=ch: E.tensor_scalar(
            u_sb[:, ch * TC:(ch + 1) * TC], ps_u[:, :], bin_c[:, :], None,
            ALU.add), inc=p.vs)
        if ch == 0:
            xn_done, bc_used = {}, {}
        xn_done[ch] = uv
        bc_used[ch] = uv
    p.wait("sync", p.vs, xn_done[NCH - 1])
    p.dma("sync", u_hand, u_sb[:, :])
    p.dma("sync", rows1[0:1, :], rb["A"][0:1, :])
    p.dma("sync", rows1[1:2, :], rb["B"][0:1, :])
    p.dma("sync", rows1[2:3, :], rb["r"][0:1, :])
    fin = p.dma("sync", rows1[3:4, :], rb["mur"][0:1, :])
    p.wait("sync", p.qs, fin)

    gsta = sbuf(ctx, nc, "gsta", [128, HSH], F32)
    gstb = sbuf(ctx, nc, "gstb", [128, HSH], F32)
    gacc = sbuf(ctx, nc, "gacc", [128, 1], F32)
    gtmp = sbuf(ctx, nc, "gtmp", [128, 1], F32)
    grow = sbuf(ctx, nc, "grow", [1, 4], F32)
    idt_sb = sbuf(ctx, nc, "idt_sb", [128, 128], F32)
    il = p.dma("gpsimd", idt_sb[:, :], idt)
    gwait = p.vs.n
    for j, (wa, ntiles, seg) in enumerate(((w1s_d, DT, HSH), (w2s_d, DT, HSH),
                                           (w3s_d, NHT, DSH))):
        p.wait("vector", p.qs, fin if j == 0 else 0)
        p.op("vector", lambda E: E.memset(gacc[:, :], 0.0))
        g_done = {}
        for i in range(ntiles):
            gst = gsta if i % 2 == 0 else gstb
            if i >= 2:
                p.wait("gpsimd", p.vs, g_done[i - 2])
            lv = p.dma("gpsimd", gst[:, 0:seg],
                       wa[i * 128:(i + 1) * 128, 0:seg])
            p.wait("vector", p.qg, lv)
            p.vop(lambda E, gst=gst, seg=seg: E.tensor_reduce(
                gtmp[:, :], gst[:, 0:seg], AX.X, ALU.add,
                apply_absolute_value=True))
            gv2 = p.vop(lambda E: E.tensor_add(gacc[:, :], gacc[:, :],
                                               gtmp[:, :]))
            g_done[i] = gv2
        p.wait("tensor", p.vs, g_done[ntiles - 1])
        p.wait("tensor", p.qg, il)
        tv = p.op("tensor", lambda E: E.matmul(
            b["ps_S"][0:1, 0:128], gacc[:, :], idt_sb[:, :],
            start=True, stop=True), inc=p.ts)
        p.wait("vector", p.ts, tv)
        p.vop(lambda E, j=j: E.tensor_reduce(
            grow[0:1, j:j + 1], b["ps_S"][0:1, 0:128], AX.X, ALU.add))
    gfin = p.vop(lambda E: E.tensor_copy(grow[0:1, 3:4], grow[0:1, 2:3]))
    p.wait("sync", p.vs, gfin)
    gfs = p.dma("sync", gpart, grow[0:1, :])
    p.wait("sync", p.qs, gfs)

    with ctx, nc.Block() as block:
        p.finalize(block)
    return nc



def build_launch2(upto=5):
    nc = bass.Bass("TRN2", target_bir_lowering=False)
    ap = lambda n, s, d=F32, k="ExternalInput": nc.dram_tensor(n, s, d, kind=k).ap()
    x_t = ap("x_t", [D, TOK])
    rows1 = ap("rows1", [4, TOK])
    u_pad = ap("u_pad", [R, SEQ])
    dec_col = ap("dec_col", [R, 1])
    woutb = ap("woutb", [R + 1, D])
    d_cols = ap("d_cols", [128, DT])
    lng = ap("lng_cols", [128, DT])
    lnb = ap("lnb_cols", [128, DT])
    wdn = ap("wdn_arr", [128, DT * 32])
    bdn = ap("bdn_col", [32, 1])
    wupb = ap("wupb", [33, D])
    l0w = ap("l0w_arr", [D, D])
    g0 = ap("g0_cols", [128, DT])
    be0 = ap("be0_cols", [128, DT])
    lnw = ap("lnw_cols", [128, DT])
    lnbb = ap("lnbb_cols", [128, DT])
    w1a = ap("w1_sh", [D, HSH])
    w2a = ap("w2_sh", [D, HSH])
    w3a = ap("w3_sh", [HID, DSH])
    gparts = ap("gparts", [8, 4])
    aqa = ap("aq_arr", [D, D])
    aka = ap("ak_arr", [128, D])
    ava = ap("av_arr", [128, D])
    bq = ap("bq_cols", [128, DT])
    bk = ap("bk_col", [128, 1])
    bv = ap("bv_col", [128, 1])
    idt = ap("idt32", [128, 128])

    x2h = ap("x2_hand", [D, TOK], F32, "ExternalOutput")
    qh = ap("q_hand", [D, TOK], BF16, "ExternalOutput")
    kh = ap("k_hand", [128, TOK], BF16, "ExternalOutput")
    vh = ap("v_hand", [128, TOK], BF16, "ExternalOutput")
    w1qo = ap("w1q", [D, HSH], BF16, "ExternalOutput")
    w2qo = ap("w2q", [D, HSH], BF16, "ExternalOutput")
    w3qo = ap("w3q", [HID, DSH], BF16, "ExternalOutput")
    x1s = ap("x1_scr", [D, TOK], F32, "Internal")
    sgs = ap("sgn_scr", [D, D], BF16, "Internal")

    ctx = ExitStack()
    p = Prog(nc, ctx)

    b = {}
    b["x16a"] = sbuf(ctx, nc, "x16a", [128, TC], BF16)
    b["x16b"] = sbuf(ctx, nc, "x16b", [128, TC], BF16)
    b["sq16a"] = sbuf(ctx, nc, "sq16a", [128, TC], BF16)
    b["sq16b"] = sbuf(ctx, nc, "sq16b", [128, TC], BF16)
    b["ones16"] = sbuf(ctx, nc, "ones16", [128, 1], BF16)
    ones1f = sbuf(ctx, nc, "ones1f", [1, 128], F32)
    rb = {k: sbuf(ctx, nc, "row_" + k, [1, TC], F32)
          for k in ("S", "SS", "A", "B", "r", "mur", "t0", "t1", "t2")}
    r1rows = sbuf(ctx, nc, "r1rows", [1, 4 * TC], F32)
    tmp1 = sbuf(ctx, nc, "tmp1", [128, TC], F32)
    tmp2 = sbuf(ctx, nc, "tmp2", [128, TC], F32)
    tmp3 = sbuf(ctx, nc, "tmp3", [128, TC], F32)
    cols = {}
    for nm, src in (("d", d_cols), ("lng", lng), ("lnb", lnb), ("g0", g0),
                    ("be0", be0), ("lnw", lnw), ("lnbb", lnbb), ("bq", bq)):
        cols[nm] = sbuf(ctx, nc, "c_" + nm, [128, DT], F32)
        p.dma("sync", cols[nm][:, :], {"d": d_cols, "lng": lng, "lnb": lnb,
                                       "g0": g0, "be0": be0, "lnw": lnw,
                                       "lnbb": lnbb, "bq": bq}[nm])
    bk_c = sbuf(ctx, nc, "bk_c", [128, 1], F32)
    bv_c = sbuf(ctx, nc, "bv_c", [128, 1], F32)
    bdn_c = sbuf(ctx, nc, "bdn_c", [32, 1], F32)
    p.dma("sync", bk_c[:, :], bk)
    p.dma("sync", bv_c[:, :], bv)
    p.dma("sync", bdn_c[:, :], bdn)
    woutb16 = sbuf(ctx, nc, "woutb16", [R + 1, D], BF16)
    wdn16 = sbuf(ctx, nc, "wdn16", [128, DT * 32], BF16)
    wupb16 = sbuf(ctx, nc, "wupb16", [33, D], BF16)
    idt32 = sbuf(ctx, nc, "idt32_sb", [128, 128], F32)
    idt16 = sbuf(ctx, nc, "idt16_sb", [128, 128], BF16)
    ak16 = sbuf(ctx, nc, "ak16", [128, D], BF16)
    av16 = sbuf(ctx, nc, "av16", [128, D], BF16)
    p.dma("gpsimd", woutb16[:, :], woutb)
    p.dma("gpsimd", wdn16[:, :], wdn)
    p.dma("gpsimd", wupb16[:, :], wupb)
    p.dma("sync", idt32[:, :], idt)
    p.dma("gpsimd", idt16[:, :], idt)
    p.dma("gpsimd", ak16[:, :], aka)
    cg0 = p.dma("gpsimd", av16[:, :], ava)
    h_ext16 = sbuf(ctx, nc, "h_ext16", [R + 1, TOK], BF16)
    hf_ext16 = sbuf(ctx, nc, "hf_ext16", [33, TC], BF16)
    srow = sbuf(ctx, nc, "srow", [1, 16], F32)
    s0col = sbuf(ctx, nc, "s0col", [128, 1], F32)
    sg0c = sbuf(ctx, nc, "sg0c", [128, DT], F32)
    thp_c = sbuf(ctx, nc, "thp_c", [128, 1], F32)
    thn_c = sbuf(ctx, nc, "thn_c", [128, 1], F32)
    mxacc = sbuf(ctx, nc, "mxacc", [128, 1], F32)
    mxt = sbuf(ctx, nc, "mxt", [128, 1], F32)
    gacc = sbuf(ctx, nc, "gacc", [128, 1], F32)
    c0 = p.qs.n

    b["ps_S"] = psum(ctx, nc, "ps_S", [1, TC])
    b["ps_SS"] = psum(ctx, nc, "ps_SS", [1, TC])
    ps_b1 = psum(ctx, nc, "ps_b1", [128, TC])
    ps_b2 = psum(ctx, nc, "ps_b2", [128, TC])
    ps_b3 = psum(ctx, nc, "ps_b3", [128, TC])
    ps_b4 = psum(ctx, nc, "ps_b4", [128, TC])
    ps_ar = psum(ctx, nc, "ps_ar", [128, TC])
    ps_hf = psum(ctx, nc, "ps_hf", [128, TC])

    p.op("vector", lambda E: E.memset(b["ones16"][:, :], 1.0))
    p.op("vector", lambda E: E.memset(ones1f[:, :], 1.0))
    p.op("vector", lambda E: E.memset(h_ext16[:, :], 1.0))
    mz = p.op("vector", lambda E: E.memset(hf_ext16[:, :], 1.0), inc=p.vs)

    sctx = ExitStack()
    u_sb = sbuf(sctx, nc, "u_sb", [R, SEQ], F32)
    a_b = sbuf(sctx, nc, "a_b", [R, SEQ], F32)
    h_pad = sbuf(sctx, nc, "h_pad", [R, SEQ], F32)
    dcol = sbuf(sctx, nc, "dcol", [R, 1], F32)
    p.dma("sync", dcol[:, :], dec_col)
    lu = p.dma("sync", u_sb[:, :], u_pad)
    p.wait("scalar", p.qs, lu)
    sv = p.op("scalar", lambda E: E.activation(a_b[:, 0:1], dcol[:, :], AF.Exp,
                                               scale=-1.0), inc=p.ss)
    p.wait("vector", p.ss, sv)
    p.vop(lambda E: E.tensor_scalar_add(a_b[:, 0:1], a_b[:, 0:1], 1.0))
    p.vop(lambda E: E.reciprocal(dcol[:, :], a_b[:, 0:1]))
    p.vop(lambda E: E.memset(a_b[:, :], 1.0))
    p.vop(lambda E: E.tensor_scalar_mul(a_b[:, :], a_b[:, :], dcol[:, :]))
    p.wait("vector", p.qs, lu)
    p.op("vector", lambda E: E.tensor_tensor_scan(
        h_pad[:, :], a_b[:, :], u_sb[:, :], 0.0, ALU.mult, ALU.add))
    scan_done = p.op("vector", lambda E: E.tensor_copy(
        h_ext16[0:R, :], h_pad[0:R, SEQ - TOK:SEQ]), inc=p.vs)
    sctx.close()

    pctx = ExitStack()
    wst_a = sbuf(pctx, nc, "wst_a", [128, D], F32)
    wst_b = sbuf(pctx, nc, "wst_b", [128, D], F32)
    sg16 = sbuf(pctx, nc, "sg16", [128, D], BF16)
    p.op("vector", lambda E: E.memset(mxacc[:, :], 0.0))
    sg_writes = {}
    for do in range(DT):
        wst = wst_a if do % 2 == 0 else wst_b
        if do < 2:
            p.wait("sync", p.vs, scan_done)
        else:
            p.wait("sync", p.vs, mx_done[do - 2])
        ldv = p.dma("sync", wst[:, :], l0w[do * 128:(do + 1) * 128, :])
        p.wait("scalar", p.qs, ldv)
        if do >= 1:
            p.wait("scalar", p.qg, sg_writes[do - 1])
        sv = p.op("scalar", lambda E, wst=wst: E.sign(sg16[:, :], wst[:, :]), inc=p.ss)
        p.wait("gpsimd", p.ss, sv)
        sg_writes[do] = p.dma("gpsimd", sgs[do * 128:(do + 1) * 128, :], sg16[:, :])
        p.wait("vector", p.ss, sv)
        p.vop(lambda E, wst=wst: E.tensor_reduce(
            mxt[:, :], wst[:, :], AX.X, ALU.max, apply_absolute_value=True))
        vv = p.vop(lambda E: E.tensor_max(mxacc[:, :], mxacc[:, :], mxt[:, :]))
        if do == 0:
            mx_done = {}
        mx_done[do] = vv
    p.wait("tensor", p.vs, mx_done[DT - 1])
    p.wait("tensor", p.qs, c0)
    tv = p.op("tensor", lambda E: E.matmul(b["ps_S"][0:1, 0:128], mxacc[:, :],
                                           idt32[:, :], start=True, stop=True),
              inc=p.ts)
    p.wait("vector", p.ts, tv)
    vv = p.vop(lambda E: E.tensor_reduce(
        srow[0:1, 0:1], b["ps_S"][0:1, 0:128], AX.X, ALU.max))
    p.wait("tensor", p.vs, vv)
    tv = p.op("tensor", lambda E: E.matmul(ps_b1[:, 0:1], ones1f[:, :],
                                           srow[0:1, 0:1], start=True, stop=True),
              inc=p.ts)
    p.wait("vector", p.ts, tv)
    p.vop(lambda E: E.tensor_copy(s0col[:, :], ps_b1[:, 0:1]))
    prep_done = p.vop(lambda E: E.tensor_scalar_mul(
        sg0c[:, :], cols["g0"][:, :], s0col[:, :]))
    pctx.close()

    p.wait("sync", p.vs, prep_done)
    if upto <= 1:
        p.wait("sync", p.qs, p.qs.n)
        p.wait("gpsimd", p.qg, p.qg.n)
        with ctx, nc.Block() as block:
            p.finalize(block)
        return nc

    mctx = ExitStack()
    xch_a = sbuf(mctx, nc, "xch_a", [128, DT * TC], F32)
    xch_b = sbuf(mctx, nc, "xch_b", [128, DT * TC], F32)
    xmid = sbuf(mctx, nc, "xmid", [128, DT * TC], F32)
    xnb = sbuf(mctx, nc, "xnb", [128, DT * TC], BF16)
    st_a = sbuf(mctx, nc, "st_a", [128, D], F32)
    st_b = sbuf(mctx, nc, "st_b", [128, D], F32)
    sb16a = sbuf(mctx, nc, "sb16a", [128, D], BF16)
    sb16b = sbuf(mctx, nc, "sb16b", [128, D], BF16)
    q16o = sbuf(mctx, nc, "q16o", [128, TC], BF16)
    q16ob = sbuf(mctx, nc, "q16ob", [128, TC], BF16)
    q16op = [q16o, q16ob]
    vtmp = sbuf(mctx, nc, "vtmp", [128, TOK], BF16)

    def load_xchunk(ch, src, guard=None):
        xch = xch_a if ch % 2 == 0 else xch_b
        if guard is not None:
            p.wait("sync", *guard)
        for dt in range(DT):
            v = p.dma("sync", xch[:, dt * TC:(dt + 1) * TC],
                      src[dt * 128:(dt + 1) * 128, ch * TC:(ch + 1) * TC])
        return xch, v

    def store_xchunk(ch, xch, dst, ready_ctr, ready_val):
        p.wait("sync", ready_ctr, ready_val)
        for dt in range(DT):
            v = p.dma("sync", dst[dt * 128:(dt + 1) * 128, ch * TC:(ch + 1) * TC],
                      xch[:, dt * TC:(dt + 1) * TC])
        return v

    def bcasts(ch_rows, n=TC):
        A_ap, B_ap = ch_rows
        return p.op("tensor", lambda E: (
            E.matmul(ps_b1[:, 0:n], ones1f[:, :], A_ap, start=True, stop=True),
            E.matmul(ps_b2[:, 0:n], ones1f[:, :], B_ap, start=True, stop=True))[-1],
            inc=p.ts)

    sign_last_ss = p.ss.n
    sign_last_qg = p.qg.n
    p.wait("tensor", p.vs, scan_done)
    p.wait("tensor", p.vs, prep_done)
    p.wait("tensor", p.qg, cg0)
    ps_work = ps_ar
    ps_workb = ps_hf
    work = [ps_work, ps_workb]
    work_free = [0, 0]
    sg_used = {}
    q_used = {}
    prev_bcast_tv = 0
    prev_chunk_done = None
    x16p = [b["x16a"], b["x16b"]]
    sq16p = [b["sq16a"], b["sq16b"]]
    st_free = [0, 0]
    if upto <= 1:
        p.wait("sync", p.qs, p.qs.n)
        p.wait("gpsimd", p.qg, p.qg.n)
        mctx_dummy = None
        with ctx, nc.Block() as block:
            p.finalize(block)
        return nc
    for ch in range(NCH):
        if ch == 0:
            p.wait("sync", p.vs, prep_done)
            p.wait("sync", p.ss, sign_last_ss)
            p.wait("sync", p.qg, sign_last_qg)
        xch, xld = load_xchunk(ch, x_t,
                               guard=(p.vs, prev_chunk_done) if prev_chunk_done else None)
        if ch > 0:
            p.wait("sync", p.ts, prev_bcast_tv)
        for i in range(4):
            rl = p.dma("sync", r1rows[0:1, i * TC:(i + 1) * TC],
                       rows1[i:i + 1, ch * TC:(ch + 1) * TC])
        p.wait("tensor", p.qs, rl)
        if ch > 0:
            p.wait("tensor", p.vs, prev_chunk_done)
        tv = p.op("tensor", lambda E: (
            E.matmul(ps_b1[:, :], ones1f[:, :], r1rows[0:1, 0:TC], start=True, stop=True),
            E.matmul(ps_b2[:, :], ones1f[:, :], r1rows[0:1, TC:2 * TC], start=True, stop=True),
            E.matmul(ps_b3[:, :], ones1f[:, :], r1rows[0:1, 2 * TC:3 * TC], start=True, stop=True))[-1],
            inc=p.ts)
        prev_bcast_tv = tv
        p.wait("vector", p.qs, rl)
        nm = p.vop(lambda E: E.tensor_scalar_mul(
            r1rows[0:1, 3 * TC:4 * TC], r1rows[0:1, 3 * TC:4 * TC], -1.0))
        p.wait("tensor", p.vs, nm)
        p.wait("vector", p.ts, tv)
        p.wait("vector", p.qs, xld)
        p.wait("scalar", p.qs, xld)
        stv = {}
        for dt in range(DT):
            par = dt % 2
            xt = xch[:, dt * TC:(dt + 1) * TC]
            p.wait("tensor", p.vs, work_free[par])
            tv2 = p.op("tensor", lambda E, dt=dt, ch=ch, par=par: (
                E.matmul(work[par][:, 0:TC], woutb16[:, dt * 128:(dt + 1) * 128],
                         h_ext16[:, ch * TC:(ch + 1) * TC], start=True, stop=False,
                         skip_group_check=True),
                E.matmul(work[par][:, 0:TC], ones1f[:, :],
                         r1rows[0:1, 3 * TC:4 * TC], start=False, stop=True,
                         skip_group_check=True))[-1], inc=p.ts)
            p.op("vector", lambda E, xt=xt: E.tensor_mul(tmp1[:, :], xt, ps_b1[:, :]))
            vv = p.op("vector", lambda E: E.tensor_sub(tmp2[:, :], tmp1[:, :],
                                                       ps_b2[:, :]), inc=p.vs)
            p.wait("scalar", p.vs, vv)
            sv = p.op("scalar", lambda E, dt=dt: E.activation(
                tmp3[:, :], tmp2[:, :], AF.Identity,
                bias=cols["lnb"][:, dt:dt + 1], scale=cols["lng"][:, dt:dt + 1]),
                inc=p.ss)
            p.op("vector", lambda E, xt=xt: E.tensor_mul(tmp1[:, :], xt, ps_b3[:, :]))
            p.wait("vector", p.ts, tv2)
            p.wait("vector", p.ss, sv)
            wv = p.op("vector", lambda E, dt=dt, par=par: E.scalar_tensor_tensor(
                tmp2[:, :], tmp3[:, :], cols["d"][:, dt:dt + 1], work[par][:, 0:TC],
                ALU.mult, ALU.add), inc=p.vs)
            work_free[par] = wv
            xmv = p.op("vector", lambda E, dt=dt: E.tensor_add(
                xmid[:, dt * TC:(dt + 1) * TC], tmp1[:, :], tmp2[:, :]), inc=p.vs)
            p.wait("scalar", p.vs, xmv)
            p.wait("scalar", p.ts, st_free[par])
            p.op("scalar", lambda E, dt=dt, par=par: E.copy(
                x16p[par][:, :], xmid[:, dt * TC:(dt + 1) * TC]))
            sqv = p.op("scalar", lambda E, par=par: E.activation(
                sq16p[par][:, :], x16p[par][:, :], AF.Square), inc=p.ss)
            p.wait("tensor", p.ss, sqv)
            stv[dt] = p.op("tensor", lambda E, dt=dt, par=par: (
                E.matmul(b["ps_S"][0:1, :], b["ones16"][:, :], x16p[par][:, :],
                         start=(dt == 0), stop=(dt == DT - 1), skip_group_check=True),
                E.matmul(b["ps_SS"][0:1, :], b["ones16"][:, :], sq16p[par][:, :],
                         start=(dt == 0), stop=(dt == DT - 1), skip_group_check=True))[-1],
                inc=p.ts)
            st_free[par] = stv[dt]
        p.wait("vector", p.ts, stv[DT - 1])
        p.op("vector", lambda E: E.tensor_copy(rb["S"][0:1, :], b["ps_S"][0:1, :]))
        p.op("vector", lambda E: E.tensor_copy(rb["SS"][0:1, :], b["ps_SS"][0:1, :]))
        rm = emit_rowmath_composite(p, rb, with_inner=False, n=TC, off=0)
        p.wait("tensor", p.vs, rm)
        tv = bcasts((rb["A"][0:1, :], rb["B"][0:1, :]))
        p.wait("vector", p.ts, tv)
        p.wait("scalar", p.ts, tv)
        hfv = {}
        for dt in range(DT):
            p.op("vector", lambda E, dt=dt: E.tensor_mul(
                tmp1[:, :], xmid[:, dt * TC:(dt + 1) * TC], ps_b1[:, :]))
            vv = p.op("vector", lambda E: E.tensor_sub(tmp2[:, :], tmp1[:, :],
                                                       ps_b2[:, :]), inc=p.vs)
            p.wait("scalar", p.vs, vv)
            sv = p.op("scalar", lambda E, dt=dt: E.activation(
                xnb[:, dt * TC:(dt + 1) * TC], tmp2[:, :], AF.Identity,
                bias=cols["lnb"][:, dt:dt + 1], scale=cols["lng"][:, dt:dt + 1]),
                inc=p.ss)
            p.wait("tensor", p.ss, sv)
            if dt == 0:
                p.wait("tensor", p.vs, work_free[1])
            hfv[dt] = p.op("tensor", lambda E, dt=dt: E.matmul(
                ps_workb[0:32, 0:TC], wdn16[:, dt * 32:(dt + 1) * 32],
                xnb[:, dt * TC:(dt + 1) * TC],
                start=(dt == 0), stop=(dt == DT - 1)), inc=p.ts)
        p.wait("vector", p.ts, hfv[DT - 1])
        hv = p.op("vector", lambda E: E.tensor_scalar(
            hf_ext16[0:32, 0:TC], ps_workb[0:32, 0:TC], bdn_c[:, :], 0.0,
            ALU.add, ALU.max), inc=p.vs)
        work_free[1] = hv
        p.wait("tensor", p.vs, hv)
        stv = {}
        for dt in range(DT):
            par = dt % 2
            p.wait("tensor", p.vs, work_free[par])
            tv2 = p.op("tensor", lambda E, dt=dt, par=par: E.matmul(
                work[par][:, 0:TC], wupb16[:, dt * 128:(dt + 1) * 128],
                hf_ext16[:, 0:TC], start=True, stop=True), inc=p.ts)
            p.op("vector", lambda E, dt=dt, xch=xch: E.tensor_add(
                tmp1[:, :], xch[:, dt * TC:(dt + 1) * TC],
                xmid[:, dt * TC:(dt + 1) * TC]))
            p.wait("vector", p.ts, tv2)
            xv = p.op("vector", lambda E, dt=dt, xch=xch, par=par: E.tensor_add(
                xch[:, dt * TC:(dt + 1) * TC], tmp1[:, :], work[par][:, 0:TC]),
                inc=p.vs)
            work_free[par] = xv
            p.wait("scalar", p.vs, xv)
            p.wait("scalar", p.ts, st_free[par])
            p.op("scalar", lambda E, dt=dt, par=par, xch=xch: E.copy(
                x16p[par][:, :], xch[:, dt * TC:(dt + 1) * TC]))
            sqv = p.op("scalar", lambda E, par=par: E.activation(
                sq16p[par][:, :], x16p[par][:, :], AF.Square), inc=p.ss)
            p.wait("tensor", p.ss, sqv)
            stv[dt] = p.op("tensor", lambda E, dt=dt, par=par: (
                E.matmul(b["ps_S"][0:1, :], b["ones16"][:, :], x16p[par][:, :],
                         start=(dt == 0), stop=(dt == DT - 1), skip_group_check=True),
                E.matmul(b["ps_SS"][0:1, :], b["ones16"][:, :], sq16p[par][:, :],
                         start=(dt == 0), stop=(dt == DT - 1), skip_group_check=True))[-1],
                inc=p.ts)
            st_free[par] = stv[dt]
        p.wait("vector", p.ts, stv[DT - 1])
        p.op("vector", lambda E: E.tensor_copy(rb["S"][0:1, :], b["ps_S"][0:1, :]))
        p.op("vector", lambda E: E.tensor_copy(rb["SS"][0:1, :], b["ps_SS"][0:1, :]))
        rm = emit_rowmath_composite(p, rb, with_inner=True, n=TC, off=0)
        p.wait("tensor", p.vs, rm)
        tv = bcasts((rb["A"][0:1, :], rb["B"][0:1, :]))
        p.wait("vector", p.ts, tv)
        p.wait("scalar", p.ts, tv)
        for dt in range(DT):
            p.op("vector", lambda E, dt=dt, xch=xch: E.tensor_mul(
                tmp1[:, :], xch[:, dt * TC:(dt + 1) * TC], ps_b1[:, :]))
            vv = p.op("vector", lambda E: E.tensor_sub(tmp2[:, :], tmp1[:, :],
                                                       ps_b2[:, :]), inc=p.vs)
            p.wait("scalar", p.vs, vv)
            sv = p.op("scalar", lambda E, dt=dt: E.activation(
                xnb[:, dt * TC:(dt + 1) * TC], tmp2[:, :], AF.Identity,
                bias=cols["lnbb"][:, dt:dt + 1], scale=cols["lnw"][:, dt:dt + 1]),
                inc=p.ss)
        xn_ready = sv
        p.wait("tensor", p.ss, xn_ready)
        stv = {}
        for do in range(DT):
            par = do % 2
            sb = sb16a if par == 0 else sb16b
            if (ch, do) != (0, 0) and (do >= 2 or ch > 0):
                key = (ch, do - 2) if do >= 2 else (ch - 1, DT - 2 + do)
                p.wait("sync", p.ts, sg_used[key])
            lv = p.dma("sync", sb[:, :], sgs[do * 128:(do + 1) * 128, :])
            p.wait("tensor", p.qs, lv)
            p.wait("tensor", p.vs, work_free[par])
            tv2 = None
            for dt in range(DT):
                tv2 = p.op("tensor", lambda E, dt=dt, sb=sb, par=par: E.matmul(
                    work[par][:, 0:TC], sb[:, dt * 128:(dt + 1) * 128],
                    xnb[:, dt * TC:(dt + 1) * TC],
                    start=(dt == 0), stop=(dt == DT - 1)),
                    inc=p.ts if dt == DT - 1 else None)
            sg_used[(ch, do)] = tv2
            p.wait("vector", p.ts, tv2)
            p.op("vector", lambda E, do=do, par=par: E.tensor_scalar(
                tmp1[:, :], work[par][:, 0:TC], sg0c[:, do:do + 1],
                cols["be0"][:, do:do + 1], ALU.mult, ALU.add))
            xv = p.op("vector", lambda E, do=do, xch=xch: E.tensor_add(
                xch[:, do * TC:(do + 1) * TC], xch[:, do * TC:(do + 1) * TC],
                tmp1[:, :]), inc=p.vs)
            work_free[par] = xv
            p.wait("scalar", p.vs, xv)
            p.wait("scalar", p.ts, st_free[par])
            p.op("scalar", lambda E, do=do, par=par, xch=xch: E.copy(
                x16p[par][:, :], xch[:, do * TC:(do + 1) * TC]))
            sqv = p.op("scalar", lambda E, par=par: E.activation(
                sq16p[par][:, :], x16p[par][:, :], AF.Square), inc=p.ss)
            p.wait("tensor", p.ss, sqv)
            stv[do] = p.op("tensor", lambda E, do=do, par=par: (
                E.matmul(b["ps_S"][0:1, :], b["ones16"][:, :], x16p[par][:, :],
                         start=(do == 0), stop=(do == DT - 1), skip_group_check=True),
                E.matmul(b["ps_SS"][0:1, :], b["ones16"][:, :], sq16p[par][:, :],
                         start=(do == 0), stop=(do == DT - 1), skip_group_check=True))[-1],
                inc=p.ts)
            st_free[par] = stv[do]
        sv2 = store_xchunk(ch, xch, x2h, p.vs, work_free[(DT - 1) % 2])
        p.wait("vector", p.ts, stv[DT - 1])
        p.op("vector", lambda E: E.tensor_copy(rb["S"][0:1, :], b["ps_S"][0:1, :]))
        p.op("vector", lambda E: E.tensor_copy(rb["SS"][0:1, :], b["ps_SS"][0:1, :]))
        rm = emit_rowmath_composite(p, rb, with_inner=False, n=TC, off=0)
        p.wait("tensor", p.vs, rm)
        tv = bcasts((rb["A"][0:1, :], rb["B"][0:1, :]))
        p.wait("vector", p.ts, tv)
        for dt in range(DT):
            p.op("vector", lambda E, dt=dt, xch=xch: E.tensor_mul(
                tmp1[:, :], xch[:, dt * TC:(dt + 1) * TC], ps_b1[:, :]))
            vv = p.op("vector", lambda E, dt=dt: E.tensor_sub(
                xnb[:, dt * TC:(dt + 1) * TC], tmp1[:, :], ps_b2[:, :]), inc=p.vs)
        xn_ready = vv
        for do in range(DT):
            par = do % 2
            sb = sb16a if par == 0 else sb16b
            if do >= 2:
                p.wait("gpsimd", p.ts, q_used[(ch, do - 2)])
            else:
                p.wait("gpsimd", p.ts, sg_used[(ch, DT - 2 + do)])
            lv = p.dma("gpsimd", sb[:, :], aqa[do * 128:(do + 1) * 128, :])
            p.wait("tensor", p.qg, lv)
            p.wait("tensor", p.vs, xn_ready)
            p.wait("tensor", p.vs, work_free[par])
            tv2 = None
            for dt in range(DT):
                tv2 = p.op("tensor", lambda E, dt=dt, sb=sb, par=par: E.matmul(
                    work[par][:, 0:TC], sb[:, dt * 128:(dt + 1) * 128],
                    xnb[:, dt * TC:(dt + 1) * TC],
                    start=(dt == 0), stop=(dt == DT - 1)),
                    inc=p.ts if dt == DT - 1 else None)
            q_used[(ch, do)] = tv2
            p.wait("vector", p.ts, tv2)
            if ch > 0 or do >= 2:
                p.wait("vector", p.qs, q_st.get((ch, do - 2) if do >= 2 else
                                                (ch - 1, DT - 2 + do), 0))
            qv = p.op("vector", lambda E, do=do, par=par: E.tensor_scalar(
                q16op[par][:, :], work[par][:, 0:TC], cols["bq"][:, do:do + 1], None,
                ALU.add), inc=p.vs)
            work_free[par] = qv
            p.wait("sync", p.vs, qv)
            if ch == 0 and do == 0:
                q_st = {}
            q_st[(ch, do)] = p.dma("sync", qh[do * 128:(do + 1) * 128,
                                              ch * TC:(ch + 1) * TC], q16op[par][:, :])
        p.wait("tensor", p.qg, cg0)
        for (a16w, bcol, outh) in ((ak16, bk_c, kh), (av16, bv_c, None)):
            par = 0 if outh is not None else 1
            p.wait("tensor", p.vs, work_free[par])
            tv2 = None
            for dt in range(DT):
                tv2 = p.op("tensor", lambda E, dt=dt, a16w=a16w, par=par: E.matmul(
                    work[par][:, 0:TC], a16w[:, dt * 128:(dt + 1) * 128],
                    xnb[:, dt * TC:(dt + 1) * TC],
                    start=(dt == 0), stop=(dt == DT - 1)),
                    inc=p.ts if dt == DT - 1 else None)
            p.wait("vector", p.ts, tv2)
            if outh is not None:
                p.wait("vector", p.qs, q_st[(ch, DT - 1)])
                qv = p.op("vector", lambda E, bcol=bcol, par=par: E.tensor_scalar(
                    q16op[par][:, :], work[par][:, 0:TC], bcol[:, :], None, ALU.add),
                    inc=p.vs)
                work_free[par] = qv
                p.wait("sync", p.vs, qv)
                q_st[(ch, DT)] = p.dma("sync", outh[:, ch * TC:(ch + 1) * TC],
                                       q16op[par][:, :])
            else:
                qv = p.op("vector", lambda E, bcol=bcol, ch=ch, par=par: E.tensor_scalar(
                    vtmp[:, ch * TC:(ch + 1) * TC], work[par][:, 0:TC], bcol[:, :],
                    None, ALU.add), inc=p.vs)
                work_free[par] = qv
        prev_chunk_done = qv
    prev_done2 = prev_chunk_done

    p.wait("tensor", p.vs, prev_done2)
    vt_prev = None
    for ktl in range(DT):
        if ktl > 0:
            p.wait("tensor", p.vs, vv)
        tv = p.op("tensor", lambda E, ktl=ktl: E.matmul(
            ps_ar[:, 0:128], vtmp[:, ktl * 128:(ktl + 1) * 128], idt16[:, :],
            start=True, stop=True), inc=p.ts)
        p.wait("vector", p.ts, tv)
        if vt_prev is None:
            p.wait("vector", p.qs, q_st[(NCH - 1, DT)])
        else:
            p.wait("vector", p.qg, vt_prev)
        vv = p.op("vector", lambda E: E.tensor_copy(q16o[:, 0:128], ps_ar[:, 0:128]),
                  inc=p.vs)
        p.wait("gpsimd", p.vs, vv)
        vt_prev = p.dma("gpsimd", vh[:, ktl * 128:(ktl + 1) * 128], q16o[:, 0:128])

    if upto <= 4:
        p.wait("sync", p.qs, p.qs.n)
        p.wait("gpsimd", p.qg, p.qg.n)
        mctx.close()
        with ctx, nc.Block() as block:
            p.finalize(block)
        return nc

    mctx.close()
    quant_guard_ts = p.ts.n
    quant_guard_vs = p.vs.n
    quant_guard_qg = p.qg.n
    p.wait("sync", p.ts, quant_guard_ts)
    p.wait("sync", p.vs, quant_guard_vs)
    p.wait("sync", p.qg, quant_guard_qg)
    qctx = ExitStack()
    HH = HID // 2
    qst_a = sbuf(qctx, nc, "qst_a", [128, HH], F32)
    qst_b = sbuf(qctx, nc, "qst_b", [128, HH], F32)
    qa_t = sbuf(qctx, nc, "qa_t", [128, HH], F32)
    qb_t = sbuf(qctx, nc, "qb_t", [128, HH], F32)
    q16h = sbuf(qctx, nc, "q16h", [128, HH], BF16)
    gp_sb = sbuf(qctx, nc, "gp_sb", [8, 4], F32)
    ones8 = sbuf(qctx, nc, "ones8", [8, 1], F32)
    gcol = sbuf(qctx, nc, "gcol", [4, 1], F32)
    gl = p.dma("sync", gp_sb[:, :], gparts)
    mv8 = p.op("vector", lambda E: E.memset(ones8[:, :], 1.0), inc=p.vs)
    p.wait("tensor", p.vs, mv8)
    p.wait("tensor", p.qs, gl)
    tv = p.op("tensor", lambda E: E.matmul(
        ps_b1[0:4, 0:1], gp_sb[:, :], ones8[:, :], start=True, stop=True),
        inc=p.ts)
    p.wait("vector", p.ts, tv)
    gvv = p.vop(lambda E: E.tensor_copy(gcol[:, :], ps_b1[0:4, 0:1]))
    p.wait("tensor", p.vs, gvv)
    tv = p.op("tensor", lambda E: E.matmul(
        b["ps_S"][0:1, 0:128], gcol[:, :], idt32[0:4, :],
        start=True, stop=True), inc=p.ts)
    p.wait("vector", p.ts, tv)
    p.vop(lambda E: E.tensor_copy(srow[0:1, 4:8], b["ps_S"][0:1, 0:4]))
    qw_prev = 0
    for j, (wa, wq, ntiles, seg) in enumerate(
            ((w1a, w1qo, DT, HSH), (w2a, w2qo, DT, HSH),
             (w3a, w3qo, NHT, DSH))):
        if j > 0:
            p.wait("sync", p.vs, p.vs.n)
        nelem = float(D * HID)
        p.vop(lambda E, j=j: E.tensor_scalar(
            srow[0:1, 1:2], srow[0:1, 4 + j:5 + j], 0.5 / nelem, 0.5e-8,
            ALU.mult, ALU.add))
        vv = p.vop(lambda E: E.tensor_scalar_mul(
            srow[0:1, 2:3], srow[0:1, 1:2], -1.0))
        p.wait("tensor", p.vs, vv)
        tv = p.op("tensor", lambda E: (
            E.matmul(ps_b1[:, 0:1], ones1f[:, :], srow[0:1, 1:2], start=True, stop=True),
            E.matmul(ps_b2[:, 0:1], ones1f[:, :], srow[0:1, 2:3], start=True, stop=True))[-1],
            inc=p.ts)
        p.wait("vector", p.ts, tv)
        p.vop(lambda E: E.tensor_copy(thp_c[:, :], ps_b1[:, 0:1]))
        thv = p.vop(lambda E: E.tensor_copy(thn_c[:, :], ps_b2[:, 0:1]))
        qq_done = {}
        for i in range(ntiles):
            qst = qst_a if i % 2 == 0 else qst_b
            if i >= 2:
                p.wait("sync", p.vs, qq_done[i - 2])
            lv = p.dma("sync", qst[:, 0:seg],
                       wa[i * 128:(i + 1) * 128, 0:seg])
            p.wait("vector", p.qs, lv)
            p.wait("vector", p.vs, thv)
            av = p.op("vector", lambda E, qst=qst, seg=seg: E.tensor_single_scalar(
                qa_t[:, 0:seg], qst[:, 0:seg], thp_c[:, :], ALU.is_gt), inc=p.vs)
            bv_ = p.op("vector", lambda E, qst=qst, seg=seg: E.tensor_single_scalar(
                qb_t[:, 0:seg], qst[:, 0:seg], thn_c[:, :], ALU.is_lt), inc=p.vs)
            if qw_prev:
                p.wait("vector", p.qg, qw_prev)
            gv = p.op("vector", lambda E, seg=seg: E.tensor_sub(
                q16h[:, 0:seg], qa_t[:, 0:seg], qb_t[:, 0:seg]), inc=p.vs)
            qq_done[i] = gv
            p.wait("gpsimd", p.vs, gv)
            qw_prev = p.dma("gpsimd", wq[i * 128:(i + 1) * 128, 0:seg],
                            q16h[:, 0:seg])
    p.wait("gpsimd", p.qg, p.qg.n)
    p.wait("sync", p.qs, p.qs.n)
    qctx.close()

    with ctx, nc.Block() as block:
        p.finalize(block)
    return nc



def _cols(v):
    return np.ascontiguousarray(v.reshape(DT, 128).T).astype(np.float32)


def _strips(w_t, m_tiles):
    Kd, M = w_t.shape
    kt = Kd // 128
    a = w_t.reshape(kt, 128, m_tiles, 128)
    a = a.transpose(2, 1, 0, 3).reshape(m_tiles * 128, kt * 128)
    return np.ascontiguousarray(a).astype(np.float32)


def _stage_common(inp):
    st = {}
    st["lng_cols"] = _cols(inp["L0_attn_ln_gamma"])
    st["lnb_cols"] = _cols(inp["L0_attn_ln_beta"])
    st["win_arr"] = np.ascontiguousarray(
        inp["L0_attn_W_in"].reshape(DT, 128, R).transpose(1, 0, 2)
        .reshape(128, DT * R)).astype(np.float32)
    st["b_in_col"] = inp["L0_attn_b_in"].reshape(R, 1).astype(np.float32)
    st["dec_col"] = inp["L0_attn_decay_logit"].reshape(R, 1).astype(np.float32)
    st["woutb"] = np.ascontiguousarray(np.concatenate(
        [inp["L0_attn_W_out"], inp["L0_attn_b_out"][None, :]], 0)).astype(np.float32)
    st["d_cols"] = _cols(inp["L0_attn_D_skip"])
    st["wdn_arr"] = np.ascontiguousarray(
        inp["L0_attn_W_down"].reshape(DT, 128, 32).transpose(1, 0, 2)
        .reshape(128, DT * 32)).astype(np.float32)
    st["bdn_col"] = inp["L0_attn_b_down"].reshape(32, 1).astype(np.float32)
    st["wupb"] = np.ascontiguousarray(np.concatenate(
        [inp["L0_attn_W_up"], inp["L0_attn_b_up"][None, :]], 0)).astype(np.float32)
    st["l0w_arr"] = _strips(np.ascontiguousarray(inp["L0_ffn_weight"].T), DT)
    st["g0_cols"] = _cols(inp["L0_ffn_gamma"])
    st["be0_cols"] = _cols(inp["L0_ffn_beta"])
    st["lnw_cols"] = _cols(inp["L0_ffn_ln_weight"])
    st["lnbb_cols"] = _cols(inp["L0_ffn_ln_bias"])
    st["w1_arr"] = np.ascontiguousarray(
        inp["L1_ffn_w1"].T.reshape(DT, 128, HID).reshape(D, HID)).astype(np.float32)
    st["w2_arr"] = np.ascontiguousarray(inp["L1_ffn_w2"].T).astype(np.float32)
    st["w3_arr"] = np.ascontiguousarray(inp["L1_ffn_w3"].T).astype(np.float32)
    st["aq_arr"] = _strips(np.ascontiguousarray(inp["L1_attn_Wq"].T), DT)
    st["ak_arr"] = np.ascontiguousarray(
        inp["L1_attn_Wk"].reshape(128, DT, 128).transpose(2, 1, 0)
        .reshape(128, D)).astype(np.float32)
    st["av_arr"] = np.ascontiguousarray(
        inp["L1_attn_Wv"].reshape(128, DT, 128).transpose(2, 1, 0)
        .reshape(128, D)).astype(np.float32)
    st["bq_cols"] = _cols(inp["L1_attn_bq"])
    st["bk_col"] = inp["L1_attn_bk"].reshape(128, 1).astype(np.float32)
    st["bv_col"] = inp["L1_attn_bv"].reshape(128, 1).astype(np.float32)
    st["idt32"] = np.eye(128, dtype=np.float32)
    st["ao_arr"] = _strips(np.ascontiguousarray(inp["L1_attn_Wo"].T), DT)
    st["bo_cols"] = _cols(inp["L1_attn_bo"])
    return st


def _in1(st, x_t_c, c):
    return dict(x_t=x_t_c, lng_cols=st["lng_cols"], lnb_cols=st["lnb_cols"],
                win_arr=st["win_arr"], b_in_col=st["b_in_col"],
                idt32=st["idt32"],
                w1_sh=st["_w1sh"][c], w2_sh=st["_w2sh"][c],
                w3_sh=st["_w3sh"][c])


def _in2(st, x_t_c, rows1_c, u_pad_c, c):
    keys = ["dec_col", "woutb", "d_cols", "lng_cols", "lnb_cols", "wdn_arr",
            "bdn_col", "wupb", "l0w_arr", "g0_cols", "be0_cols", "lnw_cols",
            "lnbb_cols", "aq_arr", "ak_arr",
            "av_arr", "bq_cols", "bk_col", "bv_col", "idt32"]
    d = {k: st[k] for k in keys}
    d["x_t"] = x_t_c
    d["rows1"] = rows1_c
    d["u_pad"] = u_pad_c
    d["w1_sh"] = st["_w1sh"][c]
    d["w2_sh"] = st["_w2sh"][c]
    d["w3_sh"] = st["_w3sh"][c]
    d["gparts"] = st["_gparts"]
    return d


def _masks_lb(chunks):
    sg = np.arange(SEQ, dtype=np.int32)[:, None]
    parts = []
    for s, gc in enumerate(chunks):
        qg = gc * TC + np.arange(TC, dtype=np.int32)[None, :]
        M = (sg < qg).reshape(NKT, 128, TC)[:NKTQ[s]]
        parts.append(M.reshape(NKTQ[s] * 128, TC))
    return np.ascontiguousarray(
        np.concatenate(parts, 0)).astype(ml_dtypes.bfloat16)


def _in3(st, c, x2c, qc_, kfull, vfull):
    return dict(x2_hand=x2c, q_own=qc_, k_full=kfull, v_full=vfull,
                masks=st["masks"][c], ao_arr=st["ao_arr"], bo_cols=st["bo_cols"],
                w1q=st["_w1q"], w2q=st["_w2q"], w3q=st["_w3q"])


def run_launches12(inp, cores=None):
    if cores is None:
        cores = list(range(8))
    x = inp["L0_attn_x"].astype(np.float32)
    st = _stage_common(inp)
    st["_w1sh"] = [np.ascontiguousarray(st["w1_arr"][:, c * HSH:(c + 1) * HSH])
                   for c in cores]
    st["_w2sh"] = [np.ascontiguousarray(st["w2_arr"][:, c * HSH:(c + 1) * HSH])
                   for c in cores]
    st["_w3sh"] = [np.ascontiguousarray(st["w3_arr"][:, c * DSH:(c + 1) * DSH])
                   for c in cores]
    x_t = []
    for c in cores:
        bidx, c4 = c // 4, c % 4
        x_t.append(np.ascontiguousarray(x[bidx, c4 * TOK:(c4 + 1) * TOK, :].T))

    nc1 = build_launch1()
    in1 = [_in1(st, x_t[c], c) for c in cores]
    r1 = run_bass_kernel_spmd(nc1, in1, core_ids=cores).results

    u_full = np.zeros((2, R, SEQ), np.float32)
    for c in cores:
        bidx, c4 = c // 4, c % 4
        u_full[bidx, :, c4 * TOK:(c4 + 1) * TOK] = r1[c]["u_hand"]
    st["_gparts"] = np.ascontiguousarray(
        np.concatenate([r1[c]["gpart"] for c in cores], 0))

    nc2 = build_launch2()
    in2 = []
    for c in cores:
        bidx, c4 = c // 4, c % 4
        off = c4 * TOK
        u_pad = np.zeros((R, SEQ), np.float32)
        u_pad[:, SEQ - (off + TOK):] = u_full[bidx][:, :off + TOK]
        in2.append(_in2(st, x_t[c], r1[c]["rows1"], np.ascontiguousarray(u_pad), c))
    r2 = run_bass_kernel_spmd(nc2, in2, core_ids=cores).results
    return st, x_t, r2


def _gchunks(c):
    return [15 - c // 2, 11 - c // 2, 7 - c // 2, 3 - c // 2]


def kernel(**inp):
    inp = {k: np.asarray(v) for k, v in inp.items()}
    cores = list(range(8))
    st, x_t, r2 = run_launches12(inp, cores)
    st["masks"] = [_masks_lb(_gchunks(c)) for c in cores]
    kfull, vfull, x2full, qfull = {}, {}, {}, {}
    for bi in range(2):
        kk = np.zeros((128, SEQ), ml_dtypes.bfloat16)
        vv = np.zeros((128, SEQ), ml_dtypes.bfloat16)
        for c4 in range(4):
            c = bi * 4 + c4
            kk[:, c4 * TOK:(c4 + 1) * TOK] = r2[c]["k_hand"]
            vv[:, c4 * TOK:(c4 + 1) * TOK] = r2[c]["v_hand"]
        kfull[bi] = kk
        vfull[bi] = vv
        x2full[bi] = np.concatenate(
            [r2[bi * 4 + c4]["x2_hand"] for c4 in range(4)], 1)
        qfull[bi] = np.concatenate(
            [r2[bi * 4 + c4]["q_hand"] for c4 in range(4)], 1)
    nc3 = build_launch3()
    st["_w1q"] = np.ascontiguousarray(
        np.concatenate([r2[c]["w1q"] for c in cores], 1))
    st["_w2q"] = np.ascontiguousarray(
        np.concatenate([r2[c]["w2q"] for c in cores], 1))
    st["_w3q"] = np.ascontiguousarray(
        np.concatenate([r2[c]["w3q"] for c in cores], 1))
    in3 = []
    for c in cores:
        bi = c % 2
        G = _gchunks(c)
        x2c = np.ascontiguousarray(np.concatenate(
            [x2full[bi][:, gc * TC:(gc + 1) * TC] for gc in G], 1))
        qc_ = np.ascontiguousarray(np.concatenate(
            [qfull[bi][:, gc * TC:(gc + 1) * TC] for gc in G], 1))
        in3.append(_in3(st, c, x2c, qc_, kfull[bi], vfull[bi]))
    r3 = run_bass_kernel_spmd(nc3, in3, core_ids=cores).results
    out = np.zeros((2, SEQ, D), np.float32)
    for c in cores:
        bi = c % 2
        for s, gc in enumerate(_gchunks(c)):
            out[bi, gc * TC:(gc + 1) * TC, :] = \
                r3[c]["out_t"][:, s * TC:(s + 1) * TC].T
    return out
